# version 03
# baseline (speedup 1.0000x reference)
import numpy as np
import ml_dtypes
from contextlib import ExitStack

import concourse.bass as bass
import concourse.mybir as mybir
from concourse.bass_utils import run_bass_kernel_spmd

F32 = mybir.dt.float32
BF16 = mybir.dt.bfloat16
BF = ml_dtypes.bfloat16
AX = mybir.AxisListType.X
ALU = mybir.AluOpType
ACT = mybir.ActivationFunctionType

B, T, D, H, HD = 4, 4096, 1024, 16, 64
WLOC, NG = 256, 64
TOWN = 2048
EXT = 2688
NQT = TOWN // 128
NKT = EXT // 128
NCH = 8
HIDN = 4096
NHI = HIDN // 128
RANK = 16

G1N, G1 = 448, 6
GN, G = 512, 4


SAFE_MODE = 1
SAFE_PHASES = None
UNSAFE_PHASES = ()


class Plan:
    def __init__(self):
        self.ops = []
        self.count = {}
        self.safe = SAFE_MODE
        self.phase = "p0"
        self.eng_sem = {}
        self.sems = []
        self.joined = {}

    def n(self, sem):
        return self.count.get(id(sem), 0)

    def add(self, eng, fn, waits=(), incs=(), dma=False):
        waits = list(waits)
        incs = list(incs)
        for s, v in waits:
            assert v <= self.n(s), f"wait for future sem value {v} > {self.n(s)}"
        base = self.phase.split("_")[0]
        insafe = (SAFE_PHASES is None or self.phase in SAFE_PHASES or
                  base in SAFE_PHASES) and \
            self.phase not in UNSAFE_PHASES and base not in () 
        if self.safe and insafe:
            if not incs and eng in self.eng_sem:
                incs = [(self.eng_sem[eng], 1)]
            j = self.joined.setdefault(eng, {})
            for s in self.sems:
                nv = self.n(s)
                if nv > j.get(id(s), 0):
                    waits.append((s, nv))
                    j[id(s)] = nv
        for s, k in incs:
            self.count[id(s)] = self.n(s) + k
        self.ops.append((eng, fn, tuple(waits), tuple(incs)))

    def dma(self, eng, out, in_, semh, waits=(), slow=False):
        self.add(eng, (lambda e, out=out, in_=in_, slow=slow: e.dma_start(
            out=out, in_=in_, allow_slow_non_contiguous=slow)),
                 waits=waits, incs=[(semh, 16)], dma=True)

    def run(self, e, eng):
        for oeng, fn, waits, incs in self.ops:
            if oeng != eng:
                continue
            for s, v in waits:
                e.wait_ge(s, v)
            inst = fn(e)
            for s, k in incs:
                inst = inst.then_inc(s, k)


def build_program(debug=()):
    debug = set(debug)
    nc = bass.Bass("TRN2", target_bir_lowering=False)

    d_x = nc.dram_tensor("x_ext", [D, EXT], F32, kind="ExternalInput").ap()
    d_wqkv = nc.dram_tensor("wqkv", [D, 3 * D], F32, kind="ExternalInput").ap()
    d_bqkv = nc.dram_tensor("bqkv_pp", [128, 24], F32, kind="ExternalInput").ap()
    d_bvrow = nc.dram_tensor("bv_row", [1, D], F32, kind="ExternalInput").ap()
    d_wout = nc.dram_tensor("wout", [D, D], F32, kind="ExternalInput").ap()
    d_bout = nc.dram_tensor("bout_pp", [128, 8], F32, kind="ExternalInput").ap()
    d_wg = nc.dram_tensor("wg", [D, 1], F32, kind="ExternalInput").ap()
    d_bg = nc.dram_tensor("bg", [1, 1], F32, kind="ExternalInput").ap()
    d_wfc = nc.dram_tensor("w_fc", [HIDN, D], F32, kind="ExternalInput").ap()
    d_wproj = nc.dram_tensor("w_proj", [D, HIDN], F32, kind="ExternalInput").ap()
    d_uT = nc.dram_tensor("uT", [RANK, HIDN], F32, kind="ExternalInput").ap()
    d_vT = nc.dram_tensor("vT", [D, RANK], F32, kind="ExternalInput").ap()
    d_fbT = nc.dram_tensor("fbT", [RANK, RANK], F32, kind="ExternalInput").ap()
    d_gate = nc.dram_tensor("gate", [1, 1], F32, kind="ExternalInput").ap()
    d_masks = nc.dram_tensor("masks", [NQT, 128, 640], BF16, kind="ExternalInput").ap()
    d_ident = nc.dram_tensor("ident", [128, 128], BF16, kind="ExternalInput").ap()
    d_negi = nc.dram_tensor("negi", [128, 128], BF16, kind="ExternalInput").ap()
    d_e16 = nc.dram_tensor("e16", [128, NCH * 16], BF16, kind="ExternalInput").ap()
    d_i16 = nc.dram_tensor("i16", [16, 16], BF16, kind="ExternalInput").ap()
    d_row1 = nc.dram_tensor("row1", [1, 128], BF16, kind="ExternalInput").ap()
    d_statc = nc.dram_tensor("statc", [128, 1], BF16, kind="ExternalInput").ap()
    d_one11 = nc.dram_tensor("one11", [1, 1], BF16, kind="ExternalInput").ap()

    d_out = nc.dram_tensor("out", [D, TOWN], F32, kind="ExternalOutput").ap()
    d_tproj = nc.dram_tensor("tproj_spill", [NHI, 128, D], BF16).ap()
    d_tfcs = nc.dram_tensor("tfc_spill", [NCH, 128, HIDN], BF16).ap()

    P = Plan()
    dbg_outs = {}

    def tap(name, src_aps, waits):
        if name not in debug:
            return
        p0, f0 = src_aps[0].shape[-2], src_aps[0].shape[-1]
        dt = src_aps[0].dtype
        dten = nc.dram_tensor("dbg_" + name, [len(src_aps), p0, f0], dt,
                              kind="ExternalOutput").ap()
        dbg_outs[name] = True
        for i, ap in enumerate(src_aps):
            P.dma("gpsimd", dten[i], ap, s_st, waits=(waits if i == 0 else []))

    ctx = ExitStack()
    with ctx:
        def mksb(stack, name, shape, dt=F32, side="left"):
            return stack.enter_context(nc.sbuf_tensor("sb_" + name, shape, dt, side=side))

        def sb(name, shape, dt=F32, side="left"):
            return mksb(ctx, name, shape, dt, side)

        def sem(name):
            return ctx.enter_context(nc.semaphore(name))

        s_ld = sem("s_ld"); s_x = sem("s_x"); s_w = sem("s_w"); s_mk = sem("s_mk")
        s_st = sem("s_st"); s_te = sem("s_te"); s_ve = sem("s_ve"); s_ac = sem("s_ac")
        s_gp = sem("s_gp")
        sxp = [sem("sxp0"), sem("sxp1")]
        swq = [sem("swq0"), sem("swq1")]
        swv = [sem("swv0"), sem("swv1")]
        swo = [sem("swo0"), sem("swo1")]
        swt = [sem("swt0"), sem("swt1")]
        sfc2 = [sem("sfc0"), sem("sfc1")]
        spr = [sem("spr0"), sem("spr1")]
        ssp = [sem("ssp0"), sem("ssp1")]
        smk2 = [sem("smk0"), sem("smk1")]
        s_u = sem("s_u")
        P.eng_sem = {"tensor": s_te, "vector": s_ve, "scalar": s_ac, "gpsimd": s_gp}
        P.sems = [s_ld, s_x, s_w, s_mk, s_st, s_te, s_ve, s_ac, s_gp, s_u] + \
            sxp + swq + swv + swo + swt + sfc2 + spr + ssp + smk2

        psA = ctx.enter_context(nc.psum_tensor("psA", [128, 1024], F32))
        psB = ctx.enter_context(nc.psum_tensor("psB", [128, 1024], F32))
        psC = ctx.enter_context(nc.psum_tensor("psC", [128, 1024], F32))
        psD = ctx.enter_context(nc.psum_tensor("psD", [128, 1024], F32))

        ident = sb("ident", [128, 128], BF16)
        i16 = sb("i16", [16, 16], BF16)
        row1 = sb("row1", [1, 128], BF16)
        statc = sb("statc", [128, 1], BF16)
        one11 = sb("one11", [1, 1], BF16)
        bqkv = sb("bqkv", [128, 24], F32)
        bvrow_bf = sb("bvrow_bf", [1, D], BF16)
        bout = sb("bout", [128, 8], F32)
        wg = sb("wg", [128, 8], F32)
        wg_bf = sb("wg_bf", [128, 8], BF16)
        bg11 = sb("bg11", [1, 1], F32)
        bg_bf = sb("bg_bf", [1, 1], BF16)
        bg128 = sb("bg128", [128, 1], F32)
        gate11 = sb("gate11", [1, 1], F32)
        gate_bf = sb("gate_bf", [1, 1], BF16)
        g16 = sb("g16", [16, 1], F32)
        vT_f = sb("vT_f", [128, NCH * 16], F32)
        vT_bf = sb("vT_bf", [128, NCH * 16], BF16)
        fbT_f = sb("fbT_f", [16, 16], F32)
        fbTg = sb("fbTg", [16, 16], BF16)
        gcol = sb("gcol", [128, NQT], F32)
        gm1col = sb("gm1col", [128, NQT], F32)
        negm8 = sb("negm8", [128, 1], F32)
        nm8bf = sb("nm8bf", [1, 1], BF16)
        rowbuf = sb("rowbuf", [1, 2 * GN], F32)
        rowbuf2 = sb("rowbuf2", [1, GN], F32)
        rown_bf = sb("rown_bf", [1, GN], BF16)
        rowr_bf = sb("rowr_bf", [1, GN], BF16)
        rb_sb = sb("rb_sb", [128, GN], F32)
        lnx = sb("lnx", [128, NCH * GN], BF16)
        lnx2 = sb("lnx2", [128, 2 * GN], BF16)
        lnt = sb("lnt", [128, GN], BF16)
        ltile = sb("ltile", [128, 16], F32)
        ltecho = sb("ltecho", [128, 16], F32)
        lgtile = sb("lgtile", [128, 16], F32)
        rltile = sb("rltile", [128, 16], F32)
        rlgtile = sb("rlgtile", [128, 16], F32)
        bndrow = sb("bndrow", [1, 16], F32)
        bnd16 = sb("bnd16", [16, 4], F32)
        bnd16b = sb("bnd16b", [16, 1], BF16)
        bndsc = sb("bndsc", [16, 12], F32)

        ctx_qkv = ExitStack()
        q_fm = [mksb(ctx_qkv, f"q{c}", [128, TOWN], BF16) for c in range(NCH)]
        k_fm = [mksb(ctx_qkv, f"k{c}", [128, EXT], BF16) for c in range(NCH)]
        v_tok = [mksb(ctx_qkv, f"v{t}", [128, D], BF16) for t in range(NKT)]

        for o, i in [(ident, d_ident), (i16, d_i16), (row1, d_row1),
                     (statc, d_statc), (one11, d_one11), (bqkv, d_bqkv),
                     (bout, d_bout), (bg11, d_bg), (gate11, d_gate), (fbT_f, d_fbT)]:
            P.dma("sync", o[:], i, s_ld)
        P.dma("sync", wg.ap().rearrange("p (c o) -> p c o", o=1),
              d_wg.rearrange("(c p) o -> p c o", p=128), s_ld, slow=True)
        P.dma("sync", vT_f.ap().rearrange("p (c n) -> p c n", c=NCH),
              d_vT.rearrange("(c p) n -> p c n", p=128), s_ld)
        P.dma("sync", rowbuf[0:1, 0:D], d_bvrow, s_ld)
        n_ld = P.n(s_ld)

        P.add("vector", lambda e: e.tensor_copy(bvrow_bf[:], rowbuf[0:1, 0:D]),
              waits=[(s_ld, n_ld)])
        P.add("vector", lambda e: e.tensor_copy(wg_bf[:], wg[:]))
        P.add("vector", lambda e: e.tensor_copy(vT_bf[:], vT_f[:]))
        P.add("vector", lambda e: e.tensor_copy(gate_bf[:], gate11[:]))
        P.add("vector", lambda e: e.tensor_copy(bg_bf[:], bg11[:]), incs=[(s_ve, 1)])
        ve0 = P.n(s_ve)
        P.add("tensor", lambda e: e.matmul(psD[0:16, 0:1], row1[:, 0:16], gate_bf[:],
                                           start=True, stop=True),
              waits=[(s_ve, ve0)], incs=[(s_te, 1)])
        P.add("vector", lambda e: e.tensor_copy(g16[:], psD[0:16, 0:1]),
              waits=[(s_te, P.n(s_te))])
        P.add("vector", lambda e: e.tensor_scalar(out=fbTg[:], in0=fbT_f[:], scalar1=g16[:],
                                                  scalar2=None, op0=ALU.mult), incs=[(s_ve, 1)])
        P.add("tensor", lambda e: e.matmul(psD[:, 1:2], row1[:], bg_bf[:], start=True, stop=True),
              waits=[(s_ve, P.n(s_ve))], incs=[(s_te, 1)])
        P.add("vector", lambda e: e.tensor_copy(bg128[:], psD[:, 1:2]),
              waits=[(s_te, P.n(s_te))], incs=[(s_ve, 1)])
        ve_consts = P.n(s_ve)

        def emit_ln(gn, get_src, put, w_first=(), cast_hook=None, post_cast=None):
            te_sq = {}
            for c in range(NCH):
                w_c = list(w_first) if c == 0 else []
                if cast_hook is not None:
                    w_c = w_c + cast_hook(c)
                sq = lnx2[:, (c % 2) * gn:(c % 2) * gn + gn]
                if c >= 2:
                    w_c.append((s_te, te_sq[c - 2]))
                P.add("vector",
                      (lambda e, c=c: e.tensor_copy(lnx[:, c * gn:(c + 1) * gn], get_src(c))),
                      waits=w_c)
                P.add("vector",
                      (lambda e, c=c, sq=sq: e.tensor_mul(sq, lnx[:, c * gn:(c + 1) * gn],
                                                          lnx[:, c * gn:(c + 1) * gn])),
                      incs=[(s_ve, 1)])
                ve_c = P.n(s_ve)
                if post_cast is not None:
                    post_cast(c)
                P.add("tensor",
                      (lambda e, c=c: e.matmul(psD[0:1, 0:gn], statc[:],
                                               lnx[:, c * gn:(c + 1) * gn],
                                               start=(c == 0), stop=(c == NCH - 1))),
                      waits=[(s_ve, ve_c)])
                P.add("tensor",
                      (lambda e, c=c, sq=sq: e.matmul(psD[0:1, 512:512 + gn], statc[:], sq,
                                               start=(c == 0), stop=(c == NCH - 1))),
                      incs=[(s_te, 1)])
                te_sq[c] = P.n(s_te)
            te_stats = P.n(s_te)
            P.add("vector", lambda e: e.tensor_copy(rowbuf[:, 0:gn], psD[0:1, 0:gn]),
                  waits=[(s_te, te_stats)])
            P.add("vector", lambda e: e.tensor_copy(rowbuf[:, gn:2 * gn],
                                                    psD[0:1, 512:512 + gn]))
            P.add("vector", lambda e: e.tensor_mul(rowbuf2[:, 0:gn], rowbuf[:, 0:gn],
                                                   rowbuf[:, 0:gn]))
            P.add("vector", lambda e: e.scalar_tensor_tensor(
                out=rowbuf2[:, 0:gn], in0=rowbuf[:, gn:2 * gn], scalar=-1.0,
                in1=rowbuf2[:, 0:gn], op0=ALU.mult, op1=ALU.subtract))
            P.add("vector", lambda e: e.tensor_copy(rown_bf[:, 0:gn], rowbuf[:, 0:gn]),
                  incs=[(s_ve, 1)])
            ve_rows = P.n(s_ve)
            P.add("scalar", lambda e: e.activation(out=rowbuf2[:, 0:gn], in_=rowbuf2[:, 0:gn],
                                                   func=ACT.Sqrt, bias=np.float32(1e-5),
                                                   scale=1.0),
                  waits=[(s_ve, ve_rows)], incs=[(s_ac, 1)])
            P.add("vector", lambda e: e.reciprocal(rowbuf2[:, 0:gn], rowbuf2[:, 0:gn]),
                  waits=[(s_ac, P.n(s_ac))])
            P.add("vector", lambda e: e.tensor_copy(rowr_bf[:, 0:gn], rowbuf2[:, 0:gn]),
                  incs=[(s_ve, 1)])
            ve_r = P.n(s_ve)
            P.add("tensor", lambda e: e.matmul(psC[:, 0:gn], row1[:], rowr_bf[:, 0:gn],
                                               start=True, stop=True),
                  waits=[(s_ve, ve_r)], incs=[(s_te, 1)])
            P.add("vector", lambda e: e.tensor_copy(rb_sb[:, 0:gn], psC[:, 0:gn]),
                  waits=[(s_te, P.n(s_te))], incs=[(s_ve, 1)])
            ve_rb = P.n(s_ve)
            for c in range(NCH):
                reg = psA[:, 0:gn] if c % 2 == 0 else psB[:, 0:gn]
                w_reg = [(s_ve, P.n(s_ve))] if c >= 2 else [(s_ve, ve_rb)]
                P.add("tensor", (lambda e, reg=reg: e.matmul(reg, row1[:], rown_bf[:, 0:gn],
                                                             start=True, stop=False)),
                      waits=w_reg)
                P.add("tensor", (lambda e, c=c, reg=reg: e.matmul(
                    reg, ident[:], lnx[:, c * gn:(c + 1) * gn], start=False, stop=True)),
                      incs=[(s_te, 1)])
                put(c, reg, P.n(s_te))

        P.phase = "p1a"
        ctx_p1 = ExitStack()
        xn = [mksb(ctx_p1, f"xn{c}", [128, EXT], BF16, side="right") for c in range(NCH)]
        ctx_p1a = ExitStack()
        xstg = [mksb(ctx_p1a, f"xstg{b}", [128, G1N], F32, side="right") for b in range(2)]

        x_view = d_x.rearrange("(cc p) n -> cc p n", p=128)
        cast_ve = {}

        for g in range(G1):
            x_lds = {}

            def plan_xdma(g, c):
                prev = (g, c - 2) if c >= 2 else ((g - 1, NCH - 2 + c) if g > 0 else None)
                if prev is not None and prev in cast_ve:
                    w = [(s_ve, cast_ve[prev])]
                else:
                    w = [(s_ve, ve_consts)]
                P.dma("sync", xstg[c % 2][:],
                      x_view[c, :, g * G1N:(g + 1) * G1N], sxp[c % 2], waits=w)
                x_lds[(g, c)] = (sxp[c % 2], P.n(sxp[c % 2]))

            plan_xdma(g, 0)
            plan_xdma(g, 1)

            def cast_hook(c, g=g):
                return [x_lds[(g, c)]]

            def post_cast(c, g=g):
                cast_ve[(g, c)] = P.n(s_ve)
                if c + 2 < NCH:
                    plan_xdma(g, c + 2)

            def put_xn(c, reg, te_val, g=g):
                P.add("vector", (lambda e, c=c, g=g, reg=reg: e.tensor_mul(
                    xn[c][:, g * G1N:(g + 1) * G1N], reg, rb_sb[:, 0:G1N])),
                      waits=[(s_te, te_val)], incs=[(s_ve, 1)])

            emit_ln(G1N, (lambda c: xstg[c % 2][:]), put_xn, cast_hook=cast_hook,
                    post_cast=post_cast)
        ve_ln1 = P.n(s_ve)
        tap("xn", [xn[c][:] for c in range(NCH)], [(s_ve, ve_ln1)])
        ctx_p1a.close()

        P.phase = "p1b"
        ctx_qk = ExitStack()
        wstg = [mksb(ctx_qk, f"wstg{b}", [128, 1024], F32, side="right") for b in range(2)]
        wbfq = [mksb(ctx_qk, f"wbfq{b}", [128, 1024], BF16, side="right") for b in range(2)]

        te_lastmm = {0: None, 1: None}
        cast_done_w = {0: None, 1: None}
        ridx = 0
        ac_ev = [None, None]
        for fo in range(16):
            par = fo % 2
            stg, wbf = wstg[par], wbfq[par]
            w_w = [(s_ve, cast_done_w[par])] if cast_done_w[par] is not None else \
                [(s_ve, ve_ln1)]
            for fi in range(NCH):
                P.dma("sync", stg[:, fi * 128:(fi + 1) * 128],
                      d_wqkv[128 * fi:128 * fi + 128, 128 * fo:128 * fo + 128], swq[par],
                      waits=(w_w if fi == 0 else []))
            w_ld = P.n(swq[par])
            w_cast = [(swq[par], w_ld)]
            if te_lastmm[par] is not None:
                w_cast.append((s_te, te_lastmm[par]))
            P.add("vector", (lambda e, stg=stg, wbf=wbf: e.tensor_copy(wbf[:], stg[:])),
                  waits=w_cast, incs=[(s_ve, 1)])
            ve_cast = P.n(s_ve)
            cast_done_w[par] = ve_cast
            is_q = fo < 8
            ngrp = 4 if is_q else G1
            gn = 512 if is_q else G1N
            for g in range(ngrp):
                reg = psA[:, 0:gn] if ridx % 2 == 0 else psB[:, 0:gn]
                w_mm = [(s_ve, ve_cast)]
                if ac_ev[ridx % 2] is not None:
                    w_mm.append((s_ac, ac_ev[ridx % 2]))
                for fi in range(NCH):
                    if is_q:
                        mk_mm = (lambda e, fi=fi, g=g, wbf=wbf, reg=reg: e.matmul(
                            reg, wbf[:, fi * 128:(fi + 1) * 128],
                            xn[fi][:, 256 + g * 512:256 + (g + 1) * 512],
                            start=(fi == 0), stop=(fi == NCH - 1)))
                    else:
                        mk_mm = (lambda e, fi=fi, g=g, wbf=wbf, reg=reg: e.matmul(
                            reg, wbf[:, fi * 128:(fi + 1) * 128],
                            xn[fi][:, g * G1N:(g + 1) * G1N],
                            start=(fi == 0), stop=(fi == NCH - 1)))
                    P.add("tensor", mk_mm, waits=(w_mm if fi == 0 else []),
                          incs=([(s_te, 1)] if fi == NCH - 1 else []))
                te_mm = P.n(s_te)
                te_lastmm[par] = te_mm
                if is_q:
                    dst = (lambda e, fo=fo, g=g, reg=reg: e.activation(
                        out=q_fm[fo][:, g * 512:(g + 1) * 512], in_=reg, func=ACT.Identity,
                        bias=bqkv[:, fo:fo + 1], scale=1.0))
                else:
                    dst = (lambda e, fo=fo, g=g, reg=reg: e.activation(
                        out=k_fm[fo - 8][:, g * G1N:(g + 1) * G1N], in_=reg, func=ACT.Identity,
                        bias=bqkv[:, fo:fo + 1], scale=1.0))
                P.add("scalar", dst, waits=[(s_te, te_mm)], incs=[(s_ac, 1)])
                ac_ev[ridx % 2] = P.n(s_ac)
                ridx += 1
        ctx_qk.close()

        P.phase = "pv"
        ctx_v = ExitStack()
        wvstg = [mksb(ctx_v, f"wvstg{b}", [128, 512], F32, side="right") for b in range(2)]
        wvbf = mksb(ctx_v, "wvbf", [128, NCH * 512], BF16, side="right")
        te_qk_done = P.n(s_te)
        for vh in range(2):
            for fi in range(NCH):
                stg = wvstg[fi % 2]
                w_w = [(s_ve, P.n(s_ve))] if (vh * NCH + fi) >= 2 else \
                    [(s_te, te_qk_done)]
                P.dma("sync", stg[:],
                      d_wqkv[128 * fi:128 * fi + 128, 2048 + vh * 512:2048 + (vh + 1) * 512],
                      swv[fi % 2], waits=w_w)
                w_c = [(swv[fi % 2], P.n(swv[fi % 2]))]
                if vh == 1 and fi == 0:
                    w_c.append((s_te, P.n(s_te)))
                P.add("vector", (lambda e, stg=stg, fi=fi: e.tensor_copy(
                    wvbf[:, fi * 512:(fi + 1) * 512], stg[:])),
                      waits=w_c, incs=[(s_ve, 1)])
            ve_wv = P.n(s_ve)
            for t in range(NKT):
                reg = psA[:, 0:512] if ridx % 2 == 0 else psB[:, 0:512]
                w_mm = [(s_ve, ve_wv)]
                if ac_ev[ridx % 2] is not None:
                    w_mm.append((s_ac, ac_ev[ridx % 2]))
                P.add("tensor", (lambda e, reg=reg, vh=vh: e.matmul(
                    reg, row1[:], bvrow_bf[:, vh * 512:(vh + 1) * 512],
                    start=True, stop=False)), waits=w_mm)
                for fi in range(NCH):
                    P.add("tensor", (lambda e, fi=fi, t=t, reg=reg: e.matmul(
                        reg, xn[fi][:, t * 128:(t + 1) * 128], wvbf[:, fi * 512:(fi + 1) * 512],
                        start=False, stop=(fi == NCH - 1))),
                          incs=([(s_te, 1)] if fi == NCH - 1 else []))
                te_mm = P.n(s_te)
                P.add("scalar", (lambda e, t=t, vh=vh, reg=reg: e.activation(
                    out=v_tok[t][:, vh * 512:(vh + 1) * 512], in_=reg, func=ACT.Copy)),
                      waits=[(s_te, te_mm)], incs=[(s_ac, 1)])
                ac_ev[ridx % 2] = P.n(s_ac)
                ridx += 1
        ctx_v.close()

        ctx_g = ExitStack()
        P.phase = "pg"
        zrow_bf = mksb(ctx_g, "zrow_bf", [1, TOWN], BF16, side="right")
        ve_zg = None
        for g in range(4):
            w_z = [(s_ve, ve_zg)] if ve_zg is not None else [(s_ve, P.n(s_ve))]
            for fi in range(NCH):
                P.add("tensor", (lambda e, fi=fi, g=g: e.matmul(
                    psD[0:1, 0:512], wg_bf[:, fi:fi + 1],
                    xn[fi][:, 256 + g * 512:256 + (g + 1) * 512],
                    start=(fi == 0), stop=(fi == NCH - 1))),
                      waits=(w_z if fi == 0 else []),
                      incs=([(s_te, 1)] if fi == NCH - 1 else []))
            te_z = P.n(s_te)
            P.add("vector", (lambda e: e.tensor_copy(rowbuf[:, 0:512], psD[0:1, 0:512])),
                  waits=[(s_te, te_z)])
            P.add("vector", (lambda e, g=g: e.tensor_copy(
                zrow_bf[:, g * 512:(g + 1) * 512], rowbuf[:, 0:512])), incs=[(s_ve, 1)])
            ve_zg = P.n(s_ve)
        for t in range(NQT):
            P.add("tensor", (lambda e, t=t: e.matmul(
                psC[:, t:t + 1], zrow_bf[:, t * 128:(t + 1) * 128], one11[:],
                start=True, stop=(t == NQT - 1))),
                  waits=([(s_ve, ve_zg)] if t == 0 else []),
                  incs=([(s_te, 1)] if t == NQT - 1 else []))
        te_zt = P.n(s_te)
        P.add("scalar", lambda e: e.activation(out=gcol[:], in_=psC[:, 0:16], func=ACT.Sigmoid,
                                               bias=bg128[:], scale=1.0),
              waits=[(s_te, te_zt)], incs=[(s_ac, 1)])
        P.add("vector", lambda e: e.tensor_scalar(out=gm1col[:], in0=gcol[:], scalar1=-1.0,
                                                  scalar2=1.0, op0=ALU.mult, op1=ALU.add),
              waits=[(s_ac, P.n(s_ac))], incs=[(s_ve, 1)])
        ve_gate = P.n(s_ve)
        tap("q", [q_fm[c][:] for c in range(NCH)], [(s_ac, P.n(s_ac))])
        tap("k", [k_fm[c][:] for c in range(NCH)], [(s_ac, P.n(s_ac))])
        tap("v", [v_tok[t][:] for t in range(NKT)], [(s_ac, P.n(s_ac))])
        tap("gate", [gcol[:], gm1col[:]], [(s_ve, ve_gate)])

        P.phase = "p2a"
        e16s = mksb(ctx_g, "e16s", [128, NCH * 16], BF16, side="right")
        q2b = [mksb(ctx_g, f"q2b{b}", [128, 512], BF16, side="right") for b in range(2)]
        P.dma("sync", e16s[:], d_e16, s_ld, waits=[(s_te, P.n(s_te))])
        n_ld2 = P.n(s_ld)

        def norms(buf_fm, ncols, outcol0):
            njg = (ncols + 511) // 512
            for jg in range(njg):
                j0 = jg * 512
                n = min(512, ncols - j0)
                for c in range(NCH):
                    b = q2b[c % 2]
                    w_sq = [(s_te, P.n(s_te))] if (jg or outcol0 or c >= 2) \
                        else [(s_ac, P.n(s_ac)), (s_ld, n_ld2)]
                    P.add("vector", (lambda e, c=c, b=b, j0=j0, n=n, buf_fm=buf_fm: e.tensor_mul(
                        b[:, 0:n], buf_fm[c][:, j0:j0 + n], buf_fm[c][:, j0:j0 + n])),
                          waits=w_sq, incs=[(s_ve, 1)])
                    ve_sq = P.n(s_ve)
                    P.add("tensor", (lambda e, c=c, b=b, n=n: e.matmul(
                        psA[0:16, 0:n], e16s[:, c * 16:(c + 1) * 16], b[:, 0:n],
                        start=(c == 0), stop=(c == NCH - 1))),
                          waits=[(s_ve, ve_sq)], incs=[(s_te, 1)])
                te_n = P.n(s_te)
                P.add("vector", (lambda e, jg=jg, n=n, outcol0=outcol0: e.reduce_max(
                    bndsc[:, outcol0 + jg:outcol0 + jg + 1], psA[0:16, 0:n], axis=AX)),
                      waits=[(s_te, te_n)], incs=[(s_ve, 1)])
            return njg

        nq = norms(q_fm, TOWN, 0)
        nk = norms(k_fm, 2624, nq)
        P.add("vector", lambda e, nq=nq: e.reduce_max(bnd16[:, 0:1], bndsc[:, 0:nq], axis=AX))
        P.add("vector", lambda e, nq=nq, nk=nk: e.reduce_max(bnd16[:, 1:2],
                                                             bndsc[:, nq:nq + nk], axis=AX))
        P.add("vector", lambda e: e.tensor_mul(bnd16[:, 2:3], bnd16[:, 0:1], bnd16[:, 1:2]))
        P.add("vector", lambda e: e.tensor_copy(bnd16b[:], bnd16[:, 2:3]), incs=[(s_ve, 1)])
        P.add("tensor", lambda e: e.matmul(psD[0:1, 0:16], bnd16b[:], i16[:],
                                           start=True, stop=True),
              waits=[(s_ve, P.n(s_ve))], incs=[(s_te, 1)])
        P.add("vector", lambda e: e.tensor_copy(bndrow[:], psD[0:1, 0:16]),
              waits=[(s_te, P.n(s_te))])
        P.add("vector", lambda e: e.reduce_max(bnd16[0:1, 3:4], bndrow[:], axis=AX),
              incs=[(s_ve, 1)])
        P.add("scalar", lambda e: e.activation(out=bndrow[:, 1:2], in_=bnd16[0:1, 3:4],
                                               func=ACT.Sqrt, bias=np.float32(1e-6), scale=1.0),
              waits=[(s_ve, P.n(s_ve))], incs=[(s_ac, 1)])
        P.add("vector", lambda e: e.tensor_scalar(out=bndrow[:, 2:3], in0=bndrow[:, 1:2],
                                                  scalar1=-0.125, scalar2=-0.25,
                                                  op0=ALU.mult, op1=ALU.add),
              waits=[(s_ac, P.n(s_ac))])
        P.add("vector", lambda e: e.tensor_copy(nm8bf[:], bndrow[:, 2:3]), incs=[(s_ve, 1)])
        P.add("tensor", lambda e: e.matmul(psD[:, 2:3], row1[:], nm8bf[:], start=True, stop=True),
              waits=[(s_ve, P.n(s_ve))], incs=[(s_te, 1)])
        P.add("vector", lambda e: e.tensor_copy(negm8[:], psD[:, 2:3]),
              waits=[(s_te, P.n(s_te))], incs=[(s_ve, 1)])
        ve_negm8 = P.n(s_ve)
        tap("negm8", [negm8[:]], [(s_ve, ve_negm8)])
        tap("bndsc", [bndsc[:]], [(s_ve, ve_negm8)])
        tap("bndrow", [bndrow[:]], [(s_ve, ve_negm8)])

        ctx_g.close()
        ctx_p1.close()

        P.phase = "p2b"
        ctx_p2 = ExitStack()
        attnfm = mksb(ctx_p2, "attnfm", [128, NCH * TOWN], BF16, side="right")
        mks = [mksb(ctx_p2, f"mks{b}", [128, 640], BF16, side="right") for b in range(2)]
        pbuf = [mksb(ctx_p2, f"pbuf{b}", [128, 704], BF16, side="right") for b in range(2)]
        ptn = mksb(ctx_p2, "ptn", [128, 768], BF16, side="right")
        negi = mksb(ctx_p2, "negi", [128, 128], BF16, side="right")
        gdiag = mksb(ctx_p2, "gdiag", [128, 128], BF16, side="right")
        ggdiag = mksb(ctx_p2, "ggdiag", [128, 128], BF16, side="right")
        dlb = [mksb(ctx_p2, f"dlb{i}", [128, 128], BF16, side="right") for i in range(2)]
        dgb = [mksb(ctx_p2, f"dgb{i}", [128, 128], BF16, side="right") for i in range(2)]
        P.dma("sync", negi[:], d_negi, s_ld, waits=[(s_te, P.n(s_te))])
        n_ld3 = P.n(s_ld)

        te_T = [None, None]
        ve_av = [None, None]
        ac_exp_last = None
        te_mask_last = [None, None]
        ev_pt_prev = None

        attn_view = attnfm.ap().rearrange("p (c t) -> p c t", c=NCH)
        psd_view = psD.ap().rearrange("p (c t) -> p c t", c=NCH)

        for qt in range(NQT):
            mk = mks[qt % 2]
            if te_mask_last[qt % 2] is not None:
                w_mk = [(s_te, te_mask_last[qt % 2])]
            elif qt == 0:
                w_mk = [(s_te, P.n(s_te))]
            else:
                w_mk = []
            P.dma("sync", mk[:], d_masks[qt], smk2[qt % 2], waits=w_mk)
            mk_ld = (smk2[qt % 2], P.n(smk2[qt % 2]))

            w0 = [(s_ve, max(ve_negm8, ve_gate))] if qt == 0 else [(s_ac, ac_exp_last)]
            P.add("vector", lambda e: e.memset(ltile[:], 0.0), waits=w0)
            P.add("vector", lambda e: e.memset(lgtile[:], 0.0))
            P.add("vector", (lambda e, qt=qt: e.tensor_scalar(
                out=gdiag[:], in0=ident[:], scalar1=gm1col[:, qt:qt + 1],
                scalar2=None, op0=ALU.mult)))
            P.add("vector", (lambda e, qt=qt: e.tensor_scalar(
                out=ggdiag[:], in0=ident[:], scalar1=gcol[:, qt:qt + 1],
                scalar2=None, op0=ALU.mult)), incs=[(s_ve, 1)])
            ve_qt = P.n(s_ve)

            for pr in range(8):
                c = pr
                qq, kk = q_fm[c], k_fm[c]
                j0 = qt * 128
                P.phase = "p2b_s"
                w_s = [(s_ac, ac_exp_last)] if ac_exp_last is not None else [(s_ld, n_ld3)]
                P.add("tensor", (lambda e, qq=qq, kk=kk, j0=j0: e.matmul(
                    psA[:, 0:512], qq[0:64, j0:j0 + 128], kk[0:64, j0:j0 + 512],
                    start=True, stop=False, tile_position=(0, 0))), waits=w_s)
                P.add("tensor", (lambda e, qq=qq, kk=kk, j0=j0: e.matmul(
                    psB[:, 0:512], qq[64:128, j0:j0 + 128], kk[64:128, j0:j0 + 512],
                    start=True, stop=False, tile_position=(64, 0))))
                P.add("tensor", (lambda e, qq=qq, kk=kk, j0=j0: e.matmul(
                    psA[:, 640:704], qq[0:64, j0:j0 + 128], kk[0:64, 2560:2624],
                    start=True, stop=False, tile_position=(0, 0))))
                P.add("tensor", (lambda e, qq=qq, kk=kk, j0=j0: e.matmul(
                    psB[:, 640:704], qq[64:128, j0:j0 + 128], kk[64:128, 2560:2624],
                    start=True, stop=False, tile_position=(64, 0))))
                P.add("tensor", (lambda e, qq=qq, kk=kk, j0=j0: e.matmul(
                    psA[:, 512:640], qq[0:64, j0:j0 + 128], kk[0:64, j0 + 512:j0 + 640],
                    start=True, stop=False, tile_position=(0, 0))))
                P.add("tensor", (lambda e, qq=qq, kk=kk, j0=j0: e.matmul(
                    psB[:, 512:640], qq[64:128, j0:j0 + 128], kk[64:128, j0 + 512:j0 + 640],
                    start=True, stop=False, tile_position=(64, 0))))
                P.add("tensor", (lambda e, mk=mk: e.matmul(
                    psA[:, 0:512], negi[:], mk[:, 0:512], start=False, stop=False)),
                      waits=[mk_ld])
                P.add("tensor", (lambda e, mk=mk: e.matmul(
                    psA[:, 512:640], negi[:], mk[:, 512:640], start=False, stop=True)),
                      incs=[(s_te, 1)])
                te_s = [P.n(s_te), None]
                P.add("tensor", (lambda e, mk=mk: e.matmul(
                    psB[:, 0:512], negi[:], mk[:, 0:512], start=False, stop=False)))
                P.add("tensor", (lambda e, mk=mk: e.matmul(
                    psB[:, 512:640], negi[:], mk[:, 512:640], start=False, stop=True)),
                      incs=[(s_te, 1)])
                te_s[1] = P.n(s_te)
                te_mask_last[qt % 2] = te_s[1]

                for hh in range(2):
                    h = 2 * pr + hh
                    ps = psA if hh == 0 else psB
                    pb = pbuf[hh]
                    P.phase = "p2b_e"
                    w_e = [(s_te, te_s[hh]), (s_ve, ve_qt)]
                    if te_T[hh] is not None:
                        w_e.append((s_te, te_T[hh]))
                    P.add("scalar", (lambda e, ps=ps, pb=pb, h=h: e.activation(
                        out=pb[:], in_=ps[:, 0:704], func=ACT.Exp, bias=negm8[:],
                        scale=0.125)),
                          waits=w_e, incs=[(s_ac, 1)])
                    ac_exp = P.n(s_ac)
                    ac_exp_last = ac_exp
                    P.phase = "p2b_d"
                    P.add("vector", (lambda e, pb=pb, h=h: e.tensor_scalar(
                        out=pb[:, 0:704], in0=pb[:, 0:704], scalar1=1.0, scalar2=0.0,
                        op0=ALU.mult, op1=ALU.add, accum_out=ltile[:, h:h + 1])),
                          waits=[(s_ac, ac_exp)])
                    P.add("vector", (lambda e, pb=pb, h=h: e.reduce_sum(
                        lgtile[:, h:h + 1], pb[:, 640:704], axis=AX)))
                    P.add("vector", (lambda e, h=h: e.tensor_sub(
                        ltile[:, h:h + 1], ltile[:, h:h + 1], lgtile[:, h:h + 1])))
                    P.add("vector", (lambda e, h=h: e.reciprocal(
                        rltile[:, h:h + 1], ltile[:, h:h + 1])))
                    P.add("vector", (lambda e, h=h: e.reciprocal(
                        rlgtile[:, h:h + 1], lgtile[:, h:h + 1])))
                    w_d = [(s_te, te_T[hh])] if te_T[hh] is not None else []
                    P.add("vector", (lambda e, h=h, hh=hh: e.tensor_scalar(
                        out=dlb[hh][:], in0=gdiag[:], scalar1=rltile[:, h:h + 1],
                        scalar2=None, op0=ALU.mult)), waits=w_d)
                    P.add("vector", (lambda e, h=h, hh=hh: e.tensor_scalar(
                        out=dgb[hh][:], in0=ggdiag[:], scalar1=rlgtile[:, h:h + 1],
                        scalar2=None, op0=ALU.mult)), incs=[(s_ve, 1)])
                    ve_d = P.n(s_ve)
                    P.phase = "p2b_t"
                    w_t = [(s_ve, ve_d)]
                    if ev_pt_prev is not None:
                        w_t.append(ev_pt_prev)
                    for s in range(5):
                        P.add("tensor", (lambda e, pb=pb, hh=hh, s=s: e.matmul(
                            psC[:, 128 * s:128 * (s + 1)], pb[:, 128 * s:128 * (s + 1)],
                            dlb[hh][:], start=True, stop=(s == 3))),
                              waits=(w_t if s == 0 else []))
                    P.add("tensor", (lambda e, pb=pb, hh=hh: e.matmul(
                        psC[0:64, 640:768], pb[:, 640:704], dgb[hh][:],
                        start=True, stop=True)), incs=[(s_te, 1)])
                    te_T[hh] = P.n(s_te)
                    P.phase = "p2b_v"
                    if qt == 0 and pr == 0 and hh == 0:
                        tap("pb0", [pb[:]], [(s_ac, ac_exp)])
                        tap("lt0", [ltile[:], lgtile[:], rltile[:], rlgtile[:]],
                            [(s_ve, ve_d)])
                        tap("dl0", [dlb[0][:], dgb[0][:]], [(s_ve, ve_d)])
                    if hh == 0:
                        P.add("scalar", (lambda e: e.activation(out=ptn[:], in_=psC[:, 0:768],
                                                                func=ACT.Copy)),
                              waits=[(s_te, te_T[hh])], incs=[(s_ac, 1)])
                        ev_pt = (s_ac, P.n(s_ac))
                        ev_pt_prev = ev_pt
                    else:
                        P.add("vector", (lambda e: e.tensor_copy(ptn[:], psC[:, 0:768])),
                              waits=[(s_te, te_T[hh])], incs=[(s_ve, 1)])
                        ev_pt = (s_ve, P.n(s_ve))
                    ev_pt_prev = ev_pt
                    if qt == 0 and pr == 0 and hh == 0:
                        tap("ptn0", [ptn[:]], [ev_pt])
                    P.phase = "p2b_pv"
                    bank = pr // 4
                    col0 = 512 * bank + 128 * (pr % 4)
                    w_pv = [ev_pt]
                    if pr % 4 == 0 and hh == 0 and ve_av[bank] is not None:
                        w_pv.append((s_ve, ve_av[bank]))
                    for s in range(5):
                        P.add("tensor", (lambda e, s=s, qt=qt, pr=pr, hh=hh, col0=col0: e.matmul(
                            psD[64 * hh:64 * hh + 64, col0:col0 + 128],
                            v_tok[qt + s][:, 128 * pr + 64 * hh:128 * pr + 64 * hh + 64],
                            ptn[:, 128 * s:128 * (s + 1)],
                            start=(s == 0), stop=False,
                            tile_position=(0, 64 * hh))),
                              waits=(w_pv if s == 0 else []))
                    is_last = (pr % 4 == 3 and hh == 1)
                    P.add("tensor", (lambda e, pr=pr, hh=hh, col0=col0: e.matmul(
                        psD[64 * hh:64 * hh + 64, col0:col0 + 128],
                        v_tok[20][0:64, 128 * pr + 64 * hh:128 * pr + 64 * hh + 64],
                        ptn[0:64, 640:768], start=False, stop=True,
                        tile_position=(0, 64 * hh))),
                          incs=([(s_te, 1)] if is_last else []))
                    if is_last:
                        te_pv = P.n(s_te)
                        P.phase = "p2b_av"
                        P.add("vector", (lambda e, qt=qt, bank=bank: e.tensor_copy(
                            attn_view[:, 4 * bank:4 * bank + 4, qt * 128:(qt + 1) * 128],
                            psd_view[:, 4 * bank:4 * bank + 4, 0:128])),
                              waits=[(s_te, te_pv)], incs=[(s_ve, 1)])
                        ve_av[bank] = P.n(s_ve)
        ve_attn = max(v for v in ve_av if v is not None)
        tap("attnfm", [attnfm[:]], [(s_ve, ve_attn)])

        P.phase = "p3"
        ctx_qkv.close()
        x_res = [sb(f"x{c}", [128, TOWN], F32) for c in range(NCH)]

        ctx_p3 = ExitStack()
        woutbf = [mksb(ctx_p3, f"wo{c}", [128, D], BF16, side="right") for c in range(NCH)]
        wostg = [mksb(ctx_p3, f"wostg{b}", [128, D], F32, side="right") for b in range(2)]
        ysb = [mksb(ctx_p3, f"ysb{c}", [128, GN], F32, side="right") for c in range(NCH)]

        for c in range(NCH):
            P.dma("sync", x_res[c][:], d_x[128 * c:128 * c + 128, 256:256 + TOWN], s_x,
                  waits=([(s_ve, ve_attn)] if c == 0 else []))
        x_ld = P.n(s_x)
        for c in range(NCH):
            stg = wostg[c % 2]
            w_w = [(s_ve, P.n(s_ve))] if c >= 2 else []
            P.dma("sync", stg[:], d_wout[128 * c:128 * c + 128, :], swo[c % 2], waits=w_w)
            P.add("vector", (lambda e, c=c, stg=stg: e.tensor_copy(woutbf[c][:], stg[:])),
                  waits=[(swo[c % 2], P.n(swo[c % 2]))], incs=[(s_ve, 1)])
        ve_wo = P.n(s_ve)

        for g in range(G):
            ac_prev = [None, None]
            for fo in range(NCH):
                reg = psA[:, 0:GN] if fo % 2 == 0 else psB[:, 0:GN]
                w_mm = [(s_ve, max(ve_wo, ve_attn))]
                if ac_prev[fo % 2] is not None:
                    w_mm.append((s_ac, ac_prev[fo % 2]))
                elif g > 0:
                    w_mm.append((s_ve, P.n(s_ve)))
                for fi in range(NCH):
                    P.add("tensor", (lambda e, fi=fi, fo=fo, g=g, reg=reg: e.matmul(
                        reg, woutbf[fi][:, 128 * fo:128 * (fo + 1)],
                        attn_view[:, fi, g * GN:(g + 1) * GN],
                        start=(fi == 0), stop=(fi == NCH - 1))),
                          waits=(w_mm if fi == 0 else []),
                          incs=([(s_te, 1)] if fi == NCH - 1 else []))
                te_mm = P.n(s_te)
                P.add("scalar", (lambda e, fo=fo, reg=reg: e.activation(
                    out=ysb[fo][:], in_=reg, func=ACT.Identity, bias=bout[:, fo:fo + 1],
                    scale=1.0)), waits=[(s_te, te_mm)], incs=[(s_ac, 1)])
                ac_prev[fo % 2] = P.n(s_ac)

            def put_res(cc, reg, te_val, g=g):
                P.add("vector", (lambda e, cc=cc, reg=reg: e.tensor_mul(
                    lnt[:, 0:GN], reg, rb_sb[:, 0:GN])),
                      waits=[(s_te, te_val)])
                P.add("vector", (lambda e, cc=cc, g=g: e.tensor_add(
                    x_res[cc][:, g * GN:(g + 1) * GN], x_res[cc][:, g * GN:(g + 1) * GN],
                    lnt[:, 0:GN])), incs=[(s_ve, 1)])

            w_ln = [(s_ac, P.n(s_ac))]
            if g == 0:
                w_ln.append((s_x, x_ld))
            emit_ln(GN, (lambda c: ysb[c][:]), put_res, w_first=w_ln)
        ve_p3 = P.n(s_ve)
        tap("x1", [x_res[c][:] for c in range(NCH)], [(s_ve, ve_p3)])
        ctx_p3.close()
        ctx_p2.close()

        P.phase = "p4"
        ctx_p4 = ExitStack()
        wtstg = [mksb(ctx_p4, f"wtstg{b}", [128, 1024], F32, side="right") for b in range(2)]
        wtbf = mksb(ctx_p4, "wtbf", [128, 1024], BF16, side="right")
        c1b = mksb(ctx_p4, "c1b", [128, 1024], BF16, side="right")
        weff = [mksb(ctx_p4, f"weff{b}", [128, 1024], BF16, side="right") for b in range(2)]
        tspill = [mksb(ctx_p4, f"tspill{b}", [128, 1024], BF16, side="right") for b in range(2)]
        scl = mksb(ctx_p4, "scl", [128, 8], F32, side="right")

        def tern_weight(d_w, d_dst, nt, wcols, ve_start):
            np_ = wcols // 1024
            ve_red = {}
            te_tr = {}
            ev_ps = {}
            st_tb = {}
            for t in range(nt):
                for dd in range(np_):
                    par = dd % 2
                    if par in ve_red:
                        w_w = [(s_ve, ve_red[par])]
                    elif t == 0:
                        w_w = [(s_ve, ve_start)]
                    else:
                        w_w = []
                    P.dma("sync", wtstg[par][:],
                          d_w[128 * t:128 * (t + 1), 1024 * dd:1024 * (dd + 1)], swt[par],
                          waits=w_w)
                    P.add("vector", (lambda e, par=par, dd=dd: e.reduce_sum(
                        scl[:, dd:dd + 1], wtstg[par][:], axis=AX,
                        apply_absolute_value=True)),
                          waits=[(swt[par], P.n(swt[par]))], incs=[(s_ve, 1)])
                    ve_red[par] = P.n(s_ve)
                if np_ > 1:
                    P.add("vector", (lambda e, np_=np_: e.reduce_sum(
                        scl[:, 4:5], scl[:, 0:np_], axis=AX)))
                else:
                    P.add("vector", (lambda e: e.tensor_copy(scl[:, 4:5], scl[:, 0:1])))
                P.add("vector", (lambda e, wcols=wcols: e.tensor_scalar(
                    out=scl[:, 5:6], in0=scl[:, 4:5], scalar1=1.0 / wcols,
                    scalar2=1e-5, op0=ALU.mult, op1=ALU.max)))
                P.add("vector", (lambda e: e.tensor_scalar(
                    out=scl[:, 6:7], in0=scl[:, 5:6], scalar1=0.5, scalar2=None,
                    op0=ALU.mult)))
                P.add("vector", (lambda e: e.tensor_scalar(
                    out=scl[:, 7:8], in0=scl[:, 5:6], scalar1=-0.5, scalar2=None,
                    op0=ALU.mult)), incs=[(s_ve, 1)])
                ve_scl = P.n(s_ve)
                for dd in range(np_):
                    par = dd % 2
                    P.dma("sync", wtstg[par][:],
                          d_w[128 * t:128 * (t + 1), 1024 * dd:1024 * (dd + 1)], swt[par],
                          waits=[(s_ve, ve_scl if dd < 2 else ve_red[par])])
                    P.add("vector", (lambda e, par=par: e.tensor_scalar(
                        out=c1b[:], in0=wtstg[par][:], scalar1=scl[:, 6:7], scalar2=scl[:, 5:6],
                        op0=ALU.is_gt, op1=ALU.mult)),
                          waits=[(swt[par], P.n(swt[par]))])
                    we = weff[par]
                    w_we = [(s_te, te_tr[par])] if par in te_tr else []
                    P.add("vector", (lambda e, we=we, par=par: e.tensor_scalar(
                        out=we[:], in0=wtstg[par][:], scalar1=scl[:, 7:8], scalar2=scl[:, 5:6],
                        op0=ALU.is_lt, op1=ALU.mult)), waits=w_we, incs=[(s_ve, 1)])
                    ve_red[par] = P.n(s_ve)
                    P.add("vector", (lambda e, we=we: e.tensor_sub(we[:], c1b[:], we[:])),
                          incs=[(s_ve, 1)])
                    ve_we = P.n(s_ve)
                    psT = psC if par == 0 else psD
                    w_tr = [(s_ve, ve_we)]
                    if id(psT) in ev_ps:
                        w_tr.append(ev_ps[id(psT)])
                    for j in range(8):
                        P.add("tensor", (lambda e, j=j, we=we, psT=psT: e.matmul(
                            psT[:, 128 * j:128 * (j + 1)], we[:, 128 * j:128 * (j + 1)],
                            ident[:], start=True, stop=(j in (3, 7)))),
                              waits=(w_tr if j == 0 else []),
                              incs=([(s_te, 1)] if j == 7 else []))
                    te_tr[par] = P.n(s_te)
                    tb = tspill[par]
                    w_e = [(s_te, te_tr[par])]
                    if par in st_tb:
                        w_e.append(st_tb[par])
                    P.add("scalar", (lambda e, tb=tb, psT=psT: e.activation(
                        out=tb[:], in_=psT[:, 0:1024], func=ACT.Copy)),
                          waits=w_e, incs=[(s_ac, 1)])
                    ev_ps[id(psT)] = (s_ac, P.n(s_ac))
                    P.add("gpsimd", (lambda e, t=t, dd=dd, tb=tb, d_dst=d_dst: e.dma_start(
                        out=d_dst.rearrange("j p n -> p j n")[:, 8 * dd:8 * dd + 8,
                                                              128 * t:128 * (t + 1)],
                        in_=tb.ap().rearrange("p (j n) -> p j n", j=8))),
                          waits=[ev_ps[id(psT)]], incs=[(ssp[par], 16)], dma=True)
                    st_tb[par] = (ssp[par], P.n(ssp[par]))

        tern_weight(d_wfc, d_tfcs, NHI, D, ve_p3)
        tern_weight(d_wproj, d_tproj, NCH, HIDN, ve_p3)
        w_spill = [(ssp[0], P.n(ssp[0])), (ssp[1], P.n(ssp[1]))]
        ctx_p4.close()

        P.phase = "p5"
        ctx_p5 = ExitStack()
        xn2 = [mksb(ctx_p5, f"xn2_{c}", [128, GN], BF16, side="right") for c in range(NCH)]
        r2 = mksb(ctx_p5, "r2", [128, NHI * GN], BF16, side="right")
        rbuf = [mksb(ctx_p5, f"rbuf{b}", [128, GN], BF16, side="right") for b in range(2)]
        t1bf = mksb(ctx_p5, "t1bf", [16, GN], BF16, side="right")
        t2bf = mksb(ctx_p5, "t2bf", [16, GN], BF16, side="right")
        fstg = [mksb(ctx_p5, f"fstg{b}", [128, 1024], BF16, side="right") for b in range(2)]
        pstg = [mksb(ctx_p5, f"pstg{b}", [128, HIDN], BF16, side="right") for b in range(2)]
        ysb5 = [mksb(ctx_p5, f"y5_{c}", [128, GN], F32, side="right") for c in range(NCH)]
        ustg = mksb(ctx_p5, "ustg", [16, 1024], F32, side="right")
        uT_bf = mksb(ctx_p5, "uT_bf", [16, HIDN], BF16, side="right")

        ve_u = None
        for dd in range(4):
            w_u = [(s_ve, ve_u)] if ve_u is not None else list(w_spill)
            P.dma("sync", ustg[:], d_uT[:, 1024 * dd:1024 * (dd + 1)], s_u, waits=w_u)
            P.add("vector", (lambda e, dd=dd: e.tensor_copy(
                uT_bf[:, 1024 * dd:1024 * (dd + 1)], ustg[:])),
                  waits=[(s_u, P.n(s_u))], incs=[(s_ve, 1)])
            ve_u = P.n(s_ve)

        tfcs_view = d_tfcs.rearrange("j p n -> p j n")
        tproj_view = d_tproj.rearrange("j p n -> p j n")

        for g in range(G):
            def put_xn2(c, reg, te_val):
                P.add("vector", (lambda e, c=c, reg=reg: e.tensor_mul(
                    xn2[c][:], reg, rb_sb[:, 0:GN])),
                      waits=[(s_te, te_val)], incs=[(s_ve, 1)])

            w_ln3 = [(s_ac, P.n(s_ac))] + list(w_spill)
            emit_ln(GN, (lambda c, g=g: x_res[c][:, g * GN:(g + 1) * GN]), put_xn2,
                    w_first=w_ln3)
            ve_xn2 = P.n(s_ve)
            for c in range(NCH):
                P.add("tensor", (lambda e, c=c: e.matmul(
                    psD[0:16, 0:GN], vT_bf[:, 16 * c:16 * (c + 1)], xn2[c][:],
                    start=(c == 0), stop=(c == NCH - 1))),
                      waits=([(s_ve, ve_xn2)] if c == 0 else []),
                      incs=([(s_te, 1)] if c == NCH - 1 else []))
            P.add("vector", lambda e: e.tensor_copy(t1bf[:], psD[0:16, 0:GN]),
                  waits=[(s_te, P.n(s_te))], incs=[(s_ve, 1)])
            P.add("tensor", lambda e: e.matmul(psD[0:16, 512:512 + GN], fbTg[:], t1bf[:],
                                               start=True, stop=True),
                  waits=[(s_ve, P.n(s_ve))], incs=[(s_te, 1)])
            P.add("vector", lambda e: e.tensor_copy(t2bf[:], psD[0:16, 512:512 + GN]),
                  waits=[(s_te, P.n(s_te))], incs=[(s_ve, 1)])
            ve_t2 = P.n(s_ve)
            ac_r = [None, None]
            ve_r2l = [None, None]
            te_fcmm = {0: None, 1: None}
            for hi in range(NHI):
                fpar = hi % 2
                fst = fstg[fpar]
                if te_fcmm[fpar] is not None:
                    w_f = [(s_te, te_fcmm[fpar])]
                elif g == 0 and hi < 2:
                    w_f = list(w_spill)
                else:
                    w_f = []
                P.dma("sync", fst.ap().rearrange("p (j n) -> p j n", j=8),
                      tfcs_view[:, :, 128 * hi:128 * (hi + 1)], sfc2[fpar], waits=w_f)
                w_fl = (sfc2[fpar], P.n(sfc2[fpar]))
                reg = psA[:, 0:GN] if hi % 2 == 0 else psB[:, 0:GN]
                w_mm = [(s_ve, ve_t2), w_fl]
                if ac_r[hi % 2] is not None:
                    w_mm.append((s_ac, ac_r[hi % 2]))
                for fi in range(NCH):
                    P.add("tensor", (lambda e, fi=fi, fst=fst, reg=reg: e.matmul(
                        reg, fst[:, 128 * fi:128 * (fi + 1)], xn2[fi][:],
                        start=(fi == 0), stop=False)),
                          waits=(w_mm if fi == 0 else []))
                P.add("tensor", (lambda e, hi=hi, reg=reg: e.matmul(
                    reg, uT_bf[:, 128 * hi:128 * (hi + 1)], t2bf[:],
                    start=False, stop=True)), incs=[(s_te, 1)])
                te_h = P.n(s_te)
                te_fcmm[fpar] = te_h
                rb = rbuf[hi % 2]
                w_relu = [(s_te, te_h)]
                if ve_r2l[hi % 2] is not None:
                    w_relu.append((s_ve, ve_r2l[hi % 2]))
                P.add("scalar", (lambda e, rb=rb, reg=reg: e.activation(
                    out=rb[:], in_=reg, func=ACT.Relu)),
                      waits=w_relu, incs=[(s_ac, 1)])
                ac_r[hi % 2] = P.n(s_ac)
                P.add("vector", (lambda e, hi=hi, rb=rb: e.tensor_mul(
                    r2[:, hi * GN:(hi + 1) * GN], rb[:], rb[:])),
                      waits=[(s_ac, ac_r[hi % 2])], incs=[(s_ve, 1)])
                ve_r2l[hi % 2] = P.n(s_ve)
            ve_r2 = P.n(s_ve)
            ac_y = [None, None]
            te_projmm = {0: None, 1: None}
            for fo in range(NCH):
                pstgb = pstg[fo % 2]
                if te_projmm[fo % 2] is not None:
                    w_p = [(s_te, te_projmm[fo % 2])]
                elif g == 0:
                    w_p = list(w_spill)
                else:
                    w_p = []
                for dd in range(4):
                    P.dma("gpsimd", pstgb.ap().rearrange("p (j n) -> p j n", j=NHI)[
                        :, 8 * dd:8 * dd + 8, :],
                        tproj_view[:, 8 * dd:8 * dd + 8, 128 * fo:128 * (fo + 1)],
                        spr[fo % 2], waits=(w_p if dd == 0 else []))
                w_pl = (spr[fo % 2], P.n(spr[fo % 2]))
                reg = psC[:, 0:GN] if fo % 2 == 0 else psC[:, 512:512 + GN]
                w_y = [w_pl, (s_ve, ve_r2)]
                if ac_y[fo % 2] is not None:
                    w_y.append((s_ac, ac_y[fo % 2]))
                for hi in range(NHI):
                    P.add("tensor", (lambda e, hi=hi, reg=reg, pstgb=pstgb: e.matmul(
                        reg, pstgb[:, 128 * hi:128 * (hi + 1)], r2[:, hi * GN:(hi + 1) * GN],
                        start=(hi == 0), stop=(hi == NHI - 1))),
                          waits=(w_y if hi == 0 else []),
                          incs=([(s_te, 1)] if hi == NHI - 1 else []))
                te_projmm[fo % 2] = P.n(s_te)
                te_y = P.n(s_te)
                P.add("scalar", (lambda e, fo=fo, reg=reg: e.activation(
                    out=ysb5[fo][:], in_=reg, func=ACT.Copy)),
                      waits=[(s_te, te_y)], incs=[(s_ac, 1)])
                ac_y[fo % 2] = P.n(s_ac)

            def put_res5(cc, reg, te_val, g=g):
                P.add("vector", (lambda e, cc=cc, reg=reg: e.tensor_mul(
                    lnt[:, 0:GN], reg, rb_sb[:, 0:GN])),
                      waits=[(s_te, te_val)])
                P.add("vector", (lambda e, cc=cc, g=g: e.tensor_add(
                    x_res[cc][:, g * GN:(g + 1) * GN], x_res[cc][:, g * GN:(g + 1) * GN],
                    lnt[:, 0:GN])), incs=[(s_ve, 1)])

            emit_ln(GN, (lambda c: ysb5[c][:]), put_res5, w_first=[(s_ac, P.n(s_ac))])
        ve_p5 = P.n(s_ve)
        tap("r2", [r2[:]], [(s_ve, ve_p5)])
        ctx_p5.close()

        P.phase = "p6"
        for c in range(NCH):
            P.dma("gpsimd", d_out[128 * c:128 * c + 128, :], x_res[c][:], s_st,
                  waits=([(s_ve, ve_p5)] if c == 0 else []))
        final_sem, final = s_st, P.n(s_st)

        block = ctx.enter_context(nc.Block())
        for eng_name, deco in [("sync", block.sync), ("tensor", block.tensor),
                               ("scalar", block.scalar), ("vector", block.vector),
                               ("gpsimd", block.gpsimd)]:
            def make(eng_name=eng_name):
                def _(e):
                    P.run(e, eng_name)
                    if eng_name == "gpsimd":
                        e.wait_ge(final_sem, final)
                return _
            deco(make())
    return nc


_NC_CACHE = {}
DEBUG = ()


def kernel(L0_attn_hidden_states, L0_attn_Wqkv, L0_attn_bqkv, L0_attn_Wout, L0_attn_bout,
           L0_attn_Wg, L0_attn_bg, L0_ffn_fast_b, L0_ffn_w_fc, L0_ffn_w_proj, L0_ffn_u,
           L0_ffn_v, L0_ffn_gate):
    f = np.float32
    x = np.ascontiguousarray(np.asarray(L0_attn_hidden_states, dtype=f))
    wqkv = np.ascontiguousarray(np.asarray(L0_attn_Wqkv, dtype=f))
    bqkv = np.asarray(L0_attn_bqkv, dtype=f).reshape(-1)
    wout = np.ascontiguousarray(np.asarray(L0_attn_Wout, dtype=f))
    bout = np.asarray(L0_attn_bout, dtype=f).reshape(-1)
    wgv = np.ascontiguousarray(np.asarray(L0_attn_Wg, dtype=f))
    bg = np.asarray(L0_attn_bg, dtype=f).reshape(1, 1)
    fb = np.asarray(L0_ffn_fast_b, dtype=f)
    wfc = np.ascontiguousarray(np.asarray(L0_ffn_w_fc, dtype=f))
    wproj = np.ascontiguousarray(np.asarray(L0_ffn_w_proj, dtype=f))
    uu = np.asarray(L0_ffn_u, dtype=f)
    vv = np.asarray(L0_ffn_v, dtype=f)
    gate = np.asarray(L0_ffn_gate, dtype=f).reshape(1, 1)

    e16 = np.zeros((128, NCH * 16), dtype=BF)
    for c in range(NCH):
        e16[0:64, c * 16 + 2 * c] = 1
        e16[64:128, c * 16 + 2 * c + 1] = 1

    shared = {
        "wqkv": wqkv,
        "bqkv_pp": np.ascontiguousarray(bqkv.reshape(24, 128).T),
        "bv_row": np.ascontiguousarray(bqkv[2048:3072].reshape(1, D)),
        "wout": wout,
        "bout_pp": np.ascontiguousarray(bout.reshape(8, 128).T),
        "wg": wgv,
        "bg": bg,
        "w_fc": wfc,
        "w_proj": wproj,
        "uT": np.ascontiguousarray(uu.T),
        "vT": np.ascontiguousarray(vv.T),
        "fbT": np.ascontiguousarray(fb.T),
        "gate": gate,
        "ident": np.eye(128, dtype=BF),
        "negi": (np.eye(128) * -1.0e9).astype(BF),
        "i16": np.eye(16, dtype=BF),
        "row1": np.ones((1, 128), dtype=BF),
        "statc": np.full((128, 1), -1.0 / 1024.0, dtype=BF),
        "one11": np.ones((1, 1), dtype=BF),
        "e16": e16,
    }

    in_maps = []
    for core in range(8):
        seq, hh = core // 2, core % 2
        start = hh * TOWN
        rows = np.zeros((EXT, D), dtype=f)
        lo = max(0, start - 256)
        rows[256 - (start - lo):256] = x[seq, lo:start]
        rows[256:256 + TOWN] = x[seq, start:start + TOWN]
        hi = min(T, start + TOWN + 256)
        rows[2304:2304 + (hi - start - TOWN)] = x[seq, start + TOWN:hi]
        rows[2560:2624] = x[seq, 0:NG]
        xe = np.ascontiguousarray(rows.T)

        t01 = np.zeros((NQT, 128, 640), dtype=BF)
        a = np.arange(128)[:, None]
        b = np.arange(640)[None, :]
        for qt in range(NQT):
            band = (b >= a) & (b <= a + 512)
            e_idx = 128 * qt + b
            gk = 2048 * hh + (e_idx - 256)
            valid = (gk >= 0) & (gk < T)
            t01[qt] = (~(band & valid)).astype(BF)

        m = dict(shared)
        m["x_ext"] = xe
        m["masks"] = np.ascontiguousarray(t01)
        in_maps.append(m)

    key = ("nc",) + tuple(sorted(DEBUG))
    if key not in _NC_CACHE:
        _NC_CACHE[key] = build_program(debug=DEBUG)
    nc = _NC_CACHE[key]
    res = run_bass_kernel_spmd(nc, in_maps, core_ids=list(range(8)))
    kernel.last_results = res.results

    out = np.empty((B, T, D), dtype=f)
    for core in range(8):
        seq, hh = core // 2, core % 2
        out[seq, hh * TOWN:(hh + 1) * TOWN] = res.results[core]["out"].T
    return out


# version 04
# speedup vs baseline: 1.2070x; 1.2070x over previous
import numpy as np
import ml_dtypes
from contextlib import ExitStack

import concourse.bass as bass
import concourse.mybir as mybir
from concourse.bass_utils import run_bass_kernel_spmd

F32 = mybir.dt.float32
BF16 = mybir.dt.bfloat16
BF = ml_dtypes.bfloat16
AX = mybir.AxisListType.X
ALU = mybir.AluOpType
ACT = mybir.ActivationFunctionType

B, T, D, H, HD = 4, 4096, 1024, 16, 64
WLOC, NG = 256, 64
TOWN = 2048
EXT = 2688
NQT = TOWN // 128
NKT = EXT // 128
NCH = 8
HIDN = 4096
NHI = HIDN // 128
RANK = 16

G1N, G1 = 448, 6
GN, G = 512, 4


SAFE_MODE = 1
SAFE_PHASES = None
UNSAFE_PHASES = ()
DMA_FINE = True
DMA_FINE_PHASES = None


class Plan:
    def __init__(self):
        self.ops = []
        self.count = {}
        self.safe = SAFE_MODE
        self.phase = "p0"
        self.eng_sem = {}
        self.sems = []
        self.joined = {}

    def n(self, sem):
        return self.count.get(id(sem), 0)

    def add(self, eng, fn, waits=(), incs=(), dma=False):
        waits = list(waits)
        incs = list(incs)
        for s, v in waits:
            assert v <= self.n(s), f"wait for future sem value {v} > {self.n(s)}"
        base = self.phase.split("_")[0]
        insafe = (SAFE_PHASES is None or self.phase in SAFE_PHASES or
                  base in SAFE_PHASES) and \
            self.phase not in UNSAFE_PHASES
        if dma and DMA_FINE and (DMA_FINE_PHASES is None or base in DMA_FINE_PHASES
                                 or self.phase in DMA_FINE_PHASES):
            insafe = False
        if self.safe and insafe:
            if not incs and eng in self.eng_sem:
                incs = [(self.eng_sem[eng], 1)]
            j = self.joined.setdefault(eng, {})
            for s in self.sems:
                nv = self.n(s)
                if nv > j.get(id(s), 0):
                    waits.append((s, nv))
                    j[id(s)] = nv
        for s, k in incs:
            self.count[id(s)] = self.n(s) + k
        self.ops.append((eng, fn, tuple(waits), tuple(incs)))

    def dma(self, eng, out, in_, semh, waits=(), slow=False):
        self.add(eng, (lambda e, out=out, in_=in_, slow=slow: e.dma_start(
            out=out, in_=in_, allow_slow_non_contiguous=slow)),
                 waits=waits, incs=[(semh, 16)], dma=True)

    def run(self, e, eng):
        for oeng, fn, waits, incs in self.ops:
            if oeng != eng:
                continue
            for s, v in waits:
                e.wait_ge(s, v)
            inst = fn(e)
            for s, k in incs:
                inst = inst.then_inc(s, k)


def build_program(debug=()):
    debug = set(debug)
    nc = bass.Bass("TRN2", target_bir_lowering=False)

    d_x = nc.dram_tensor("x_ext", [D, EXT], F32, kind="ExternalInput").ap()
    d_wqkv = nc.dram_tensor("wqkv", [D, 3 * D], F32, kind="ExternalInput").ap()
    d_bqkv = nc.dram_tensor("bqkv_pp", [128, 24], F32, kind="ExternalInput").ap()
    d_bvrow = nc.dram_tensor("bv_row", [1, D], F32, kind="ExternalInput").ap()
    d_wout = nc.dram_tensor("wout", [D, D], F32, kind="ExternalInput").ap()
    d_bout = nc.dram_tensor("bout_pp", [128, 8], F32, kind="ExternalInput").ap()
    d_wg = nc.dram_tensor("wg", [D, 1], F32, kind="ExternalInput").ap()
    d_bg = nc.dram_tensor("bg", [1, 1], F32, kind="ExternalInput").ap()
    d_wfc = nc.dram_tensor("w_fc", [HIDN, D], F32, kind="ExternalInput").ap()
    d_wproj = nc.dram_tensor("w_proj", [D, HIDN], F32, kind="ExternalInput").ap()
    d_uT = nc.dram_tensor("uT", [RANK, HIDN], F32, kind="ExternalInput").ap()
    d_vT = nc.dram_tensor("vT", [D, RANK], F32, kind="ExternalInput").ap()
    d_fbT = nc.dram_tensor("fbT", [RANK, RANK], F32, kind="ExternalInput").ap()
    d_gate = nc.dram_tensor("gate", [1, 1], F32, kind="ExternalInput").ap()
    d_masks = nc.dram_tensor("masks", [NQT, 128, 640], BF16, kind="ExternalInput").ap()
    d_ident = nc.dram_tensor("ident", [128, 128], BF16, kind="ExternalInput").ap()
    d_negi = nc.dram_tensor("negi", [128, 128], BF16, kind="ExternalInput").ap()
    d_e16 = nc.dram_tensor("e16", [128, NCH * 16], BF16, kind="ExternalInput").ap()
    d_i16 = nc.dram_tensor("i16", [16, 16], BF16, kind="ExternalInput").ap()
    d_row1 = nc.dram_tensor("row1", [1, 128], BF16, kind="ExternalInput").ap()
    d_statc = nc.dram_tensor("statc", [128, 1], BF16, kind="ExternalInput").ap()
    d_one11 = nc.dram_tensor("one11", [1, 1], BF16, kind="ExternalInput").ap()

    d_out = nc.dram_tensor("out", [D, TOWN], F32, kind="ExternalOutput").ap()
    d_tproj = nc.dram_tensor("tproj_spill", [NHI, 128, D], BF16).ap()
    d_tfcs = nc.dram_tensor("tfc_spill", [NCH, 128, HIDN], BF16).ap()

    P = Plan()
    dbg_outs = {}

    def tap(name, src_aps, waits):
        if name not in debug:
            return
        p0, f0 = src_aps[0].shape[-2], src_aps[0].shape[-1]
        dt = src_aps[0].dtype
        dten = nc.dram_tensor("dbg_" + name, [len(src_aps), p0, f0], dt,
                              kind="ExternalOutput").ap()
        dbg_outs[name] = True
        for i, ap in enumerate(src_aps):
            P.dma("gpsimd", dten[i], ap, s_st, waits=(waits if i == 0 else []))

    ctx = ExitStack()
    with ctx:
        def mksb(stack, name, shape, dt=F32, side="left"):
            return stack.enter_context(nc.sbuf_tensor("sb_" + name, shape, dt, side=side))

        def sb(name, shape, dt=F32, side="left"):
            return mksb(ctx, name, shape, dt, side)

        def sem(name):
            return ctx.enter_context(nc.semaphore(name))

        s_ld = sem("s_ld"); s_x = sem("s_x"); s_w = sem("s_w"); s_mk = sem("s_mk")
        s_st = sem("s_st"); s_te = sem("s_te"); s_ve = sem("s_ve"); s_ac = sem("s_ac")
        s_gp = sem("s_gp")
        sxp = [sem("sxp0"), sem("sxp1")]
        swq = [sem("swq0"), sem("swq1")]
        swv = [sem("swv0"), sem("swv1")]
        swo = [sem("swo0"), sem("swo1")]
        swt = [sem("swt0"), sem("swt1")]
        sfc2 = [sem("sfc0"), sem("sfc1")]
        spr = [sem("spr0"), sem("spr1")]
        ssp = [sem("ssp0"), sem("ssp1")]
        smk2 = [sem("smk0"), sem("smk1")]
        s_u = sem("s_u")
        P.eng_sem = {"tensor": s_te, "vector": s_ve, "scalar": s_ac, "gpsimd": s_gp}
        P.sems = [s_ld, s_x, s_w, s_mk, s_st, s_te, s_ve, s_ac, s_gp, s_u] + \
            sxp + swq + swv + swo + swt + sfc2 + spr + ssp + smk2

        psA = ctx.enter_context(nc.psum_tensor("psA", [128, 1024], F32))
        psB = ctx.enter_context(nc.psum_tensor("psB", [128, 1024], F32))
        psC = ctx.enter_context(nc.psum_tensor("psC", [128, 1024], F32))
        psD = ctx.enter_context(nc.psum_tensor("psD", [128, 1024], F32))

        ident = sb("ident", [128, 128], BF16)
        i16 = sb("i16", [16, 16], BF16)
        row1 = sb("row1", [1, 128], BF16)
        statc = sb("statc", [128, 1], BF16)
        one11 = sb("one11", [1, 1], BF16)
        bqkv = sb("bqkv", [128, 24], F32)
        bvrow_bf = sb("bvrow_bf", [1, D], BF16)
        bout = sb("bout", [128, 8], F32)
        wg = sb("wg", [128, 8], F32)
        wg_bf = sb("wg_bf", [128, 8], BF16)
        bg11 = sb("bg11", [1, 1], F32)
        bg_bf = sb("bg_bf", [1, 1], BF16)
        bg128 = sb("bg128", [128, 1], F32)
        gate11 = sb("gate11", [1, 1], F32)
        gate_bf = sb("gate_bf", [1, 1], BF16)
        g16 = sb("g16", [16, 1], F32)
        vT_f = sb("vT_f", [128, NCH * 16], F32)
        vT_bf = sb("vT_bf", [128, NCH * 16], BF16)
        fbT_f = sb("fbT_f", [16, 16], F32)
        fbTg = sb("fbTg", [16, 16], BF16)
        gcol = sb("gcol", [128, NQT], F32)
        gm1col = sb("gm1col", [128, NQT], F32)
        negm8 = sb("negm8", [128, 1], F32)
        nm8bf = sb("nm8bf", [1, 1], BF16)
        rowbuf = sb("rowbuf", [1, 2 * GN], F32)
        rowbuf2 = sb("rowbuf2", [1, GN], F32)
        rown_bf = sb("rown_bf", [1, GN], BF16)
        rowr_bf = sb("rowr_bf", [1, GN], BF16)
        rb_sb = sb("rb_sb", [128, GN], F32)
        lnx = sb("lnx", [128, NCH * GN], BF16)
        lnx2 = sb("lnx2", [128, 2 * GN], BF16)
        lnt = sb("lnt", [128, GN], BF16)
        ltile = sb("ltile", [128, 16], F32)
        ltecho = sb("ltecho", [128, 16], F32)
        lgtile = sb("lgtile", [128, 16], F32)
        rltile = sb("rltile", [128, 16], F32)
        rlgtile = sb("rlgtile", [128, 16], F32)
        bndrow = sb("bndrow", [1, 16], F32)
        bnd16 = sb("bnd16", [16, 4], F32)
        bnd16b = sb("bnd16b", [16, 1], BF16)
        bndsc = sb("bndsc", [16, 12], F32)

        ctx_qkv = ExitStack()
        q_fm = [mksb(ctx_qkv, f"q{c}", [128, TOWN], BF16) for c in range(NCH)]
        k_fm = [mksb(ctx_qkv, f"k{c}", [128, EXT], BF16) for c in range(NCH)]
        v_tok = [mksb(ctx_qkv, f"v{t}", [128, D], BF16) for t in range(NKT)]

        for o, i in [(ident, d_ident), (i16, d_i16), (row1, d_row1),
                     (statc, d_statc), (one11, d_one11), (bqkv, d_bqkv),
                     (bout, d_bout), (bg11, d_bg), (gate11, d_gate), (fbT_f, d_fbT)]:
            P.dma("sync", o[:], i, s_ld)
        P.dma("sync", wg.ap().rearrange("p (c o) -> p c o", o=1),
              d_wg.rearrange("(c p) o -> p c o", p=128), s_ld, slow=True)
        P.dma("sync", vT_f.ap().rearrange("p (c n) -> p c n", c=NCH),
              d_vT.rearrange("(c p) n -> p c n", p=128), s_ld)
        P.dma("sync", rowbuf[0:1, 0:D], d_bvrow, s_ld)
        n_ld = P.n(s_ld)

        P.add("vector", lambda e: e.tensor_copy(bvrow_bf[:], rowbuf[0:1, 0:D]),
              waits=[(s_ld, n_ld)])
        P.add("vector", lambda e: e.tensor_copy(wg_bf[:], wg[:]))
        P.add("vector", lambda e: e.tensor_copy(vT_bf[:], vT_f[:]))
        P.add("vector", lambda e: e.tensor_copy(gate_bf[:], gate11[:]))
        P.add("vector", lambda e: e.tensor_copy(bg_bf[:], bg11[:]), incs=[(s_ve, 1)])
        ve0 = P.n(s_ve)
        P.add("tensor", lambda e: e.matmul(psD[0:16, 0:1], row1[:, 0:16], gate_bf[:],
                                           start=True, stop=True),
              waits=[(s_ve, ve0)], incs=[(s_te, 1)])
        P.add("vector", lambda e: e.tensor_copy(g16[:], psD[0:16, 0:1]),
              waits=[(s_te, P.n(s_te))])
        P.add("vector", lambda e: e.tensor_scalar(out=fbTg[:], in0=fbT_f[:], scalar1=g16[:],
                                                  scalar2=None, op0=ALU.mult), incs=[(s_ve, 1)])
        P.add("tensor", lambda e: e.matmul(psD[:, 1:2], row1[:], bg_bf[:], start=True, stop=True),
              waits=[(s_ve, P.n(s_ve))], incs=[(s_te, 1)])
        P.add("vector", lambda e: e.tensor_copy(bg128[:], psD[:, 1:2]),
              waits=[(s_te, P.n(s_te))], incs=[(s_ve, 1)])
        ve_consts = P.n(s_ve)

        def emit_ln(gn, get_src, put, w_first=(), cast_hook=None, post_cast=None):
            te_sq = {}
            for c in range(NCH):
                w_c = list(w_first) if c == 0 else []
                if cast_hook is not None:
                    w_c = w_c + cast_hook(c)
                sq = lnx2[:, (c % 2) * gn:(c % 2) * gn + gn]
                if c >= 2:
                    w_c.append((s_te, te_sq[c - 2]))
                P.add("vector",
                      (lambda e, c=c: e.tensor_copy(lnx[:, c * gn:(c + 1) * gn], get_src(c))),
                      waits=w_c)
                P.add("vector",
                      (lambda e, c=c, sq=sq: e.tensor_mul(sq, lnx[:, c * gn:(c + 1) * gn],
                                                          lnx[:, c * gn:(c + 1) * gn])),
                      incs=[(s_ve, 1)])
                ve_c = P.n(s_ve)
                if post_cast is not None:
                    post_cast(c)
                P.add("tensor",
                      (lambda e, c=c: e.matmul(psD[0:1, 0:gn], statc[:],
                                               lnx[:, c * gn:(c + 1) * gn],
                                               start=(c == 0), stop=(c == NCH - 1))),
                      waits=[(s_ve, ve_c)])
                P.add("tensor",
                      (lambda e, c=c, sq=sq: e.matmul(psD[0:1, 512:512 + gn], statc[:], sq,
                                               start=(c == 0), stop=(c == NCH - 1))),
                      incs=[(s_te, 1)])
                te_sq[c] = P.n(s_te)
            te_stats = P.n(s_te)
            P.add("vector", lambda e: e.tensor_copy(rowbuf[:, 0:gn], psD[0:1, 0:gn]),
                  waits=[(s_te, te_stats)])
            P.add("vector", lambda e: e.tensor_copy(rowbuf[:, gn:2 * gn],
                                                    psD[0:1, 512:512 + gn]))
            P.add("vector", lambda e: e.tensor_mul(rowbuf2[:, 0:gn], rowbuf[:, 0:gn],
                                                   rowbuf[:, 0:gn]))
            P.add("vector", lambda e: e.scalar_tensor_tensor(
                out=rowbuf2[:, 0:gn], in0=rowbuf[:, gn:2 * gn], scalar=-1.0,
                in1=rowbuf2[:, 0:gn], op0=ALU.mult, op1=ALU.subtract))
            P.add("vector", lambda e: e.tensor_copy(rown_bf[:, 0:gn], rowbuf[:, 0:gn]),
                  incs=[(s_ve, 1)])
            ve_rows = P.n(s_ve)
            P.add("scalar", lambda e: e.activation(out=rowbuf2[:, 0:gn], in_=rowbuf2[:, 0:gn],
                                                   func=ACT.Sqrt, bias=np.float32(1e-5),
                                                   scale=1.0),
                  waits=[(s_ve, ve_rows)], incs=[(s_ac, 1)])
            P.add("vector", lambda e: e.reciprocal(rowbuf2[:, 0:gn], rowbuf2[:, 0:gn]),
                  waits=[(s_ac, P.n(s_ac))])
            P.add("vector", lambda e: e.tensor_copy(rowr_bf[:, 0:gn], rowbuf2[:, 0:gn]),
                  incs=[(s_ve, 1)])
            ve_r = P.n(s_ve)
            P.add("tensor", lambda e: e.matmul(psC[:, 0:gn], row1[:], rowr_bf[:, 0:gn],
                                               start=True, stop=True),
                  waits=[(s_ve, ve_r)], incs=[(s_te, 1)])
            P.add("vector", lambda e: e.tensor_copy(rb_sb[:, 0:gn], psC[:, 0:gn]),
                  waits=[(s_te, P.n(s_te))], incs=[(s_ve, 1)])
            ve_rb = P.n(s_ve)
            for c in range(NCH):
                reg = psA[:, 0:gn] if c % 2 == 0 else psB[:, 0:gn]
                w_reg = [(s_ve, P.n(s_ve))] if c >= 2 else [(s_ve, ve_rb)]
                P.add("tensor", (lambda e, reg=reg: e.matmul(reg, row1[:], rown_bf[:, 0:gn],
                                                             start=True, stop=False)),
                      waits=w_reg)
                P.add("tensor", (lambda e, c=c, reg=reg: e.matmul(
                    reg, ident[:], lnx[:, c * gn:(c + 1) * gn], start=False, stop=True)),
                      incs=[(s_te, 1)])
                put(c, reg, P.n(s_te))

        P.phase = "p1a"
        ctx_p1 = ExitStack()
        xn = [mksb(ctx_p1, f"xn{c}", [128, EXT], BF16, side="right") for c in range(NCH)]
        ctx_p1a = ExitStack()
        xstg = [mksb(ctx_p1a, f"xstg{b}", [128, G1N], F32, side="right") for b in range(2)]

        x_view = d_x.rearrange("(cc p) n -> cc p n", p=128)
        cast_ve = {}

        for g in range(G1):
            x_lds = {}

            def plan_xdma(g, c):
                prev = (g, c - 2) if c >= 2 else ((g - 1, NCH - 2 + c) if g > 0 else None)
                if prev is not None and prev in cast_ve:
                    w = [(s_ve, cast_ve[prev])]
                else:
                    w = [(s_ve, ve_consts)]
                P.dma("sync", xstg[c % 2][:],
                      x_view[c, :, g * G1N:(g + 1) * G1N], sxp[c % 2], waits=w)
                x_lds[(g, c)] = (sxp[c % 2], P.n(sxp[c % 2]))

            plan_xdma(g, 0)
            plan_xdma(g, 1)

            def cast_hook(c, g=g):
                return [x_lds[(g, c)]]

            def post_cast(c, g=g):
                cast_ve[(g, c)] = P.n(s_ve)
                if c + 2 < NCH:
                    plan_xdma(g, c + 2)

            def put_xn(c, reg, te_val, g=g):
                P.add("vector", (lambda e, c=c, g=g, reg=reg: e.tensor_mul(
                    xn[c][:, g * G1N:(g + 1) * G1N], reg, rb_sb[:, 0:G1N])),
                      waits=[(s_te, te_val)], incs=[(s_ve, 1)])

            emit_ln(G1N, (lambda c: xstg[c % 2][:]), put_xn, cast_hook=cast_hook,
                    post_cast=post_cast)
        ve_ln1 = P.n(s_ve)
        tap("xn", [xn[c][:] for c in range(NCH)], [(s_ve, ve_ln1)])
        ctx_p1a.close()

        P.phase = "p1b"
        ctx_qk = ExitStack()
        wstg = [mksb(ctx_qk, f"wstg{b}", [128, 1024], F32, side="right") for b in range(2)]
        wbfq = [mksb(ctx_qk, f"wbfq{b}", [128, 1024], BF16, side="right") for b in range(2)]

        te_lastmm = {0: None, 1: None}
        cast_done_w = {0: None, 1: None}
        ridx = 0
        ac_ev = [None, None]
        for fo in range(16):
            par = fo % 2
            stg, wbf = wstg[par], wbfq[par]
            w_w = [(s_ve, cast_done_w[par])] if cast_done_w[par] is not None else \
                [(s_ve, ve_ln1)]
            for fi in range(NCH):
                P.dma("sync", stg[:, fi * 128:(fi + 1) * 128],
                      d_wqkv[128 * fi:128 * fi + 128, 128 * fo:128 * fo + 128], swq[par],
                      waits=(w_w if fi == 0 else []))
            w_ld = P.n(swq[par])
            w_cast = [(swq[par], w_ld)]
            if te_lastmm[par] is not None:
                w_cast.append((s_te, te_lastmm[par]))
            P.add("vector", (lambda e, stg=stg, wbf=wbf: e.tensor_copy(wbf[:], stg[:])),
                  waits=w_cast, incs=[(s_ve, 1)])
            ve_cast = P.n(s_ve)
            cast_done_w[par] = ve_cast
            is_q = fo < 8
            ngrp = 4 if is_q else G1
            gn = 512 if is_q else G1N
            for g in range(ngrp):
                reg = psA[:, 0:gn] if ridx % 2 == 0 else psB[:, 0:gn]
                w_mm = [(s_ve, ve_cast)]
                if ac_ev[ridx % 2] is not None:
                    w_mm.append((s_ac, ac_ev[ridx % 2]))
                for fi in range(NCH):
                    if is_q:
                        mk_mm = (lambda e, fi=fi, g=g, wbf=wbf, reg=reg: e.matmul(
                            reg, wbf[:, fi * 128:(fi + 1) * 128],
                            xn[fi][:, 256 + g * 512:256 + (g + 1) * 512],
                            start=(fi == 0), stop=(fi == NCH - 1)))
                    else:
                        mk_mm = (lambda e, fi=fi, g=g, wbf=wbf, reg=reg: e.matmul(
                            reg, wbf[:, fi * 128:(fi + 1) * 128],
                            xn[fi][:, g * G1N:(g + 1) * G1N],
                            start=(fi == 0), stop=(fi == NCH - 1)))
                    P.add("tensor", mk_mm, waits=(w_mm if fi == 0 else []),
                          incs=([(s_te, 1)] if fi == NCH - 1 else []))
                te_mm = P.n(s_te)
                te_lastmm[par] = te_mm
                if is_q:
                    dst = (lambda e, fo=fo, g=g, reg=reg: e.activation(
                        out=q_fm[fo][:, g * 512:(g + 1) * 512], in_=reg, func=ACT.Identity,
                        bias=bqkv[:, fo:fo + 1], scale=1.0))
                else:
                    dst = (lambda e, fo=fo, g=g, reg=reg: e.activation(
                        out=k_fm[fo - 8][:, g * G1N:(g + 1) * G1N], in_=reg, func=ACT.Identity,
                        bias=bqkv[:, fo:fo + 1], scale=1.0))
                P.add("scalar", dst, waits=[(s_te, te_mm)], incs=[(s_ac, 1)])
                ac_ev[ridx % 2] = P.n(s_ac)
                ridx += 1
        ctx_qk.close()

        P.phase = "pv"
        ctx_v = ExitStack()
        wvstg = [mksb(ctx_v, f"wvstg{b}", [128, 512], F32, side="right") for b in range(2)]
        wvbf = mksb(ctx_v, "wvbf", [128, NCH * 512], BF16, side="right")
        te_qk_done = P.n(s_te)
        for vh in range(2):
            for fi in range(NCH):
                stg = wvstg[fi % 2]
                w_w = [(s_ve, P.n(s_ve))] if (vh * NCH + fi) >= 2 else \
                    [(s_te, te_qk_done)]
                P.dma("sync", stg[:],
                      d_wqkv[128 * fi:128 * fi + 128, 2048 + vh * 512:2048 + (vh + 1) * 512],
                      swv[fi % 2], waits=w_w)
                w_c = [(swv[fi % 2], P.n(swv[fi % 2]))]
                if vh == 1 and fi == 0:
                    w_c.append((s_te, P.n(s_te)))
                P.add("vector", (lambda e, stg=stg, fi=fi: e.tensor_copy(
                    wvbf[:, fi * 512:(fi + 1) * 512], stg[:])),
                      waits=w_c, incs=[(s_ve, 1)])
            ve_wv = P.n(s_ve)
            for t in range(NKT):
                reg = psA[:, 0:512] if ridx % 2 == 0 else psB[:, 0:512]
                w_mm = [(s_ve, ve_wv)]
                if ac_ev[ridx % 2] is not None:
                    w_mm.append((s_ac, ac_ev[ridx % 2]))
                P.add("tensor", (lambda e, reg=reg, vh=vh: e.matmul(
                    reg, row1[:], bvrow_bf[:, vh * 512:(vh + 1) * 512],
                    start=True, stop=False)), waits=w_mm)
                for fi in range(NCH):
                    P.add("tensor", (lambda e, fi=fi, t=t, reg=reg: e.matmul(
                        reg, xn[fi][:, t * 128:(t + 1) * 128], wvbf[:, fi * 512:(fi + 1) * 512],
                        start=False, stop=(fi == NCH - 1))),
                          incs=([(s_te, 1)] if fi == NCH - 1 else []))
                te_mm = P.n(s_te)
                P.add("scalar", (lambda e, t=t, vh=vh, reg=reg: e.activation(
                    out=v_tok[t][:, vh * 512:(vh + 1) * 512], in_=reg, func=ACT.Copy)),
                      waits=[(s_te, te_mm)], incs=[(s_ac, 1)])
                ac_ev[ridx % 2] = P.n(s_ac)
                ridx += 1
        ctx_v.close()

        ctx_g = ExitStack()
        P.phase = "pg"
        zrow_bf = mksb(ctx_g, "zrow_bf", [1, TOWN], BF16, side="right")
        ve_zg = None
        for g in range(4):
            w_z = [(s_ve, ve_zg)] if ve_zg is not None else [(s_ve, P.n(s_ve))]
            for fi in range(NCH):
                P.add("tensor", (lambda e, fi=fi, g=g: e.matmul(
                    psD[0:1, 0:512], wg_bf[:, fi:fi + 1],
                    xn[fi][:, 256 + g * 512:256 + (g + 1) * 512],
                    start=(fi == 0), stop=(fi == NCH - 1))),
                      waits=(w_z if fi == 0 else []),
                      incs=([(s_te, 1)] if fi == NCH - 1 else []))
            te_z = P.n(s_te)
            P.add("vector", (lambda e: e.tensor_copy(rowbuf[:, 0:512], psD[0:1, 0:512])),
                  waits=[(s_te, te_z)])
            P.add("vector", (lambda e, g=g: e.tensor_copy(
                zrow_bf[:, g * 512:(g + 1) * 512], rowbuf[:, 0:512])), incs=[(s_ve, 1)])
            ve_zg = P.n(s_ve)
        for t in range(NQT):
            P.add("tensor", (lambda e, t=t: e.matmul(
                psC[:, t:t + 1], zrow_bf[:, t * 128:(t + 1) * 128], one11[:],
                start=True, stop=(t == NQT - 1))),
                  waits=([(s_ve, ve_zg)] if t == 0 else []),
                  incs=([(s_te, 1)] if t == NQT - 1 else []))
        te_zt = P.n(s_te)
        P.add("scalar", lambda e: e.activation(out=gcol[:], in_=psC[:, 0:16], func=ACT.Sigmoid,
                                               bias=bg128[:], scale=1.0),
              waits=[(s_te, te_zt)], incs=[(s_ac, 1)])
        P.add("vector", lambda e: e.tensor_scalar(out=gm1col[:], in0=gcol[:], scalar1=-1.0,
                                                  scalar2=1.0, op0=ALU.mult, op1=ALU.add),
              waits=[(s_ac, P.n(s_ac))], incs=[(s_ve, 1)])
        ve_gate = P.n(s_ve)
        tap("q", [q_fm[c][:] for c in range(NCH)], [(s_ac, P.n(s_ac))])
        tap("k", [k_fm[c][:] for c in range(NCH)], [(s_ac, P.n(s_ac))])
        tap("v", [v_tok[t][:] for t in range(NKT)], [(s_ac, P.n(s_ac))])
        tap("gate", [gcol[:], gm1col[:]], [(s_ve, ve_gate)])

        P.phase = "p2a"
        e16s = mksb(ctx_g, "e16s", [128, NCH * 16], BF16, side="right")
        q2b = [mksb(ctx_g, f"q2b{b}", [128, 512], BF16, side="right") for b in range(2)]
        P.dma("sync", e16s[:], d_e16, s_ld, waits=[(s_te, P.n(s_te))])
        n_ld2 = P.n(s_ld)

        def norms(buf_fm, ncols, outcol0):
            njg = (ncols + 511) // 512
            for jg in range(njg):
                j0 = jg * 512
                n = min(512, ncols - j0)
                for c in range(NCH):
                    b = q2b[c % 2]
                    w_sq = [(s_te, P.n(s_te))] if (jg or outcol0 or c >= 2) \
                        else [(s_ac, P.n(s_ac)), (s_ld, n_ld2)]
                    P.add("vector", (lambda e, c=c, b=b, j0=j0, n=n, buf_fm=buf_fm: e.tensor_mul(
                        b[:, 0:n], buf_fm[c][:, j0:j0 + n], buf_fm[c][:, j0:j0 + n])),
                          waits=w_sq, incs=[(s_ve, 1)])
                    ve_sq = P.n(s_ve)
                    P.add("tensor", (lambda e, c=c, b=b, n=n: e.matmul(
                        psA[0:16, 0:n], e16s[:, c * 16:(c + 1) * 16], b[:, 0:n],
                        start=(c == 0), stop=(c == NCH - 1))),
                          waits=[(s_ve, ve_sq)], incs=[(s_te, 1)])
                te_n = P.n(s_te)
                P.add("vector", (lambda e, jg=jg, n=n, outcol0=outcol0: e.reduce_max(
                    bndsc[:, outcol0 + jg:outcol0 + jg + 1], psA[0:16, 0:n], axis=AX)),
                      waits=[(s_te, te_n)], incs=[(s_ve, 1)])
            return njg

        nq = norms(q_fm, TOWN, 0)
        nk = norms(k_fm, 2624, nq)
        P.add("vector", lambda e, nq=nq: e.reduce_max(bnd16[:, 0:1], bndsc[:, 0:nq], axis=AX))
        P.add("vector", lambda e, nq=nq, nk=nk: e.reduce_max(bnd16[:, 1:2],
                                                             bndsc[:, nq:nq + nk], axis=AX))
        P.add("vector", lambda e: e.tensor_mul(bnd16[:, 2:3], bnd16[:, 0:1], bnd16[:, 1:2]))
        P.add("vector", lambda e: e.tensor_copy(bnd16b[:], bnd16[:, 2:3]), incs=[(s_ve, 1)])
        P.add("tensor", lambda e: e.matmul(psD[0:1, 0:16], bnd16b[:], i16[:],
                                           start=True, stop=True),
              waits=[(s_ve, P.n(s_ve))], incs=[(s_te, 1)])
        P.add("vector", lambda e: e.tensor_copy(bndrow[:], psD[0:1, 0:16]),
              waits=[(s_te, P.n(s_te))])
        P.add("vector", lambda e: e.reduce_max(bnd16[0:1, 3:4], bndrow[:], axis=AX),
              incs=[(s_ve, 1)])
        P.add("scalar", lambda e: e.activation(out=bndrow[:, 1:2], in_=bnd16[0:1, 3:4],
                                               func=ACT.Sqrt, bias=np.float32(1e-6), scale=1.0),
              waits=[(s_ve, P.n(s_ve))], incs=[(s_ac, 1)])
        P.add("vector", lambda e: e.tensor_scalar(out=bndrow[:, 2:3], in0=bndrow[:, 1:2],
                                                  scalar1=-0.125, scalar2=-0.25,
                                                  op0=ALU.mult, op1=ALU.add),
              waits=[(s_ac, P.n(s_ac))])
        P.add("vector", lambda e: e.tensor_copy(nm8bf[:], bndrow[:, 2:3]), incs=[(s_ve, 1)])
        P.add("tensor", lambda e: e.matmul(psD[:, 2:3], row1[:], nm8bf[:], start=True, stop=True),
              waits=[(s_ve, P.n(s_ve))], incs=[(s_te, 1)])
        P.add("vector", lambda e: e.tensor_copy(negm8[:], psD[:, 2:3]),
              waits=[(s_te, P.n(s_te))], incs=[(s_ve, 1)])
        ve_negm8 = P.n(s_ve)
        tap("negm8", [negm8[:]], [(s_ve, ve_negm8)])
        tap("bndsc", [bndsc[:]], [(s_ve, ve_negm8)])
        tap("bndrow", [bndrow[:]], [(s_ve, ve_negm8)])

        ctx_g.close()
        ctx_p1.close()

        P.phase = "p2b"
        ctx_p2 = ExitStack()
        attnfm = mksb(ctx_p2, "attnfm", [128, NCH * TOWN], BF16, side="right")
        mks = [mksb(ctx_p2, f"mks{b}", [128, 640], BF16, side="right") for b in range(2)]
        pbuf = [mksb(ctx_p2, f"pbuf{b}", [128, 704], BF16, side="right") for b in range(2)]
        ptn = mksb(ctx_p2, "ptn", [128, 768], BF16, side="right")
        negi = mksb(ctx_p2, "negi", [128, 128], BF16, side="right")
        gdiag = mksb(ctx_p2, "gdiag", [128, 128], BF16, side="right")
        ggdiag = mksb(ctx_p2, "ggdiag", [128, 128], BF16, side="right")
        dlb = [mksb(ctx_p2, f"dlb{i}", [128, 128], BF16, side="right") for i in range(2)]
        dgb = [mksb(ctx_p2, f"dgb{i}", [128, 128], BF16, side="right") for i in range(2)]
        P.dma("sync", negi[:], d_negi, s_ld, waits=[(s_te, P.n(s_te))])
        n_ld3 = P.n(s_ld)

        te_T = [None, None]
        ve_av = [None, None]
        ac_exp_last = None
        te_mask_last = [None, None]
        ev_pt_prev = None

        attn_view = attnfm.ap().rearrange("p (c t) -> p c t", c=NCH)
        psd_view = psD.ap().rearrange("p (c t) -> p c t", c=NCH)

        for qt in range(NQT):
            mk = mks[qt % 2]
            if te_mask_last[qt % 2] is not None:
                w_mk = [(s_te, te_mask_last[qt % 2])]
            elif qt == 0:
                w_mk = [(s_te, P.n(s_te))]
            else:
                w_mk = []
            P.dma("sync", mk[:], d_masks[qt], smk2[qt % 2], waits=w_mk)
            mk_ld = (smk2[qt % 2], P.n(smk2[qt % 2]))

            w0 = [(s_ve, max(ve_negm8, ve_gate))] if qt == 0 else [(s_ac, ac_exp_last)]
            P.add("vector", lambda e: e.memset(ltile[:], 0.0), waits=w0)
            P.add("vector", lambda e: e.memset(lgtile[:], 0.0))
            P.add("vector", (lambda e, qt=qt: e.tensor_scalar(
                out=gdiag[:], in0=ident[:], scalar1=gm1col[:, qt:qt + 1],
                scalar2=None, op0=ALU.mult)))
            P.add("vector", (lambda e, qt=qt: e.tensor_scalar(
                out=ggdiag[:], in0=ident[:], scalar1=gcol[:, qt:qt + 1],
                scalar2=None, op0=ALU.mult)), incs=[(s_ve, 1)])
            ve_qt = P.n(s_ve)

            for pr in range(8):
                c = pr
                qq, kk = q_fm[c], k_fm[c]
                j0 = qt * 128
                P.phase = "p2b_s"
                w_s = [(s_ac, ac_exp_last)] if ac_exp_last is not None else [(s_ld, n_ld3)]
                P.add("tensor", (lambda e, qq=qq, kk=kk, j0=j0: e.matmul(
                    psA[:, 0:512], qq[0:64, j0:j0 + 128], kk[0:64, j0:j0 + 512],
                    start=True, stop=False, tile_position=(0, 0))), waits=w_s)
                P.add("tensor", (lambda e, qq=qq, kk=kk, j0=j0: e.matmul(
                    psB[:, 0:512], qq[64:128, j0:j0 + 128], kk[64:128, j0:j0 + 512],
                    start=True, stop=False, tile_position=(64, 0))))
                P.add("tensor", (lambda e, qq=qq, kk=kk, j0=j0: e.matmul(
                    psA[:, 640:704], qq[0:64, j0:j0 + 128], kk[0:64, 2560:2624],
                    start=True, stop=False, tile_position=(0, 0))))
                P.add("tensor", (lambda e, qq=qq, kk=kk, j0=j0: e.matmul(
                    psB[:, 640:704], qq[64:128, j0:j0 + 128], kk[64:128, 2560:2624],
                    start=True, stop=False, tile_position=(64, 0))))
                P.add("tensor", (lambda e, qq=qq, kk=kk, j0=j0: e.matmul(
                    psA[:, 512:640], qq[0:64, j0:j0 + 128], kk[0:64, j0 + 512:j0 + 640],
                    start=True, stop=False, tile_position=(0, 0))))
                P.add("tensor", (lambda e, qq=qq, kk=kk, j0=j0: e.matmul(
                    psB[:, 512:640], qq[64:128, j0:j0 + 128], kk[64:128, j0 + 512:j0 + 640],
                    start=True, stop=False, tile_position=(64, 0))))
                P.add("tensor", (lambda e, mk=mk: e.matmul(
                    psA[:, 0:512], negi[:], mk[:, 0:512], start=False, stop=False)),
                      waits=[mk_ld])
                P.add("tensor", (lambda e, mk=mk: e.matmul(
                    psA[:, 512:640], negi[:], mk[:, 512:640], start=False, stop=True)),
                      incs=[(s_te, 1)])
                te_s = [P.n(s_te), None]
                P.add("tensor", (lambda e, mk=mk: e.matmul(
                    psB[:, 0:512], negi[:], mk[:, 0:512], start=False, stop=False)))
                P.add("tensor", (lambda e, mk=mk: e.matmul(
                    psB[:, 512:640], negi[:], mk[:, 512:640], start=False, stop=True)),
                      incs=[(s_te, 1)])
                te_s[1] = P.n(s_te)
                te_mask_last[qt % 2] = te_s[1]

                for hh in range(2):
                    h = 2 * pr + hh
                    ps = psA if hh == 0 else psB
                    pb = pbuf[hh]
                    P.phase = "p2b_e"
                    w_e = [(s_te, te_s[hh]), (s_ve, ve_qt)]
                    if te_T[hh] is not None:
                        w_e.append((s_te, te_T[hh]))
                    P.add("scalar", (lambda e, ps=ps, pb=pb, h=h: e.activation(
                        out=pb[:], in_=ps[:, 0:704], func=ACT.Exp, bias=negm8[:],
                        scale=0.125)),
                          waits=w_e, incs=[(s_ac, 1)])
                    ac_exp = P.n(s_ac)
                    ac_exp_last = ac_exp
                    P.phase = "p2b_d"
                    P.add("vector", (lambda e, pb=pb, h=h: e.tensor_scalar(
                        out=pb[:, 0:704], in0=pb[:, 0:704], scalar1=1.0, scalar2=0.0,
                        op0=ALU.mult, op1=ALU.add, accum_out=ltile[:, h:h + 1])),
                          waits=[(s_ac, ac_exp)])
                    P.add("vector", (lambda e, pb=pb, h=h: e.reduce_sum(
                        lgtile[:, h:h + 1], pb[:, 640:704], axis=AX)))
                    P.add("vector", (lambda e, h=h: e.tensor_sub(
                        ltile[:, h:h + 1], ltile[:, h:h + 1], lgtile[:, h:h + 1])))
                    P.add("vector", (lambda e, h=h: e.reciprocal(
                        rltile[:, h:h + 1], ltile[:, h:h + 1])))
                    P.add("vector", (lambda e, h=h: e.reciprocal(
                        rlgtile[:, h:h + 1], lgtile[:, h:h + 1])))
                    w_d = [(s_te, te_T[hh])] if te_T[hh] is not None else []
                    P.add("vector", (lambda e, h=h, hh=hh: e.tensor_scalar(
                        out=dlb[hh][:], in0=gdiag[:], scalar1=rltile[:, h:h + 1],
                        scalar2=None, op0=ALU.mult)), waits=w_d)
                    P.add("vector", (lambda e, h=h, hh=hh: e.tensor_scalar(
                        out=dgb[hh][:], in0=ggdiag[:], scalar1=rlgtile[:, h:h + 1],
                        scalar2=None, op0=ALU.mult)), incs=[(s_ve, 1)])
                    ve_d = P.n(s_ve)
                    P.phase = "p2b_t"
                    w_t = [(s_ve, ve_d)]
                    if ev_pt_prev is not None:
                        w_t.append(ev_pt_prev)
                    for s in range(5):
                        P.add("tensor", (lambda e, pb=pb, hh=hh, s=s: e.matmul(
                            psC[:, 128 * s:128 * (s + 1)], pb[:, 128 * s:128 * (s + 1)],
                            dlb[hh][:], start=True, stop=(s == 3))),
                              waits=(w_t if s == 0 else []))
                    P.add("tensor", (lambda e, pb=pb, hh=hh: e.matmul(
                        psC[0:64, 640:768], pb[:, 640:704], dgb[hh][:],
                        start=True, stop=True)), incs=[(s_te, 1)])
                    te_T[hh] = P.n(s_te)
                    P.phase = "p2b_v"
                    if qt == 0 and pr == 0 and hh == 0:
                        tap("pb0", [pb[:]], [(s_ac, ac_exp)])
                        tap("lt0", [ltile[:], lgtile[:], rltile[:], rlgtile[:]],
                            [(s_ve, ve_d)])
                        tap("dl0", [dlb[0][:], dgb[0][:]], [(s_ve, ve_d)])
                    if hh == 0:
                        P.add("scalar", (lambda e: e.activation(out=ptn[:], in_=psC[:, 0:768],
                                                                func=ACT.Copy)),
                              waits=[(s_te, te_T[hh])], incs=[(s_ac, 1)])
                        ev_pt = (s_ac, P.n(s_ac))
                        ev_pt_prev = ev_pt
                    else:
                        P.add("vector", (lambda e: e.tensor_copy(ptn[:], psC[:, 0:768])),
                              waits=[(s_te, te_T[hh])], incs=[(s_ve, 1)])
                        ev_pt = (s_ve, P.n(s_ve))
                    ev_pt_prev = ev_pt
                    if qt == 0 and pr == 0 and hh == 0:
                        tap("ptn0", [ptn[:]], [ev_pt])
                    P.phase = "p2b_pv"
                    bank = pr // 4
                    col0 = 512 * bank + 128 * (pr % 4)
                    w_pv = [ev_pt]
                    if pr % 4 == 0 and hh == 0 and ve_av[bank] is not None:
                        w_pv.append((s_ve, ve_av[bank]))
                    for s in range(5):
                        P.add("tensor", (lambda e, s=s, qt=qt, pr=pr, hh=hh, col0=col0: e.matmul(
                            psD[64 * hh:64 * hh + 64, col0:col0 + 128],
                            v_tok[qt + s][:, 128 * pr + 64 * hh:128 * pr + 64 * hh + 64],
                            ptn[:, 128 * s:128 * (s + 1)],
                            start=(s == 0), stop=False,
                            tile_position=(0, 64 * hh))),
                              waits=(w_pv if s == 0 else []))
                    is_last = (pr % 4 == 3 and hh == 1)
                    P.add("tensor", (lambda e, pr=pr, hh=hh, col0=col0: e.matmul(
                        psD[64 * hh:64 * hh + 64, col0:col0 + 128],
                        v_tok[20][0:64, 128 * pr + 64 * hh:128 * pr + 64 * hh + 64],
                        ptn[0:64, 640:768], start=False, stop=True,
                        tile_position=(0, 64 * hh))),
                          incs=([(s_te, 1)] if is_last else []))
                    if is_last:
                        te_pv = P.n(s_te)
                        P.phase = "p2b_av"
                        P.add("vector", (lambda e, qt=qt, bank=bank: e.tensor_copy(
                            attn_view[:, 4 * bank:4 * bank + 4, qt * 128:(qt + 1) * 128],
                            psd_view[:, 4 * bank:4 * bank + 4, 0:128])),
                              waits=[(s_te, te_pv)], incs=[(s_ve, 1)])
                        ve_av[bank] = P.n(s_ve)
        ve_attn = max(v for v in ve_av if v is not None)
        tap("attnfm", [attnfm[:]], [(s_ve, ve_attn)])

        P.phase = "p3"
        ctx_qkv.close()
        x_res = [sb(f"x{c}", [128, TOWN], F32) for c in range(NCH)]

        ctx_p3 = ExitStack()
        woutbf = [mksb(ctx_p3, f"wo{c}", [128, D], BF16, side="right") for c in range(NCH)]
        wostg = [mksb(ctx_p3, f"wostg{b}", [128, D], F32, side="right") for b in range(2)]
        ysb = [mksb(ctx_p3, f"ysb{c}", [128, GN], F32, side="right") for c in range(NCH)]

        for c in range(NCH):
            P.dma("sync", x_res[c][:], d_x[128 * c:128 * c + 128, 256:256 + TOWN], s_x,
                  waits=([(s_ve, ve_attn)] if c == 0 else []))
        x_ld = P.n(s_x)
        for c in range(NCH):
            stg = wostg[c % 2]
            w_w = [(s_ve, P.n(s_ve))] if c >= 2 else []
            P.dma("sync", stg[:], d_wout[128 * c:128 * c + 128, :], swo[c % 2], waits=w_w)
            P.add("vector", (lambda e, c=c, stg=stg: e.tensor_copy(woutbf[c][:], stg[:])),
                  waits=[(swo[c % 2], P.n(swo[c % 2]))], incs=[(s_ve, 1)])
        ve_wo = P.n(s_ve)

        for g in range(G):
            ac_prev = [None, None]
            for fo in range(NCH):
                reg = psA[:, 0:GN] if fo % 2 == 0 else psB[:, 0:GN]
                w_mm = [(s_ve, max(ve_wo, ve_attn))]
                if ac_prev[fo % 2] is not None:
                    w_mm.append((s_ac, ac_prev[fo % 2]))
                elif g > 0:
                    w_mm.append((s_ve, P.n(s_ve)))
                for fi in range(NCH):
                    P.add("tensor", (lambda e, fi=fi, fo=fo, g=g, reg=reg: e.matmul(
                        reg, woutbf[fi][:, 128 * fo:128 * (fo + 1)],
                        attn_view[:, fi, g * GN:(g + 1) * GN],
                        start=(fi == 0), stop=(fi == NCH - 1))),
                          waits=(w_mm if fi == 0 else []),
                          incs=([(s_te, 1)] if fi == NCH - 1 else []))
                te_mm = P.n(s_te)
                P.add("scalar", (lambda e, fo=fo, reg=reg: e.activation(
                    out=ysb[fo][:], in_=reg, func=ACT.Identity, bias=bout[:, fo:fo + 1],
                    scale=1.0)), waits=[(s_te, te_mm)], incs=[(s_ac, 1)])
                ac_prev[fo % 2] = P.n(s_ac)

            def put_res(cc, reg, te_val, g=g):
                P.add("vector", (lambda e, cc=cc, reg=reg: e.tensor_mul(
                    lnt[:, 0:GN], reg, rb_sb[:, 0:GN])),
                      waits=[(s_te, te_val)])
                P.add("vector", (lambda e, cc=cc, g=g: e.tensor_add(
                    x_res[cc][:, g * GN:(g + 1) * GN], x_res[cc][:, g * GN:(g + 1) * GN],
                    lnt[:, 0:GN])), incs=[(s_ve, 1)])

            w_ln = [(s_ac, P.n(s_ac))]
            if g == 0:
                w_ln.append((s_x, x_ld))
            emit_ln(GN, (lambda c: ysb[c][:]), put_res, w_first=w_ln)
        ve_p3 = P.n(s_ve)
        tap("x1", [x_res[c][:] for c in range(NCH)], [(s_ve, ve_p3)])
        ctx_p3.close()
        ctx_p2.close()

        P.phase = "p4"
        ctx_p4 = ExitStack()
        wtstg = [mksb(ctx_p4, f"wtstg{b}", [128, 1024], F32, side="right") for b in range(2)]
        wtbf = mksb(ctx_p4, "wtbf", [128, 1024], BF16, side="right")
        c1b = mksb(ctx_p4, "c1b", [128, 1024], BF16, side="right")
        weff = [mksb(ctx_p4, f"weff{b}", [128, 1024], BF16, side="right") for b in range(2)]
        tspill = [mksb(ctx_p4, f"tspill{b}", [128, 1024], BF16, side="right") for b in range(2)]
        scl = mksb(ctx_p4, "scl", [128, 8], F32, side="right")

        tern_state = {"ve_red": {}, "te_tr": {}, "ev_ps": {}, "st_tb": {}}

        def tern_weight(d_w, d_dst, nt, wcols, ve_start):
            np_ = wcols // 1024
            ve_red = tern_state["ve_red"]
            te_tr = tern_state["te_tr"]
            ev_ps = tern_state["ev_ps"]
            st_tb = tern_state["st_tb"]
            for t in range(nt):
                for dd in range(np_):
                    par = dd % 2
                    if par in ve_red:
                        w_w = [(s_ve, ve_red[par])]
                    elif t == 0:
                        w_w = [(s_ve, ve_start)]
                    else:
                        w_w = []
                    P.dma("sync", wtstg[par][:],
                          d_w[128 * t:128 * (t + 1), 1024 * dd:1024 * (dd + 1)], swt[par],
                          waits=w_w)
                    P.add("vector", (lambda e, par=par, dd=dd: e.reduce_sum(
                        scl[:, dd:dd + 1], wtstg[par][:], axis=AX,
                        apply_absolute_value=True)),
                          waits=[(swt[par], P.n(swt[par]))], incs=[(s_ve, 1)])
                    ve_red[par] = P.n(s_ve)
                if np_ > 1:
                    P.add("vector", (lambda e, np_=np_: e.reduce_sum(
                        scl[:, 4:5], scl[:, 0:np_], axis=AX)))
                else:
                    P.add("vector", (lambda e: e.tensor_copy(scl[:, 4:5], scl[:, 0:1])))
                P.add("vector", (lambda e, wcols=wcols: e.tensor_scalar(
                    out=scl[:, 5:6], in0=scl[:, 4:5], scalar1=1.0 / wcols,
                    scalar2=1e-5, op0=ALU.mult, op1=ALU.max)))
                P.add("vector", (lambda e: e.tensor_scalar(
                    out=scl[:, 6:7], in0=scl[:, 5:6], scalar1=0.5, scalar2=None,
                    op0=ALU.mult)))
                P.add("vector", (lambda e: e.tensor_scalar(
                    out=scl[:, 7:8], in0=scl[:, 5:6], scalar1=-0.5, scalar2=None,
                    op0=ALU.mult)), incs=[(s_ve, 1)])
                ve_scl = P.n(s_ve)
                for dd in range(np_):
                    par = dd % 2
                    P.dma("sync", wtstg[par][:],
                          d_w[128 * t:128 * (t + 1), 1024 * dd:1024 * (dd + 1)], swt[par],
                          waits=[(s_ve, ve_scl if dd < 2 else ve_red[par])])
                    P.add("vector", (lambda e, par=par: e.tensor_scalar(
                        out=c1b[:], in0=wtstg[par][:], scalar1=scl[:, 6:7], scalar2=scl[:, 5:6],
                        op0=ALU.is_gt, op1=ALU.mult)),
                          waits=[(swt[par], P.n(swt[par]))])
                    we = weff[par]
                    w_we = [(s_te, te_tr[par])] if par in te_tr else []
                    P.add("vector", (lambda e, we=we, par=par: e.tensor_scalar(
                        out=we[:], in0=wtstg[par][:], scalar1=scl[:, 7:8], scalar2=scl[:, 5:6],
                        op0=ALU.is_lt, op1=ALU.mult)), waits=w_we, incs=[(s_ve, 1)])
                    ve_red[par] = P.n(s_ve)
                    P.add("vector", (lambda e, we=we: e.tensor_sub(we[:], c1b[:], we[:])),
                          incs=[(s_ve, 1)])
                    ve_we = P.n(s_ve)
                    psT = psC if par == 0 else psD
                    w_tr = [(s_ve, ve_we)]
                    if id(psT) in ev_ps:
                        w_tr.append(ev_ps[id(psT)])
                    for j in range(8):
                        P.add("tensor", (lambda e, j=j, we=we, psT=psT: e.matmul(
                            psT[:, 128 * j:128 * (j + 1)], we[:, 128 * j:128 * (j + 1)],
                            ident[:], start=True, stop=(j in (3, 7)))),
                              waits=(w_tr if j == 0 else []),
                              incs=([(s_te, 1)] if j == 7 else []))
                    te_tr[par] = P.n(s_te)
                    tb = tspill[par]
                    w_e = [(s_te, te_tr[par])]
                    if par in st_tb:
                        w_e.append(st_tb[par])
                    P.add("scalar", (lambda e, tb=tb, psT=psT: e.activation(
                        out=tb[:], in_=psT[:, 0:1024], func=ACT.Copy)),
                          waits=w_e, incs=[(s_ac, 1)])
                    ev_ps[id(psT)] = (s_ac, P.n(s_ac))
                    P.add("gpsimd", (lambda e, t=t, dd=dd, tb=tb, d_dst=d_dst: e.dma_start(
                        out=d_dst.rearrange("j p n -> p j n")[:, 8 * dd:8 * dd + 8,
                                                              128 * t:128 * (t + 1)],
                        in_=tb.ap().rearrange("p (j n) -> p j n", j=8))),
                          waits=[ev_ps[id(psT)]], incs=[(ssp[par], 16)], dma=True)
                    st_tb[par] = (ssp[par], P.n(ssp[par]))

        tern_weight(d_wfc, d_tfcs, NHI, D, ve_p3)
        tern_weight(d_wproj, d_tproj, NCH, HIDN, P.n(s_ve))
        w_spill = [(ssp[0], P.n(ssp[0])), (ssp[1], P.n(ssp[1]))]
        ctx_p4.close()

        P.phase = "p5"
        ctx_p5 = ExitStack()
        xn2 = [mksb(ctx_p5, f"xn2_{c}", [128, GN], BF16, side="right") for c in range(NCH)]
        r2 = mksb(ctx_p5, "r2", [128, NHI * GN], BF16, side="right")
        rbuf = [mksb(ctx_p5, f"rbuf{b}", [128, GN], BF16, side="right") for b in range(2)]
        t1bf = mksb(ctx_p5, "t1bf", [16, GN], BF16, side="right")
        t2bf = mksb(ctx_p5, "t2bf", [16, GN], BF16, side="right")
        fstg = [mksb(ctx_p5, f"fstg{b}", [128, 1024], BF16, side="right") for b in range(2)]
        pstg = [mksb(ctx_p5, f"pstg{b}", [128, HIDN], BF16, side="right") for b in range(2)]
        ysb5 = [mksb(ctx_p5, f"y5_{c}", [128, GN], F32, side="right") for c in range(NCH)]
        ustg = mksb(ctx_p5, "ustg", [16, 1024], F32, side="right")
        uT_bf = mksb(ctx_p5, "uT_bf", [16, HIDN], BF16, side="right")

        ve_u = None
        for dd in range(4):
            w_u = [(s_ve, ve_u)] if ve_u is not None else list(w_spill)
            P.dma("sync", ustg[:], d_uT[:, 1024 * dd:1024 * (dd + 1)], s_u, waits=w_u)
            P.add("vector", (lambda e, dd=dd: e.tensor_copy(
                uT_bf[:, 1024 * dd:1024 * (dd + 1)], ustg[:])),
                  waits=[(s_u, P.n(s_u))], incs=[(s_ve, 1)])
            ve_u = P.n(s_ve)

        tfcs_view = d_tfcs.rearrange("j p n -> p j n")
        tproj_view = d_tproj.rearrange("j p n -> p j n")
        te_fcmm = {0: None, 1: None}
        te_projmm = {0: None, 1: None}

        for g in range(G):
            def put_xn2(c, reg, te_val):
                P.add("vector", (lambda e, c=c, reg=reg: e.tensor_mul(
                    xn2[c][:], reg, rb_sb[:, 0:GN])),
                      waits=[(s_te, te_val)], incs=[(s_ve, 1)])

            w_ln3 = [(s_ac, P.n(s_ac))] + list(w_spill)
            emit_ln(GN, (lambda c, g=g: x_res[c][:, g * GN:(g + 1) * GN]), put_xn2,
                    w_first=w_ln3)
            ve_xn2 = P.n(s_ve)
            for c in range(NCH):
                P.add("tensor", (lambda e, c=c: e.matmul(
                    psD[0:16, 0:GN], vT_bf[:, 16 * c:16 * (c + 1)], xn2[c][:],
                    start=(c == 0), stop=(c == NCH - 1))),
                      waits=([(s_ve, ve_xn2)] if c == 0 else []),
                      incs=([(s_te, 1)] if c == NCH - 1 else []))
            P.add("vector", lambda e: e.tensor_copy(t1bf[:], psD[0:16, 0:GN]),
                  waits=[(s_te, P.n(s_te))], incs=[(s_ve, 1)])
            P.add("tensor", lambda e: e.matmul(psD[0:16, 512:512 + GN], fbTg[:], t1bf[:],
                                               start=True, stop=True),
                  waits=[(s_ve, P.n(s_ve))], incs=[(s_te, 1)])
            P.add("vector", lambda e: e.tensor_copy(t2bf[:], psD[0:16, 512:512 + GN]),
                  waits=[(s_te, P.n(s_te))], incs=[(s_ve, 1)])
            ve_t2 = P.n(s_ve)
            ac_r = [None, None]
            ve_r2l = [None, None]
            for hi in range(NHI):
                fpar = hi % 2
                fst = fstg[fpar]
                if te_fcmm[fpar] is not None:
                    w_f = [(s_te, te_fcmm[fpar])]
                elif g == 0 and hi < 2:
                    w_f = list(w_spill)
                else:
                    w_f = []
                P.dma("sync", fst.ap().rearrange("p (j n) -> p j n", j=8),
                      tfcs_view[:, :, 128 * hi:128 * (hi + 1)], sfc2[fpar], waits=w_f)
                w_fl = (sfc2[fpar], P.n(sfc2[fpar]))
                reg = psA[:, 0:GN] if hi % 2 == 0 else psB[:, 0:GN]
                w_mm = [(s_ve, ve_t2), w_fl]
                if ac_r[hi % 2] is not None:
                    w_mm.append((s_ac, ac_r[hi % 2]))
                for fi in range(NCH):
                    P.add("tensor", (lambda e, fi=fi, fst=fst, reg=reg: e.matmul(
                        reg, fst[:, 128 * fi:128 * (fi + 1)], xn2[fi][:],
                        start=(fi == 0), stop=False)),
                          waits=(w_mm if fi == 0 else []))
                P.add("tensor", (lambda e, hi=hi, reg=reg: e.matmul(
                    reg, uT_bf[:, 128 * hi:128 * (hi + 1)], t2bf[:],
                    start=False, stop=True)), incs=[(s_te, 1)])
                te_h = P.n(s_te)
                te_fcmm[fpar] = te_h
                rb = rbuf[hi % 2]
                w_relu = [(s_te, te_h)]
                if ve_r2l[hi % 2] is not None:
                    w_relu.append((s_ve, ve_r2l[hi % 2]))
                P.add("scalar", (lambda e, rb=rb, reg=reg: e.activation(
                    out=rb[:], in_=reg, func=ACT.Relu)),
                      waits=w_relu, incs=[(s_ac, 1)])
                ac_r[hi % 2] = P.n(s_ac)
                P.add("vector", (lambda e, hi=hi, rb=rb: e.tensor_mul(
                    r2[:, hi * GN:(hi + 1) * GN], rb[:], rb[:])),
                      waits=[(s_ac, ac_r[hi % 2])], incs=[(s_ve, 1)])
                ve_r2l[hi % 2] = P.n(s_ve)
            ve_r2 = P.n(s_ve)
            ac_y = [None, None]
            for fo in range(NCH):
                pstgb = pstg[fo % 2]
                if te_projmm[fo % 2] is not None:
                    w_p = [(s_te, te_projmm[fo % 2])]
                elif g == 0:
                    w_p = list(w_spill)
                else:
                    w_p = []
                for dd in range(4):
                    P.dma("gpsimd", pstgb.ap().rearrange("p (j n) -> p j n", j=NHI)[
                        :, 8 * dd:8 * dd + 8, :],
                        tproj_view[:, 8 * dd:8 * dd + 8, 128 * fo:128 * (fo + 1)],
                        spr[fo % 2], waits=(w_p if dd == 0 else []))
                w_pl = (spr[fo % 2], P.n(spr[fo % 2]))
                reg = psC[:, 0:GN] if fo % 2 == 0 else psC[:, 512:512 + GN]
                w_y = [w_pl, (s_ve, ve_r2)]
                if ac_y[fo % 2] is not None:
                    w_y.append((s_ac, ac_y[fo % 2]))
                for hi in range(NHI):
                    P.add("tensor", (lambda e, hi=hi, reg=reg, pstgb=pstgb: e.matmul(
                        reg, pstgb[:, 128 * hi:128 * (hi + 1)], r2[:, hi * GN:(hi + 1) * GN],
                        start=(hi == 0), stop=(hi == NHI - 1))),
                          waits=(w_y if hi == 0 else []),
                          incs=([(s_te, 1)] if hi == NHI - 1 else []))
                te_projmm[fo % 2] = P.n(s_te)
                te_y = P.n(s_te)
                P.add("scalar", (lambda e, fo=fo, reg=reg: e.activation(
                    out=ysb5[fo][:], in_=reg, func=ACT.Copy)),
                      waits=[(s_te, te_y)], incs=[(s_ac, 1)])
                ac_y[fo % 2] = P.n(s_ac)

            def put_res5(cc, reg, te_val, g=g):
                P.add("vector", (lambda e, cc=cc, reg=reg: e.tensor_mul(
                    lnt[:, 0:GN], reg, rb_sb[:, 0:GN])),
                      waits=[(s_te, te_val)])
                P.add("vector", (lambda e, cc=cc, g=g: e.tensor_add(
                    x_res[cc][:, g * GN:(g + 1) * GN], x_res[cc][:, g * GN:(g + 1) * GN],
                    lnt[:, 0:GN])), incs=[(s_ve, 1)])

            emit_ln(GN, (lambda c: ysb5[c][:]), put_res5, w_first=[(s_ac, P.n(s_ac))])
        ve_p5 = P.n(s_ve)
        tap("r2", [r2[:]], [(s_ve, ve_p5)])
        ctx_p5.close()

        P.phase = "p6"
        for c in range(NCH):
            P.dma("gpsimd", d_out[128 * c:128 * c + 128, :], x_res[c][:], s_st,
                  waits=([(s_ve, ve_p5)] if c == 0 else []))
        final_sem, final = s_st, P.n(s_st)

        block = ctx.enter_context(nc.Block())
        for eng_name, deco in [("sync", block.sync), ("tensor", block.tensor),
                               ("scalar", block.scalar), ("vector", block.vector),
                               ("gpsimd", block.gpsimd)]:
            def make(eng_name=eng_name):
                def _(e):
                    P.run(e, eng_name)
                    if eng_name == "gpsimd":
                        e.wait_ge(final_sem, final)
                return _
            deco(make())
    return nc


_NC_CACHE = {}
DEBUG = ()


def kernel(L0_attn_hidden_states, L0_attn_Wqkv, L0_attn_bqkv, L0_attn_Wout, L0_attn_bout,
           L0_attn_Wg, L0_attn_bg, L0_ffn_fast_b, L0_ffn_w_fc, L0_ffn_w_proj, L0_ffn_u,
           L0_ffn_v, L0_ffn_gate):
    f = np.float32
    x = np.ascontiguousarray(np.asarray(L0_attn_hidden_states, dtype=f))
    wqkv = np.ascontiguousarray(np.asarray(L0_attn_Wqkv, dtype=f))
    bqkv = np.asarray(L0_attn_bqkv, dtype=f).reshape(-1)
    wout = np.ascontiguousarray(np.asarray(L0_attn_Wout, dtype=f))
    bout = np.asarray(L0_attn_bout, dtype=f).reshape(-1)
    wgv = np.ascontiguousarray(np.asarray(L0_attn_Wg, dtype=f))
    bg = np.asarray(L0_attn_bg, dtype=f).reshape(1, 1)
    fb = np.asarray(L0_ffn_fast_b, dtype=f)
    wfc = np.ascontiguousarray(np.asarray(L0_ffn_w_fc, dtype=f))
    wproj = np.ascontiguousarray(np.asarray(L0_ffn_w_proj, dtype=f))
    uu = np.asarray(L0_ffn_u, dtype=f)
    vv = np.asarray(L0_ffn_v, dtype=f)
    gate = np.asarray(L0_ffn_gate, dtype=f).reshape(1, 1)

    e16 = np.zeros((128, NCH * 16), dtype=BF)
    for c in range(NCH):
        e16[0:64, c * 16 + 2 * c] = 1
        e16[64:128, c * 16 + 2 * c + 1] = 1

    shared = {
        "wqkv": wqkv,
        "bqkv_pp": np.ascontiguousarray(bqkv.reshape(24, 128).T),
        "bv_row": np.ascontiguousarray(bqkv[2048:3072].reshape(1, D)),
        "wout": wout,
        "bout_pp": np.ascontiguousarray(bout.reshape(8, 128).T),
        "wg": wgv,
        "bg": bg,
        "w_fc": wfc,
        "w_proj": wproj,
        "uT": np.ascontiguousarray(uu.T),
        "vT": np.ascontiguousarray(vv.T),
        "fbT": np.ascontiguousarray(fb.T),
        "gate": gate,
        "ident": np.eye(128, dtype=BF),
        "negi": (np.eye(128) * -1.0e9).astype(BF),
        "i16": np.eye(16, dtype=BF),
        "row1": np.ones((1, 128), dtype=BF),
        "statc": np.full((128, 1), -1.0 / 1024.0, dtype=BF),
        "one11": np.ones((1, 1), dtype=BF),
        "e16": e16,
    }

    in_maps = []
    for core in range(8):
        seq, hh = core // 2, core % 2
        start = hh * TOWN
        rows = np.zeros((EXT, D), dtype=f)
        lo = max(0, start - 256)
        rows[256 - (start - lo):256] = x[seq, lo:start]
        rows[256:256 + TOWN] = x[seq, start:start + TOWN]
        hi = min(T, start + TOWN + 256)
        rows[2304:2304 + (hi - start - TOWN)] = x[seq, start + TOWN:hi]
        rows[2560:2624] = x[seq, 0:NG]
        xe = np.ascontiguousarray(rows.T)

        t01 = np.zeros((NQT, 128, 640), dtype=BF)
        a = np.arange(128)[:, None]
        b = np.arange(640)[None, :]
        for qt in range(NQT):
            band = (b >= a) & (b <= a + 512)
            e_idx = 128 * qt + b
            gk = 2048 * hh + (e_idx - 256)
            valid = (gk >= 0) & (gk < T)
            t01[qt] = (~(band & valid)).astype(BF)

        m = dict(shared)
        m["x_ext"] = xe
        m["masks"] = np.ascontiguousarray(t01)
        in_maps.append(m)

    key = ("nc",) + tuple(sorted(DEBUG))
    if key not in _NC_CACHE:
        _NC_CACHE[key] = build_program(debug=DEBUG)
    nc = _NC_CACHE[key]
    res = run_bass_kernel_spmd(nc, in_maps, core_ids=list(range(8)))
    kernel.last_results = res.results

    out = np.empty((B, T, D), dtype=f)
    for core in range(8):
        seq, hh = core // 2, core % 2
        out[seq, hh * TOWN:(hh + 1) * TOWN] = res.results[core]["out"].T
    return out


# version 05
# speedup vs baseline: 1.6076x; 1.3319x over previous
import numpy as np
import ml_dtypes
from contextlib import ExitStack

import concourse.bass as bass
import concourse.mybir as mybir
from concourse.bass_utils import run_bass_kernel_spmd

F32 = mybir.dt.float32
BF16 = mybir.dt.bfloat16
BF = ml_dtypes.bfloat16
AX = mybir.AxisListType.X
ALU = mybir.AluOpType
ACT = mybir.ActivationFunctionType

B, T, D, H, HD = 4, 4096, 1024, 16, 64
WLOC, NG = 256, 64
TOWN = 2048
EXT = 2688
NQT = TOWN // 128
NKT = EXT // 128
NCH = 8
HIDN = 4096
NHI = HIDN // 128
RANK = 16

G1N, G1 = 448, 6
GN, G = 512, 4


SAFE_MODE = 1
SAFE_PHASES = None
UNSAFE_PHASES = ("p1a", "p1b", "pv", "pg", "p5")
DMA_FINE = True
DMA_FINE_PHASES = None


class Plan:
    def __init__(self):
        self.ops = []
        self.count = {}
        self.safe = SAFE_MODE
        self.phase = "p0"
        self.eng_sem = {}
        self.sems = []
        self.joined = {}

    def n(self, sem):
        return self.count.get(id(sem), 0)

    def add(self, eng, fn, waits=(), incs=(), dma=False):
        waits = list(waits)
        incs = list(incs)
        for s, v in waits:
            assert v <= self.n(s), f"wait for future sem value {v} > {self.n(s)}"
        base = self.phase.split("_")[0]
        insafe = (SAFE_PHASES is None or self.phase in SAFE_PHASES or
                  base in SAFE_PHASES) and \
            self.phase not in UNSAFE_PHASES
        if dma and DMA_FINE and (DMA_FINE_PHASES is None or base in DMA_FINE_PHASES
                                 or self.phase in DMA_FINE_PHASES):
            insafe = False
        if self.safe and insafe:
            if not incs and eng in self.eng_sem:
                incs = [(self.eng_sem[eng], 1)]
            j = self.joined.setdefault(eng, {})
            for s in self.sems:
                nv = self.n(s)
                if nv > j.get(id(s), 0):
                    waits.append((s, nv))
                    j[id(s)] = nv
        for s, k in incs:
            self.count[id(s)] = self.n(s) + k
        self.ops.append((eng, fn, tuple(waits), tuple(incs)))

    def dma(self, eng, out, in_, semh, waits=(), slow=False):
        self.add(eng, (lambda e, out=out, in_=in_, slow=slow: e.dma_start(
            out=out, in_=in_, allow_slow_non_contiguous=slow)),
                 waits=waits, incs=[(semh, 16)], dma=True)

    def run(self, e, eng):
        for oeng, fn, waits, incs in self.ops:
            if oeng != eng:
                continue
            for s, v in waits:
                e.wait_ge(s, v)
            inst = fn(e)
            for s, k in incs:
                inst = inst.then_inc(s, k)


def build_program(debug=()):
    debug = set(debug)
    nc = bass.Bass("TRN2", target_bir_lowering=False)

    d_x = nc.dram_tensor("x_ext", [D, EXT], F32, kind="ExternalInput").ap()
    d_wqkv = nc.dram_tensor("wqkv", [D, 3 * D], F32, kind="ExternalInput").ap()
    d_bqkv = nc.dram_tensor("bqkv_pp", [128, 24], F32, kind="ExternalInput").ap()
    d_bvrow = nc.dram_tensor("bv_row", [1, D], F32, kind="ExternalInput").ap()
    d_wout = nc.dram_tensor("wout", [D, D], F32, kind="ExternalInput").ap()
    d_bout = nc.dram_tensor("bout_pp", [128, 8], F32, kind="ExternalInput").ap()
    d_wg = nc.dram_tensor("wg", [D, 1], F32, kind="ExternalInput").ap()
    d_bg = nc.dram_tensor("bg", [1, 1], F32, kind="ExternalInput").ap()
    d_wfc = nc.dram_tensor("w_fc", [HIDN, D], F32, kind="ExternalInput").ap()
    d_wproj = nc.dram_tensor("w_proj", [D, HIDN], F32, kind="ExternalInput").ap()
    d_uT = nc.dram_tensor("uT", [RANK, HIDN], F32, kind="ExternalInput").ap()
    d_vT = nc.dram_tensor("vT", [D, RANK], F32, kind="ExternalInput").ap()
    d_fbT = nc.dram_tensor("fbT", [RANK, RANK], F32, kind="ExternalInput").ap()
    d_gate = nc.dram_tensor("gate", [1, 1], F32, kind="ExternalInput").ap()
    d_masks = nc.dram_tensor("masks", [NQT, 128, 640], BF16, kind="ExternalInput").ap()
    d_ident = nc.dram_tensor("ident", [128, 128], BF16, kind="ExternalInput").ap()
    d_negi = nc.dram_tensor("negi", [128, 128], BF16, kind="ExternalInput").ap()
    d_e16 = nc.dram_tensor("e16", [128, NCH * 16], BF16, kind="ExternalInput").ap()
    d_i16 = nc.dram_tensor("i16", [16, 16], BF16, kind="ExternalInput").ap()
    d_row1 = nc.dram_tensor("row1", [1, 128], BF16, kind="ExternalInput").ap()
    d_statc = nc.dram_tensor("statc", [128, 1], BF16, kind="ExternalInput").ap()
    d_one11 = nc.dram_tensor("one11", [1, 1], BF16, kind="ExternalInput").ap()

    d_out = nc.dram_tensor("out", [D, TOWN], F32, kind="ExternalOutput").ap()
    d_tproj = nc.dram_tensor("tproj_spill", [NHI, 128, D], BF16).ap()
    d_tfcs = nc.dram_tensor("tfc_spill", [NCH, 128, HIDN], BF16).ap()

    P = Plan()
    dbg_outs = {}

    def tap(name, src_aps, waits):
        if name not in debug:
            return
        p0, f0 = src_aps[0].shape[-2], src_aps[0].shape[-1]
        dt = src_aps[0].dtype
        dten = nc.dram_tensor("dbg_" + name, [len(src_aps), p0, f0], dt,
                              kind="ExternalOutput").ap()
        dbg_outs[name] = True
        for i, ap in enumerate(src_aps):
            P.dma("gpsimd", dten[i], ap, s_st, waits=(waits if i == 0 else []))

    ctx = ExitStack()
    with ctx:
        def mksb(stack, name, shape, dt=F32, side="left"):
            return stack.enter_context(nc.sbuf_tensor("sb_" + name, shape, dt, side=side))

        def sb(name, shape, dt=F32, side="left"):
            return mksb(ctx, name, shape, dt, side)

        def sem(name):
            return ctx.enter_context(nc.semaphore(name))

        s_ld = sem("s_ld"); s_x = sem("s_x"); s_w = sem("s_w"); s_mk = sem("s_mk")
        s_st = sem("s_st"); s_te = sem("s_te"); s_ve = sem("s_ve"); s_ac = sem("s_ac")
        s_gp = sem("s_gp")
        sxp = [sem("sxp0"), sem("sxp1")]
        swq = [sem("swq0"), sem("swq1")]
        swv = [sem("swv0"), sem("swv1")]
        swo = [sem("swo0"), sem("swo1")]
        swt = [sem("swt0"), sem("swt1")]
        sfc2 = [sem("sfc0"), sem("sfc1")]
        spr = [sem("spr0"), sem("spr1")]
        ssp = [sem("ssp0"), sem("ssp1")]
        smk2 = [sem("smk0"), sem("smk1")]
        s_u = sem("s_u")
        P.eng_sem = {"tensor": s_te, "vector": s_ve, "scalar": s_ac, "gpsimd": s_gp}
        P.sems = [s_ld, s_x, s_w, s_mk, s_st, s_te, s_ve, s_ac, s_gp, s_u] + \
            sxp + swq + swv + swo + swt + sfc2 + spr + ssp + smk2

        psA = ctx.enter_context(nc.psum_tensor("psA", [128, 1024], F32))
        psB = ctx.enter_context(nc.psum_tensor("psB", [128, 1024], F32))
        psC = ctx.enter_context(nc.psum_tensor("psC", [128, 1024], F32))
        psD = ctx.enter_context(nc.psum_tensor("psD", [128, 1024], F32))

        ident = sb("ident", [128, 128], BF16)
        i16 = sb("i16", [16, 16], BF16)
        row1 = sb("row1", [1, 128], BF16)
        statc = sb("statc", [128, 1], BF16)
        one11 = sb("one11", [1, 1], BF16)
        bqkv = sb("bqkv", [128, 24], F32)
        bvrow_bf = sb("bvrow_bf", [1, D], BF16)
        bout = sb("bout", [128, 8], F32)
        wg = sb("wg", [128, 8], F32)
        wg_bf = sb("wg_bf", [128, 8], BF16)
        bg11 = sb("bg11", [1, 1], F32)
        bg_bf = sb("bg_bf", [1, 1], BF16)
        bg128 = sb("bg128", [128, 1], F32)
        gate11 = sb("gate11", [1, 1], F32)
        gate_bf = sb("gate_bf", [1, 1], BF16)
        g16 = sb("g16", [16, 1], F32)
        vT_f = sb("vT_f", [128, NCH * 16], F32)
        vT_bf = sb("vT_bf", [128, NCH * 16], BF16)
        fbT_f = sb("fbT_f", [16, 16], F32)
        fbTg = sb("fbTg", [16, 16], BF16)
        gcol = sb("gcol", [128, NQT], F32)
        gm1col = sb("gm1col", [128, NQT], F32)
        negm8 = sb("negm8", [128, 1], F32)
        nm8bf = sb("nm8bf", [1, 1], BF16)
        rowbuf = sb("rowbuf", [1, 2 * GN], F32)
        rowbuf2 = sb("rowbuf2", [1, GN], F32)
        rown_bf = sb("rown_bf", [1, GN], BF16)
        rowr_bf = sb("rowr_bf", [1, GN], BF16)
        rb_sb = sb("rb_sb", [128, GN], F32)
        lnx = sb("lnx", [128, NCH * GN], BF16)
        lnx2 = sb("lnx2", [128, 2 * GN], BF16)
        lnt = sb("lnt", [128, GN], BF16)
        ltile = sb("ltile", [128, 16], F32)
        ltecho = sb("ltecho", [128, 16], F32)
        lgtile = sb("lgtile", [128, 16], F32)
        rltile = sb("rltile", [128, 16], F32)
        rlgtile = sb("rlgtile", [128, 16], F32)
        bndrow = sb("bndrow", [1, 16], F32)
        bnd16 = sb("bnd16", [16, 4], F32)
        bnd16b = sb("bnd16b", [16, 1], BF16)
        bndsc = sb("bndsc", [16, 12], F32)

        ctx_qkv = ExitStack()
        q_fm = [mksb(ctx_qkv, f"q{c}", [128, TOWN], BF16) for c in range(NCH)]
        k_fm = [mksb(ctx_qkv, f"k{c}", [128, EXT], BF16) for c in range(NCH)]
        v_tok = [mksb(ctx_qkv, f"v{t}", [128, D], BF16) for t in range(NKT)]

        for o, i in [(ident, d_ident), (i16, d_i16), (row1, d_row1),
                     (statc, d_statc), (one11, d_one11), (bqkv, d_bqkv),
                     (bout, d_bout), (bg11, d_bg), (gate11, d_gate), (fbT_f, d_fbT)]:
            P.dma("sync", o[:], i, s_ld)
        P.dma("sync", wg.ap().rearrange("p (c o) -> p c o", o=1),
              d_wg.rearrange("(c p) o -> p c o", p=128), s_ld, slow=True)
        P.dma("sync", vT_f.ap().rearrange("p (c n) -> p c n", c=NCH),
              d_vT.rearrange("(c p) n -> p c n", p=128), s_ld)
        P.dma("sync", rowbuf[0:1, 0:D], d_bvrow, s_ld)
        n_ld = P.n(s_ld)

        P.add("vector", lambda e: e.tensor_copy(bvrow_bf[:], rowbuf[0:1, 0:D]),
              waits=[(s_ld, n_ld)])
        P.add("vector", lambda e: e.tensor_copy(wg_bf[:], wg[:]))
        P.add("vector", lambda e: e.tensor_copy(vT_bf[:], vT_f[:]))
        P.add("vector", lambda e: e.tensor_copy(gate_bf[:], gate11[:]))
        P.add("vector", lambda e: e.tensor_copy(bg_bf[:], bg11[:]), incs=[(s_ve, 1)])
        ve0 = P.n(s_ve)
        P.add("tensor", lambda e: e.matmul(psD[0:16, 0:1], row1[:, 0:16], gate_bf[:],
                                           start=True, stop=True),
              waits=[(s_ve, ve0)], incs=[(s_te, 1)])
        P.add("vector", lambda e: e.tensor_copy(g16[:], psD[0:16, 0:1]),
              waits=[(s_te, P.n(s_te))])
        P.add("vector", lambda e: e.tensor_scalar(out=fbTg[:], in0=fbT_f[:], scalar1=g16[:],
                                                  scalar2=None, op0=ALU.mult), incs=[(s_ve, 1)])
        P.add("tensor", lambda e: e.matmul(psD[:, 1:2], row1[:], bg_bf[:], start=True, stop=True),
              waits=[(s_ve, P.n(s_ve))], incs=[(s_te, 1)])
        P.add("vector", lambda e: e.tensor_copy(bg128[:], psD[:, 1:2]),
              waits=[(s_te, P.n(s_te))], incs=[(s_ve, 1)])
        ve_consts = P.n(s_ve)

        def emit_ln(gn, get_src, put, w_first=(), cast_hook=None, post_cast=None):
            te_sq = {}
            for c in range(NCH):
                w_c = list(w_first) if c == 0 else []
                if cast_hook is not None:
                    w_c = w_c + cast_hook(c)
                sq = lnx2[:, (c % 2) * gn:(c % 2) * gn + gn]
                if c >= 2:
                    w_c.append((s_te, te_sq[c - 2]))
                P.add("vector",
                      (lambda e, c=c: e.tensor_copy(lnx[:, c * gn:(c + 1) * gn], get_src(c))),
                      waits=w_c)
                P.add("vector",
                      (lambda e, c=c, sq=sq: e.tensor_mul(sq, lnx[:, c * gn:(c + 1) * gn],
                                                          lnx[:, c * gn:(c + 1) * gn])),
                      incs=[(s_ve, 1)])
                ve_c = P.n(s_ve)
                if post_cast is not None:
                    post_cast(c)
                P.add("tensor",
                      (lambda e, c=c: e.matmul(psD[0:1, 0:gn], statc[:],
                                               lnx[:, c * gn:(c + 1) * gn],
                                               start=(c == 0), stop=(c == NCH - 1))),
                      waits=[(s_ve, ve_c)])
                P.add("tensor",
                      (lambda e, c=c, sq=sq: e.matmul(psD[0:1, 512:512 + gn], statc[:], sq,
                                               start=(c == 0), stop=(c == NCH - 1))),
                      incs=[(s_te, 1)])
                te_sq[c] = P.n(s_te)
            te_stats = P.n(s_te)
            P.add("vector", lambda e: e.tensor_copy(rowbuf[:, 0:gn], psD[0:1, 0:gn]),
                  waits=[(s_te, te_stats)])
            P.add("vector", lambda e: e.tensor_copy(rowbuf[:, gn:2 * gn],
                                                    psD[0:1, 512:512 + gn]))
            P.add("vector", lambda e: e.tensor_mul(rowbuf2[:, 0:gn], rowbuf[:, 0:gn],
                                                   rowbuf[:, 0:gn]))
            P.add("vector", lambda e: e.scalar_tensor_tensor(
                out=rowbuf2[:, 0:gn], in0=rowbuf[:, gn:2 * gn], scalar=-1.0,
                in1=rowbuf2[:, 0:gn], op0=ALU.mult, op1=ALU.subtract))
            P.add("vector", lambda e: e.tensor_copy(rown_bf[:, 0:gn], rowbuf[:, 0:gn]),
                  incs=[(s_ve, 1)])
            ve_rows = P.n(s_ve)
            P.add("scalar", lambda e: e.activation(out=rowbuf2[:, 0:gn], in_=rowbuf2[:, 0:gn],
                                                   func=ACT.Sqrt, bias=np.float32(1e-5),
                                                   scale=1.0),
                  waits=[(s_ve, ve_rows)], incs=[(s_ac, 1)])
            P.add("vector", lambda e: e.reciprocal(rowbuf2[:, 0:gn], rowbuf2[:, 0:gn]),
                  waits=[(s_ac, P.n(s_ac))])
            P.add("vector", lambda e: e.tensor_copy(rowr_bf[:, 0:gn], rowbuf2[:, 0:gn]),
                  incs=[(s_ve, 1)])
            ve_r = P.n(s_ve)
            P.add("tensor", lambda e: e.matmul(psC[:, 0:gn], row1[:], rowr_bf[:, 0:gn],
                                               start=True, stop=True),
                  waits=[(s_ve, ve_r)], incs=[(s_te, 1)])
            P.add("vector", lambda e: e.tensor_copy(rb_sb[:, 0:gn], psC[:, 0:gn]),
                  waits=[(s_te, P.n(s_te))], incs=[(s_ve, 1)])
            ve_rb = P.n(s_ve)
            for c in range(NCH):
                reg = psA[:, 0:gn] if c % 2 == 0 else psB[:, 0:gn]
                w_reg = [(s_ve, P.n(s_ve))] if c >= 2 else [(s_ve, ve_rb)]
                P.add("tensor", (lambda e, reg=reg: e.matmul(reg, row1[:], rown_bf[:, 0:gn],
                                                             start=True, stop=False)),
                      waits=w_reg)
                P.add("tensor", (lambda e, c=c, reg=reg: e.matmul(
                    reg, ident[:], lnx[:, c * gn:(c + 1) * gn], start=False, stop=True)),
                      incs=[(s_te, 1)])
                put(c, reg, P.n(s_te))

        P.phase = "p1a"
        ctx_p1 = ExitStack()
        xn = [mksb(ctx_p1, f"xn{c}", [128, EXT], BF16, side="right") for c in range(NCH)]
        ctx_p1a = ExitStack()
        xstg = [mksb(ctx_p1a, f"xstg{b}", [128, G1N], F32, side="right") for b in range(2)]

        x_view = d_x.rearrange("(cc p) n -> cc p n", p=128)
        cast_ve = {}

        for g in range(G1):
            x_lds = {}

            def plan_xdma(g, c):
                prev = (g, c - 2) if c >= 2 else ((g - 1, NCH - 2 + c) if g > 0 else None)
                if prev is not None and prev in cast_ve:
                    w = [(s_ve, cast_ve[prev])]
                else:
                    w = [(s_ve, ve_consts)]
                P.dma("sync", xstg[c % 2][:],
                      x_view[c, :, g * G1N:(g + 1) * G1N], sxp[c % 2], waits=w)
                x_lds[(g, c)] = (sxp[c % 2], P.n(sxp[c % 2]))

            plan_xdma(g, 0)
            plan_xdma(g, 1)

            def cast_hook(c, g=g):
                return [x_lds[(g, c)]]

            def post_cast(c, g=g):
                cast_ve[(g, c)] = P.n(s_ve)
                if c + 2 < NCH:
                    plan_xdma(g, c + 2)

            def put_xn(c, reg, te_val, g=g):
                P.add("vector", (lambda e, c=c, g=g, reg=reg: e.tensor_mul(
                    xn[c][:, g * G1N:(g + 1) * G1N], reg, rb_sb[:, 0:G1N])),
                      waits=[(s_te, te_val)], incs=[(s_ve, 1)])

            emit_ln(G1N, (lambda c: xstg[c % 2][:]), put_xn, cast_hook=cast_hook,
                    post_cast=post_cast)
        ve_ln1 = P.n(s_ve)
        tap("xn", [xn[c][:] for c in range(NCH)], [(s_ve, ve_ln1)])
        ctx_p1a.close()

        P.phase = "p1b"
        ctx_qk = ExitStack()
        wstg = [mksb(ctx_qk, f"wstg{b}", [128, 1024], F32, side="right") for b in range(2)]
        wbfq = [mksb(ctx_qk, f"wbfq{b}", [128, 1024], BF16, side="right") for b in range(2)]

        te_lastmm = {0: None, 1: None}
        cast_done_w = {0: None, 1: None}
        ridx = 0
        ac_ev = [None, None]
        for fo in range(16):
            par = fo % 2
            stg, wbf = wstg[par], wbfq[par]
            w_w = [(s_ve, cast_done_w[par])] if cast_done_w[par] is not None else \
                [(s_ve, ve_ln1)]
            for fi in range(NCH):
                P.dma("sync", stg[:, fi * 128:(fi + 1) * 128],
                      d_wqkv[128 * fi:128 * fi + 128, 128 * fo:128 * fo + 128], swq[par],
                      waits=(w_w if fi == 0 else []))
            w_ld = P.n(swq[par])
            w_cast = [(swq[par], w_ld)]
            if te_lastmm[par] is not None:
                w_cast.append((s_te, te_lastmm[par]))
            P.add("vector", (lambda e, stg=stg, wbf=wbf: e.tensor_copy(wbf[:], stg[:])),
                  waits=w_cast, incs=[(s_ve, 1)])
            ve_cast = P.n(s_ve)
            cast_done_w[par] = ve_cast
            is_q = fo < 8
            ngrp = 4 if is_q else G1
            gn = 512 if is_q else G1N
            for g in range(ngrp):
                reg = psA[:, 0:gn] if ridx % 2 == 0 else psB[:, 0:gn]
                w_mm = [(s_ve, ve_cast)]
                if ac_ev[ridx % 2] is not None:
                    w_mm.append((s_ac, ac_ev[ridx % 2]))
                for fi in range(NCH):
                    if is_q:
                        mk_mm = (lambda e, fi=fi, g=g, wbf=wbf, reg=reg: e.matmul(
                            reg, wbf[:, fi * 128:(fi + 1) * 128],
                            xn[fi][:, 256 + g * 512:256 + (g + 1) * 512],
                            start=(fi == 0), stop=(fi == NCH - 1)))
                    else:
                        mk_mm = (lambda e, fi=fi, g=g, wbf=wbf, reg=reg: e.matmul(
                            reg, wbf[:, fi * 128:(fi + 1) * 128],
                            xn[fi][:, g * G1N:(g + 1) * G1N],
                            start=(fi == 0), stop=(fi == NCH - 1)))
                    P.add("tensor", mk_mm, waits=(w_mm if fi == 0 else []),
                          incs=([(s_te, 1)] if fi == NCH - 1 else []))
                te_mm = P.n(s_te)
                te_lastmm[par] = te_mm
                if is_q:
                    dst = (lambda e, fo=fo, g=g, reg=reg: e.activation(
                        out=q_fm[fo][:, g * 512:(g + 1) * 512], in_=reg, func=ACT.Identity,
                        bias=bqkv[:, fo:fo + 1], scale=1.0))
                else:
                    dst = (lambda e, fo=fo, g=g, reg=reg: e.activation(
                        out=k_fm[fo - 8][:, g * G1N:(g + 1) * G1N], in_=reg, func=ACT.Identity,
                        bias=bqkv[:, fo:fo + 1], scale=1.0))
                P.add("scalar", dst, waits=[(s_te, te_mm)], incs=[(s_ac, 1)])
                ac_ev[ridx % 2] = P.n(s_ac)
                ridx += 1
        ctx_qk.close()

        P.phase = "pv"
        ctx_v = ExitStack()
        wvstg = [mksb(ctx_v, f"wvstg{b}", [128, 512], F32, side="right") for b in range(2)]
        wvbf = mksb(ctx_v, "wvbf", [128, NCH * 512], BF16, side="right")
        te_qk_done = P.n(s_te)
        for vh in range(2):
            for fi in range(NCH):
                stg = wvstg[fi % 2]
                w_w = [(s_ve, P.n(s_ve))] if (vh * NCH + fi) >= 2 else \
                    [(s_te, te_qk_done)]
                P.dma("sync", stg[:],
                      d_wqkv[128 * fi:128 * fi + 128, 2048 + vh * 512:2048 + (vh + 1) * 512],
                      swv[fi % 2], waits=w_w)
                w_c = [(swv[fi % 2], P.n(swv[fi % 2]))]
                if vh == 1 and fi == 0:
                    w_c.append((s_te, P.n(s_te)))
                P.add("vector", (lambda e, stg=stg, fi=fi: e.tensor_copy(
                    wvbf[:, fi * 512:(fi + 1) * 512], stg[:])),
                      waits=w_c, incs=[(s_ve, 1)])
            ve_wv = P.n(s_ve)
            for t in range(NKT):
                reg = psA[:, 0:512] if ridx % 2 == 0 else psB[:, 0:512]
                w_mm = [(s_ve, ve_wv)]
                if ac_ev[ridx % 2] is not None:
                    w_mm.append((s_ac, ac_ev[ridx % 2]))
                P.add("tensor", (lambda e, reg=reg, vh=vh: e.matmul(
                    reg, row1[:], bvrow_bf[:, vh * 512:(vh + 1) * 512],
                    start=True, stop=False)), waits=w_mm)
                for fi in range(NCH):
                    P.add("tensor", (lambda e, fi=fi, t=t, reg=reg: e.matmul(
                        reg, xn[fi][:, t * 128:(t + 1) * 128], wvbf[:, fi * 512:(fi + 1) * 512],
                        start=False, stop=(fi == NCH - 1))),
                          incs=([(s_te, 1)] if fi == NCH - 1 else []))
                te_mm = P.n(s_te)
                P.add("scalar", (lambda e, t=t, vh=vh, reg=reg: e.activation(
                    out=v_tok[t][:, vh * 512:(vh + 1) * 512], in_=reg, func=ACT.Copy)),
                      waits=[(s_te, te_mm)], incs=[(s_ac, 1)])
                ac_ev[ridx % 2] = P.n(s_ac)
                ridx += 1
        ctx_v.close()

        ctx_g = ExitStack()
        P.phase = "pg"
        zrow_bf = mksb(ctx_g, "zrow_bf", [1, TOWN], BF16, side="right")
        ve_zg = None
        for g in range(4):
            w_z = [(s_ve, ve_zg)] if ve_zg is not None else [(s_ve, P.n(s_ve))]
            for fi in range(NCH):
                P.add("tensor", (lambda e, fi=fi, g=g: e.matmul(
                    psD[0:1, 0:512], wg_bf[:, fi:fi + 1],
                    xn[fi][:, 256 + g * 512:256 + (g + 1) * 512],
                    start=(fi == 0), stop=(fi == NCH - 1))),
                      waits=(w_z if fi == 0 else []),
                      incs=([(s_te, 1)] if fi == NCH - 1 else []))
            te_z = P.n(s_te)
            P.add("vector", (lambda e: e.tensor_copy(rowbuf[:, 0:512], psD[0:1, 0:512])),
                  waits=[(s_te, te_z)])
            P.add("vector", (lambda e, g=g: e.tensor_copy(
                zrow_bf[:, g * 512:(g + 1) * 512], rowbuf[:, 0:512])), incs=[(s_ve, 1)])
            ve_zg = P.n(s_ve)
        for t in range(NQT):
            P.add("tensor", (lambda e, t=t: e.matmul(
                psC[:, t:t + 1], zrow_bf[:, t * 128:(t + 1) * 128], one11[:],
                start=True, stop=(t == NQT - 1))),
                  waits=([(s_ve, ve_zg)] if t == 0 else []),
                  incs=([(s_te, 1)] if t == NQT - 1 else []))
        te_zt = P.n(s_te)
        P.add("scalar", lambda e: e.activation(out=gcol[:], in_=psC[:, 0:16], func=ACT.Sigmoid,
                                               bias=bg128[:], scale=1.0),
              waits=[(s_te, te_zt)], incs=[(s_ac, 1)])
        P.add("vector", lambda e: e.tensor_scalar(out=gm1col[:], in0=gcol[:], scalar1=-1.0,
                                                  scalar2=1.0, op0=ALU.mult, op1=ALU.add),
              waits=[(s_ac, P.n(s_ac))], incs=[(s_ve, 1)])
        ve_gate = P.n(s_ve)
        tap("q", [q_fm[c][:] for c in range(NCH)], [(s_ac, P.n(s_ac))])
        tap("k", [k_fm[c][:] for c in range(NCH)], [(s_ac, P.n(s_ac))])
        tap("v", [v_tok[t][:] for t in range(NKT)], [(s_ac, P.n(s_ac))])
        tap("gate", [gcol[:], gm1col[:]], [(s_ve, ve_gate)])

        P.phase = "p2a"
        e16s = mksb(ctx_g, "e16s", [128, NCH * 16], BF16, side="right")
        q2b = [mksb(ctx_g, f"q2b{b}", [128, 512], BF16, side="right") for b in range(2)]
        P.dma("sync", e16s[:], d_e16, s_ld, waits=[(s_te, P.n(s_te))])
        n_ld2 = P.n(s_ld)

        def norms(buf_fm, ncols, outcol0):
            njg = (ncols + 511) // 512
            for jg in range(njg):
                j0 = jg * 512
                n = min(512, ncols - j0)
                for c in range(NCH):
                    b = q2b[c % 2]
                    w_sq = [(s_te, P.n(s_te))] if (jg or outcol0 or c >= 2) \
                        else [(s_ac, P.n(s_ac)), (s_ld, n_ld2)]
                    P.add("vector", (lambda e, c=c, b=b, j0=j0, n=n, buf_fm=buf_fm: e.tensor_mul(
                        b[:, 0:n], buf_fm[c][:, j0:j0 + n], buf_fm[c][:, j0:j0 + n])),
                          waits=w_sq, incs=[(s_ve, 1)])
                    ve_sq = P.n(s_ve)
                    P.add("tensor", (lambda e, c=c, b=b, n=n: e.matmul(
                        psA[0:16, 0:n], e16s[:, c * 16:(c + 1) * 16], b[:, 0:n],
                        start=(c == 0), stop=(c == NCH - 1))),
                          waits=[(s_ve, ve_sq)], incs=[(s_te, 1)])
                te_n = P.n(s_te)
                P.add("vector", (lambda e, jg=jg, n=n, outcol0=outcol0: e.reduce_max(
                    bndsc[:, outcol0 + jg:outcol0 + jg + 1], psA[0:16, 0:n], axis=AX)),
                      waits=[(s_te, te_n)], incs=[(s_ve, 1)])
            return njg

        nq = norms(q_fm, TOWN, 0)
        nk = norms(k_fm, 2624, nq)
        P.add("vector", lambda e, nq=nq: e.reduce_max(bnd16[:, 0:1], bndsc[:, 0:nq], axis=AX))
        P.add("vector", lambda e, nq=nq, nk=nk: e.reduce_max(bnd16[:, 1:2],
                                                             bndsc[:, nq:nq + nk], axis=AX))
        P.add("vector", lambda e: e.tensor_mul(bnd16[:, 2:3], bnd16[:, 0:1], bnd16[:, 1:2]))
        P.add("vector", lambda e: e.tensor_copy(bnd16b[:], bnd16[:, 2:3]), incs=[(s_ve, 1)])
        P.add("tensor", lambda e: e.matmul(psD[0:1, 0:16], bnd16b[:], i16[:],
                                           start=True, stop=True),
              waits=[(s_ve, P.n(s_ve))], incs=[(s_te, 1)])
        P.add("vector", lambda e: e.tensor_copy(bndrow[:], psD[0:1, 0:16]),
              waits=[(s_te, P.n(s_te))])
        P.add("vector", lambda e: e.reduce_max(bnd16[0:1, 3:4], bndrow[:], axis=AX),
              incs=[(s_ve, 1)])
        P.add("scalar", lambda e: e.activation(out=bndrow[:, 1:2], in_=bnd16[0:1, 3:4],
                                               func=ACT.Sqrt, bias=np.float32(1e-6), scale=1.0),
              waits=[(s_ve, P.n(s_ve))], incs=[(s_ac, 1)])
        P.add("vector", lambda e: e.tensor_scalar(out=bndrow[:, 2:3], in0=bndrow[:, 1:2],
                                                  scalar1=-0.125, scalar2=-0.25,
                                                  op0=ALU.mult, op1=ALU.add),
              waits=[(s_ac, P.n(s_ac))])
        P.add("vector", lambda e: e.tensor_copy(nm8bf[:], bndrow[:, 2:3]), incs=[(s_ve, 1)])
        P.add("tensor", lambda e: e.matmul(psD[:, 2:3], row1[:], nm8bf[:], start=True, stop=True),
              waits=[(s_ve, P.n(s_ve))], incs=[(s_te, 1)])
        P.add("vector", lambda e: e.tensor_copy(negm8[:], psD[:, 2:3]),
              waits=[(s_te, P.n(s_te))], incs=[(s_ve, 1)])
        ve_negm8 = P.n(s_ve)
        tap("negm8", [negm8[:]], [(s_ve, ve_negm8)])
        tap("bndsc", [bndsc[:]], [(s_ve, ve_negm8)])
        tap("bndrow", [bndrow[:]], [(s_ve, ve_negm8)])

        ctx_g.close()
        ctx_p1.close()

        P.phase = "p2b"
        ctx_p2 = ExitStack()
        attnfm = mksb(ctx_p2, "attnfm", [128, NCH * TOWN], BF16, side="right")
        mks = [mksb(ctx_p2, f"mks{b}", [128, 640], BF16, side="right") for b in range(2)]
        pbuf = [mksb(ctx_p2, f"pbuf{b}", [128, 704], BF16, side="right") for b in range(2)]
        ptn = mksb(ctx_p2, "ptn", [128, 768], BF16, side="right")
        negi = mksb(ctx_p2, "negi", [128, 128], BF16, side="right")
        gdiag = mksb(ctx_p2, "gdiag", [128, 128], BF16, side="right")
        ggdiag = mksb(ctx_p2, "ggdiag", [128, 128], BF16, side="right")
        dlb = [mksb(ctx_p2, f"dlb{i}", [128, 128], BF16, side="right") for i in range(2)]
        dgb = [mksb(ctx_p2, f"dgb{i}", [128, 128], BF16, side="right") for i in range(2)]
        P.dma("sync", negi[:], d_negi, s_ld, waits=[(s_te, P.n(s_te))])
        n_ld3 = P.n(s_ld)

        te_T = [None, None]
        ve_av = [None, None]
        ac_exp_last = None
        te_mask_last = [None, None]
        ev_pt_prev = None

        attn_view = attnfm.ap().rearrange("p (c t) -> p c t", c=NCH)
        psd_view = psD.ap().rearrange("p (c t) -> p c t", c=NCH)

        for qt in range(NQT):
            mk = mks[qt % 2]
            if te_mask_last[qt % 2] is not None:
                w_mk = [(s_te, te_mask_last[qt % 2])]
            elif qt == 0:
                w_mk = [(s_te, P.n(s_te))]
            else:
                w_mk = []
            P.dma("sync", mk[:], d_masks[qt], smk2[qt % 2], waits=w_mk)
            mk_ld = (smk2[qt % 2], P.n(smk2[qt % 2]))

            w0 = [(s_ve, max(ve_negm8, ve_gate))] if qt == 0 else [(s_ac, ac_exp_last)]
            P.add("vector", lambda e: e.memset(ltile[:], 0.0), waits=w0)
            P.add("vector", lambda e: e.memset(lgtile[:], 0.0))
            P.add("vector", (lambda e, qt=qt: e.tensor_scalar(
                out=gdiag[:], in0=ident[:], scalar1=gm1col[:, qt:qt + 1],
                scalar2=None, op0=ALU.mult)))
            P.add("vector", (lambda e, qt=qt: e.tensor_scalar(
                out=ggdiag[:], in0=ident[:], scalar1=gcol[:, qt:qt + 1],
                scalar2=None, op0=ALU.mult)), incs=[(s_ve, 1)])
            ve_qt = P.n(s_ve)

            for pr in range(8):
                c = pr
                qq, kk = q_fm[c], k_fm[c]
                j0 = qt * 128
                P.phase = "p2b_s"
                w_s = [(s_ac, ac_exp_last)] if ac_exp_last is not None else [(s_ld, n_ld3)]
                P.add("tensor", (lambda e, qq=qq, kk=kk, j0=j0: e.matmul(
                    psA[:, 0:512], qq[0:64, j0:j0 + 128], kk[0:64, j0:j0 + 512],
                    start=True, stop=False, tile_position=(0, 0))), waits=w_s)
                P.add("tensor", (lambda e, qq=qq, kk=kk, j0=j0: e.matmul(
                    psB[:, 0:512], qq[64:128, j0:j0 + 128], kk[64:128, j0:j0 + 512],
                    start=True, stop=False, tile_position=(64, 0))))
                P.add("tensor", (lambda e, qq=qq, kk=kk, j0=j0: e.matmul(
                    psA[:, 640:704], qq[0:64, j0:j0 + 128], kk[0:64, 2560:2624],
                    start=True, stop=False, tile_position=(0, 0))))
                P.add("tensor", (lambda e, qq=qq, kk=kk, j0=j0: e.matmul(
                    psB[:, 640:704], qq[64:128, j0:j0 + 128], kk[64:128, 2560:2624],
                    start=True, stop=False, tile_position=(64, 0))))
                P.add("tensor", (lambda e, qq=qq, kk=kk, j0=j0: e.matmul(
                    psA[:, 512:640], qq[0:64, j0:j0 + 128], kk[0:64, j0 + 512:j0 + 640],
                    start=True, stop=False, tile_position=(0, 0))))
                P.add("tensor", (lambda e, qq=qq, kk=kk, j0=j0: e.matmul(
                    psB[:, 512:640], qq[64:128, j0:j0 + 128], kk[64:128, j0 + 512:j0 + 640],
                    start=True, stop=False, tile_position=(64, 0))))
                P.add("tensor", (lambda e, mk=mk: e.matmul(
                    psA[:, 0:512], negi[:], mk[:, 0:512], start=False, stop=False)),
                      waits=[mk_ld])
                P.add("tensor", (lambda e, mk=mk: e.matmul(
                    psA[:, 512:640], negi[:], mk[:, 512:640], start=False, stop=True)),
                      incs=[(s_te, 1)])
                te_s = [P.n(s_te), None]
                P.add("tensor", (lambda e, mk=mk: e.matmul(
                    psB[:, 0:512], negi[:], mk[:, 0:512], start=False, stop=False)))
                P.add("tensor", (lambda e, mk=mk: e.matmul(
                    psB[:, 512:640], negi[:], mk[:, 512:640], start=False, stop=True)),
                      incs=[(s_te, 1)])
                te_s[1] = P.n(s_te)
                te_mask_last[qt % 2] = te_s[1]

                for hh in range(2):
                    h = 2 * pr + hh
                    ps = psA if hh == 0 else psB
                    pb = pbuf[hh]
                    P.phase = "p2b_e"
                    w_e = [(s_te, te_s[hh]), (s_ve, ve_qt)]
                    if te_T[hh] is not None:
                        w_e.append((s_te, te_T[hh]))
                    P.add("scalar", (lambda e, ps=ps, pb=pb, h=h: e.activation(
                        out=pb[:], in_=ps[:, 0:704], func=ACT.Exp, bias=negm8[:],
                        scale=0.125)),
                          waits=w_e, incs=[(s_ac, 1)])
                    ac_exp = P.n(s_ac)
                    ac_exp_last = ac_exp
                    P.phase = "p2b_d"
                    P.add("vector", (lambda e, pb=pb, h=h: e.tensor_scalar(
                        out=pb[:, 0:704], in0=pb[:, 0:704], scalar1=1.0, scalar2=0.0,
                        op0=ALU.mult, op1=ALU.add, accum_out=ltile[:, h:h + 1])),
                          waits=[(s_ac, ac_exp)])
                    P.add("vector", (lambda e, pb=pb, h=h: e.reduce_sum(
                        lgtile[:, h:h + 1], pb[:, 640:704], axis=AX)))
                    P.add("vector", (lambda e, h=h: e.tensor_sub(
                        ltile[:, h:h + 1], ltile[:, h:h + 1], lgtile[:, h:h + 1])))
                    P.add("vector", (lambda e, h=h: e.reciprocal(
                        rltile[:, h:h + 1], ltile[:, h:h + 1])))
                    P.add("vector", (lambda e, h=h: e.reciprocal(
                        rlgtile[:, h:h + 1], lgtile[:, h:h + 1])))
                    w_d = [(s_te, te_T[hh])] if te_T[hh] is not None else []
                    P.add("vector", (lambda e, h=h, hh=hh: e.tensor_scalar(
                        out=dlb[hh][:], in0=gdiag[:], scalar1=rltile[:, h:h + 1],
                        scalar2=None, op0=ALU.mult)), waits=w_d)
                    P.add("vector", (lambda e, h=h, hh=hh: e.tensor_scalar(
                        out=dgb[hh][:], in0=ggdiag[:], scalar1=rlgtile[:, h:h + 1],
                        scalar2=None, op0=ALU.mult)), incs=[(s_ve, 1)])
                    ve_d = P.n(s_ve)
                    P.phase = "p2b_t"
                    w_t = [(s_ve, ve_d)]
                    if ev_pt_prev is not None:
                        w_t.append(ev_pt_prev)
                    for s in range(5):
                        P.add("tensor", (lambda e, pb=pb, hh=hh, s=s: e.matmul(
                            psC[:, 128 * s:128 * (s + 1)], pb[:, 128 * s:128 * (s + 1)],
                            dlb[hh][:], start=True, stop=(s == 3))),
                              waits=(w_t if s == 0 else []))
                    P.add("tensor", (lambda e, pb=pb, hh=hh: e.matmul(
                        psC[0:64, 640:768], pb[:, 640:704], dgb[hh][:],
                        start=True, stop=True)), incs=[(s_te, 1)])
                    te_T[hh] = P.n(s_te)
                    P.phase = "p2b_v"
                    if qt == 0 and pr == 0 and hh == 0:
                        tap("pb0", [pb[:]], [(s_ac, ac_exp)])
                        tap("lt0", [ltile[:], lgtile[:], rltile[:], rlgtile[:]],
                            [(s_ve, ve_d)])
                        tap("dl0", [dlb[0][:], dgb[0][:]], [(s_ve, ve_d)])
                    if hh == 0:
                        P.add("scalar", (lambda e: e.activation(out=ptn[:], in_=psC[:, 0:768],
                                                                func=ACT.Copy)),
                              waits=[(s_te, te_T[hh])], incs=[(s_ac, 1)])
                        ev_pt = (s_ac, P.n(s_ac))
                        ev_pt_prev = ev_pt
                    else:
                        P.add("vector", (lambda e: e.tensor_copy(ptn[:], psC[:, 0:768])),
                              waits=[(s_te, te_T[hh])], incs=[(s_ve, 1)])
                        ev_pt = (s_ve, P.n(s_ve))
                    ev_pt_prev = ev_pt
                    if qt == 0 and pr == 0 and hh == 0:
                        tap("ptn0", [ptn[:]], [ev_pt])
                    P.phase = "p2b_pv"
                    bank = pr // 4
                    col0 = 512 * bank + 128 * (pr % 4)
                    w_pv = [ev_pt]
                    if pr % 4 == 0 and hh == 0 and ve_av[bank] is not None:
                        w_pv.append((s_ve, ve_av[bank]))
                    for s in range(5):
                        P.add("tensor", (lambda e, s=s, qt=qt, pr=pr, hh=hh, col0=col0: e.matmul(
                            psD[64 * hh:64 * hh + 64, col0:col0 + 128],
                            v_tok[qt + s][:, 128 * pr + 64 * hh:128 * pr + 64 * hh + 64],
                            ptn[:, 128 * s:128 * (s + 1)],
                            start=(s == 0), stop=False,
                            tile_position=(0, 64 * hh))),
                              waits=(w_pv if s == 0 else []))
                    is_last = (pr % 4 == 3 and hh == 1)
                    P.add("tensor", (lambda e, pr=pr, hh=hh, col0=col0: e.matmul(
                        psD[64 * hh:64 * hh + 64, col0:col0 + 128],
                        v_tok[20][0:64, 128 * pr + 64 * hh:128 * pr + 64 * hh + 64],
                        ptn[0:64, 640:768], start=False, stop=True,
                        tile_position=(0, 64 * hh))),
                          incs=([(s_te, 1)] if is_last else []))
                    if is_last:
                        te_pv = P.n(s_te)
                        P.phase = "p2b_av"
                        P.add("vector", (lambda e, qt=qt, bank=bank: e.tensor_copy(
                            attn_view[:, 4 * bank:4 * bank + 4, qt * 128:(qt + 1) * 128],
                            psd_view[:, 4 * bank:4 * bank + 4, 0:128])),
                              waits=[(s_te, te_pv)], incs=[(s_ve, 1)])
                        ve_av[bank] = P.n(s_ve)
        ve_attn = max(v for v in ve_av if v is not None)
        tap("attnfm", [attnfm[:]], [(s_ve, ve_attn)])

        P.phase = "p3"
        ctx_qkv.close()
        x_res = [sb(f"x{c}", [128, TOWN], F32) for c in range(NCH)]

        ctx_p3 = ExitStack()
        woutbf = [mksb(ctx_p3, f"wo{c}", [128, D], BF16, side="right") for c in range(NCH)]
        wostg = [mksb(ctx_p3, f"wostg{b}", [128, D], F32, side="right") for b in range(2)]
        ysb = [mksb(ctx_p3, f"ysb{c}", [128, GN], F32, side="right") for c in range(NCH)]

        for c in range(NCH):
            P.dma("sync", x_res[c][:], d_x[128 * c:128 * c + 128, 256:256 + TOWN], s_x,
                  waits=([(s_ve, ve_attn)] if c == 0 else []))
        x_ld = P.n(s_x)
        for c in range(NCH):
            stg = wostg[c % 2]
            w_w = [(s_ve, P.n(s_ve))] if c >= 2 else []
            P.dma("sync", stg[:], d_wout[128 * c:128 * c + 128, :], swo[c % 2], waits=w_w)
            P.add("vector", (lambda e, c=c, stg=stg: e.tensor_copy(woutbf[c][:], stg[:])),
                  waits=[(swo[c % 2], P.n(swo[c % 2]))], incs=[(s_ve, 1)])
        ve_wo = P.n(s_ve)

        for g in range(G):
            ac_prev = [None, None]
            for fo in range(NCH):
                reg = psA[:, 0:GN] if fo % 2 == 0 else psB[:, 0:GN]
                w_mm = [(s_ve, max(ve_wo, ve_attn))]
                if ac_prev[fo % 2] is not None:
                    w_mm.append((s_ac, ac_prev[fo % 2]))
                elif g > 0:
                    w_mm.append((s_ve, P.n(s_ve)))
                for fi in range(NCH):
                    P.add("tensor", (lambda e, fi=fi, fo=fo, g=g, reg=reg: e.matmul(
                        reg, woutbf[fi][:, 128 * fo:128 * (fo + 1)],
                        attn_view[:, fi, g * GN:(g + 1) * GN],
                        start=(fi == 0), stop=(fi == NCH - 1))),
                          waits=(w_mm if fi == 0 else []),
                          incs=([(s_te, 1)] if fi == NCH - 1 else []))
                te_mm = P.n(s_te)
                P.add("scalar", (lambda e, fo=fo, reg=reg: e.activation(
                    out=ysb[fo][:], in_=reg, func=ACT.Identity, bias=bout[:, fo:fo + 1],
                    scale=1.0)), waits=[(s_te, te_mm)], incs=[(s_ac, 1)])
                ac_prev[fo % 2] = P.n(s_ac)

            def put_res(cc, reg, te_val, g=g):
                P.add("vector", (lambda e, cc=cc, reg=reg: e.tensor_mul(
                    lnt[:, 0:GN], reg, rb_sb[:, 0:GN])),
                      waits=[(s_te, te_val)])
                P.add("vector", (lambda e, cc=cc, g=g: e.tensor_add(
                    x_res[cc][:, g * GN:(g + 1) * GN], x_res[cc][:, g * GN:(g + 1) * GN],
                    lnt[:, 0:GN])), incs=[(s_ve, 1)])

            w_ln = [(s_ac, P.n(s_ac))]
            if g == 0:
                w_ln.append((s_x, x_ld))
            emit_ln(GN, (lambda c: ysb[c][:]), put_res, w_first=w_ln)
        ve_p3 = P.n(s_ve)
        tap("x1", [x_res[c][:] for c in range(NCH)], [(s_ve, ve_p3)])
        ctx_p3.close()
        ctx_p2.close()

        P.phase = "p4"
        ctx_p4 = ExitStack()
        wtstg = [mksb(ctx_p4, f"wtstg{b}", [128, 1024], F32, side="right") for b in range(2)]
        wtbf = mksb(ctx_p4, "wtbf", [128, 1024], BF16, side="right")
        c1b = mksb(ctx_p4, "c1b", [128, 1024], BF16, side="right")
        weff = [mksb(ctx_p4, f"weff{b}", [128, 1024], BF16, side="right") for b in range(2)]
        tspill = [mksb(ctx_p4, f"tspill{b}", [128, 1024], BF16, side="right") for b in range(2)]
        scl = mksb(ctx_p4, "scl", [128, 8], F32, side="right")

        tern_state = {"ve_red": {}, "te_tr": {}, "ev_ps": {}, "st_tb": {}}

        def tern_weight(d_w, d_dst, nt, wcols, ve_start):
            np_ = wcols // 1024
            ve_red = tern_state["ve_red"]
            te_tr = tern_state["te_tr"]
            ev_ps = tern_state["ev_ps"]
            st_tb = tern_state["st_tb"]
            for t in range(nt):
                for dd in range(np_):
                    par = dd % 2
                    if par in ve_red:
                        w_w = [(s_ve, ve_red[par])]
                    elif t == 0:
                        w_w = [(s_ve, ve_start)]
                    else:
                        w_w = []
                    P.dma("sync", wtstg[par][:],
                          d_w[128 * t:128 * (t + 1), 1024 * dd:1024 * (dd + 1)], swt[par],
                          waits=w_w)
                    P.add("vector", (lambda e, par=par, dd=dd: e.reduce_sum(
                        scl[:, dd:dd + 1], wtstg[par][:], axis=AX,
                        apply_absolute_value=True)),
                          waits=[(swt[par], P.n(swt[par]))], incs=[(s_ve, 1)])
                    ve_red[par] = P.n(s_ve)
                if np_ > 1:
                    P.add("vector", (lambda e, np_=np_: e.reduce_sum(
                        scl[:, 4:5], scl[:, 0:np_], axis=AX)))
                else:
                    P.add("vector", (lambda e: e.tensor_copy(scl[:, 4:5], scl[:, 0:1])))
                P.add("vector", (lambda e, wcols=wcols: e.tensor_scalar(
                    out=scl[:, 5:6], in0=scl[:, 4:5], scalar1=1.0 / wcols,
                    scalar2=1e-5, op0=ALU.mult, op1=ALU.max)))
                P.add("vector", (lambda e: e.tensor_scalar(
                    out=scl[:, 6:7], in0=scl[:, 5:6], scalar1=0.5, scalar2=None,
                    op0=ALU.mult)))
                P.add("vector", (lambda e: e.tensor_scalar(
                    out=scl[:, 7:8], in0=scl[:, 5:6], scalar1=-0.5, scalar2=None,
                    op0=ALU.mult)), incs=[(s_ve, 1)])
                ve_scl = P.n(s_ve)
                for dd in range(np_):
                    par = dd % 2
                    P.dma("sync", wtstg[par][:],
                          d_w[128 * t:128 * (t + 1), 1024 * dd:1024 * (dd + 1)], swt[par],
                          waits=[(s_ve, ve_scl if dd < 2 else ve_red[par])])
                    P.add("vector", (lambda e, par=par: e.tensor_scalar(
                        out=c1b[:], in0=wtstg[par][:], scalar1=scl[:, 6:7], scalar2=scl[:, 5:6],
                        op0=ALU.is_gt, op1=ALU.mult)),
                          waits=[(swt[par], P.n(swt[par]))])
                    we = weff[par]
                    w_we = [(s_te, te_tr[par])] if par in te_tr else []
                    P.add("vector", (lambda e, we=we, par=par: e.tensor_scalar(
                        out=we[:], in0=wtstg[par][:], scalar1=scl[:, 7:8], scalar2=scl[:, 5:6],
                        op0=ALU.is_lt, op1=ALU.mult)), waits=w_we, incs=[(s_ve, 1)])
                    ve_red[par] = P.n(s_ve)
                    P.add("vector", (lambda e, we=we: e.tensor_sub(we[:], c1b[:], we[:])),
                          incs=[(s_ve, 1)])
                    ve_we = P.n(s_ve)
                    psT = psC if par == 0 else psD
                    w_tr = [(s_ve, ve_we)]
                    if id(psT) in ev_ps:
                        w_tr.append(ev_ps[id(psT)])
                    for j in range(8):
                        P.add("tensor", (lambda e, j=j, we=we, psT=psT: e.matmul(
                            psT[:, 128 * j:128 * (j + 1)], we[:, 128 * j:128 * (j + 1)],
                            ident[:], start=True, stop=(j in (3, 7)))),
                              waits=(w_tr if j == 0 else []),
                              incs=([(s_te, 1)] if j == 7 else []))
                    te_tr[par] = P.n(s_te)
                    tb = tspill[par]
                    w_e = [(s_te, te_tr[par])]
                    if par in st_tb:
                        w_e.append(st_tb[par])
                    P.add("scalar", (lambda e, tb=tb, psT=psT: e.activation(
                        out=tb[:], in_=psT[:, 0:1024], func=ACT.Copy)),
                          waits=w_e, incs=[(s_ac, 1)])
                    ev_ps[id(psT)] = (s_ac, P.n(s_ac))
                    P.add("gpsimd", (lambda e, t=t, dd=dd, tb=tb, d_dst=d_dst: e.dma_start(
                        out=d_dst.rearrange("j p n -> p j n")[:, 8 * dd:8 * dd + 8,
                                                              128 * t:128 * (t + 1)],
                        in_=tb.ap().rearrange("p (j n) -> p j n", j=8))),
                          waits=[ev_ps[id(psT)]], incs=[(ssp[par], 16)], dma=True)
                    st_tb[par] = (ssp[par], P.n(ssp[par]))

        tern_weight(d_wfc, d_tfcs, NHI, D, ve_p3)
        tern_weight(d_wproj, d_tproj, NCH, HIDN, P.n(s_ve))
        w_spill = [(ssp[0], P.n(ssp[0])), (ssp[1], P.n(ssp[1]))]
        ctx_p4.close()

        P.phase = "p5"
        ctx_p5 = ExitStack()
        xn2 = [mksb(ctx_p5, f"xn2_{c}", [128, GN], BF16, side="right") for c in range(NCH)]
        r2 = mksb(ctx_p5, "r2", [128, NHI * GN], BF16, side="right")
        rbuf = [mksb(ctx_p5, f"rbuf{b}", [128, GN], BF16, side="right") for b in range(2)]
        t1bf = mksb(ctx_p5, "t1bf", [16, GN], BF16, side="right")
        t2bf = mksb(ctx_p5, "t2bf", [16, GN], BF16, side="right")
        fstg = [mksb(ctx_p5, f"fstg{b}", [128, 1024], BF16, side="right") for b in range(2)]
        pstg = [mksb(ctx_p5, f"pstg{b}", [128, HIDN], BF16, side="right") for b in range(2)]
        ysb5 = [mksb(ctx_p5, f"y5_{c}", [128, GN], F32, side="right") for c in range(NCH)]
        ustg = mksb(ctx_p5, "ustg", [16, 1024], F32, side="right")
        uT_bf = mksb(ctx_p5, "uT_bf", [16, HIDN], BF16, side="right")

        ve_u = None
        for dd in range(4):
            w_u = [(s_ve, ve_u)] if ve_u is not None else list(w_spill)
            P.dma("sync", ustg[:], d_uT[:, 1024 * dd:1024 * (dd + 1)], s_u, waits=w_u)
            P.add("vector", (lambda e, dd=dd: e.tensor_copy(
                uT_bf[:, 1024 * dd:1024 * (dd + 1)], ustg[:])),
                  waits=[(s_u, P.n(s_u))], incs=[(s_ve, 1)])
            ve_u = P.n(s_ve)

        tfcs_view = d_tfcs.rearrange("j p n -> p j n")
        tproj_view = d_tproj.rearrange("j p n -> p j n")
        te_fcmm = {0: None, 1: None}
        te_projmm = {0: None, 1: None}

        for g in range(G):
            def put_xn2(c, reg, te_val):
                P.add("vector", (lambda e, c=c, reg=reg: e.tensor_mul(
                    xn2[c][:], reg, rb_sb[:, 0:GN])),
                      waits=[(s_te, te_val)], incs=[(s_ve, 1)])

            w_ln3 = [(s_ac, P.n(s_ac))] + list(w_spill)
            emit_ln(GN, (lambda c, g=g: x_res[c][:, g * GN:(g + 1) * GN]), put_xn2,
                    w_first=w_ln3)
            ve_xn2 = P.n(s_ve)
            for c in range(NCH):
                P.add("tensor", (lambda e, c=c: e.matmul(
                    psD[0:16, 0:GN], vT_bf[:, 16 * c:16 * (c + 1)], xn2[c][:],
                    start=(c == 0), stop=(c == NCH - 1))),
                      waits=([(s_ve, ve_xn2)] if c == 0 else []),
                      incs=([(s_te, 1)] if c == NCH - 1 else []))
            P.add("vector", lambda e: e.tensor_copy(t1bf[:], psD[0:16, 0:GN]),
                  waits=[(s_te, P.n(s_te))], incs=[(s_ve, 1)])
            P.add("tensor", lambda e: e.matmul(psD[0:16, 512:512 + GN], fbTg[:], t1bf[:],
                                               start=True, stop=True),
                  waits=[(s_ve, P.n(s_ve))], incs=[(s_te, 1)])
            P.add("vector", lambda e: e.tensor_copy(t2bf[:], psD[0:16, 512:512 + GN]),
                  waits=[(s_te, P.n(s_te))], incs=[(s_ve, 1)])
            ve_t2 = P.n(s_ve)
            ac_r = [None, None]
            ve_r2l = [None, None]
            for hi in range(NHI):
                fpar = hi % 2
                fst = fstg[fpar]
                if te_fcmm[fpar] is not None:
                    w_f = [(s_te, te_fcmm[fpar])]
                elif g == 0 and hi < 2:
                    w_f = list(w_spill)
                else:
                    w_f = []
                P.dma("sync", fst.ap().rearrange("p (j n) -> p j n", j=8),
                      tfcs_view[:, :, 128 * hi:128 * (hi + 1)], sfc2[fpar], waits=w_f)
                w_fl = (sfc2[fpar], P.n(sfc2[fpar]))
                reg = psA[:, 0:GN] if hi % 2 == 0 else psB[:, 0:GN]
                w_mm = [(s_ve, ve_t2), w_fl]
                if ac_r[hi % 2] is not None:
                    w_mm.append((s_ac, ac_r[hi % 2]))
                for fi in range(NCH):
                    P.add("tensor", (lambda e, fi=fi, fst=fst, reg=reg: e.matmul(
                        reg, fst[:, 128 * fi:128 * (fi + 1)], xn2[fi][:],
                        start=(fi == 0), stop=False)),
                          waits=(w_mm if fi == 0 else []))
                P.add("tensor", (lambda e, hi=hi, reg=reg: e.matmul(
                    reg, uT_bf[:, 128 * hi:128 * (hi + 1)], t2bf[:],
                    start=False, stop=True)), incs=[(s_te, 1)])
                te_h = P.n(s_te)
                te_fcmm[fpar] = te_h
                rb = rbuf[hi % 2]
                w_relu = [(s_te, te_h)]
                if ve_r2l[hi % 2] is not None:
                    w_relu.append((s_ve, ve_r2l[hi % 2]))
                P.add("scalar", (lambda e, rb=rb, reg=reg: e.activation(
                    out=rb[:], in_=reg, func=ACT.Relu)),
                      waits=w_relu, incs=[(s_ac, 1)])
                ac_r[hi % 2] = P.n(s_ac)
                P.add("vector", (lambda e, hi=hi, rb=rb: e.tensor_mul(
                    r2[:, hi * GN:(hi + 1) * GN], rb[:], rb[:])),
                      waits=[(s_ac, ac_r[hi % 2])], incs=[(s_ve, 1)])
                ve_r2l[hi % 2] = P.n(s_ve)
            ve_r2 = P.n(s_ve)
            ac_y = [None, None]
            for fo in range(NCH):
                pstgb = pstg[fo % 2]
                if te_projmm[fo % 2] is not None:
                    w_p = [(s_te, te_projmm[fo % 2])]
                elif g == 0:
                    w_p = list(w_spill)
                else:
                    w_p = []
                for dd in range(4):
                    P.dma("gpsimd", pstgb.ap().rearrange("p (j n) -> p j n", j=NHI)[
                        :, 8 * dd:8 * dd + 8, :],
                        tproj_view[:, 8 * dd:8 * dd + 8, 128 * fo:128 * (fo + 1)],
                        spr[fo % 2], waits=(w_p if dd == 0 else []))
                w_pl = (spr[fo % 2], P.n(spr[fo % 2]))
                reg = psC[:, 0:GN] if fo % 2 == 0 else psC[:, 512:512 + GN]
                w_y = [w_pl, (s_ve, ve_r2)]
                if ac_y[fo % 2] is not None:
                    w_y.append((s_ac, ac_y[fo % 2]))
                for hi in range(NHI):
                    P.add("tensor", (lambda e, hi=hi, reg=reg, pstgb=pstgb: e.matmul(
                        reg, pstgb[:, 128 * hi:128 * (hi + 1)], r2[:, hi * GN:(hi + 1) * GN],
                        start=(hi == 0), stop=(hi == NHI - 1))),
                          waits=(w_y if hi == 0 else []),
                          incs=([(s_te, 1)] if hi == NHI - 1 else []))
                te_projmm[fo % 2] = P.n(s_te)
                te_y = P.n(s_te)
                P.add("scalar", (lambda e, fo=fo, reg=reg: e.activation(
                    out=ysb5[fo][:], in_=reg, func=ACT.Copy)),
                      waits=[(s_te, te_y)], incs=[(s_ac, 1)])
                ac_y[fo % 2] = P.n(s_ac)

            def put_res5(cc, reg, te_val, g=g):
                P.add("vector", (lambda e, cc=cc, reg=reg: e.tensor_mul(
                    lnt[:, 0:GN], reg, rb_sb[:, 0:GN])),
                      waits=[(s_te, te_val)])
                P.add("vector", (lambda e, cc=cc, g=g: e.tensor_add(
                    x_res[cc][:, g * GN:(g + 1) * GN], x_res[cc][:, g * GN:(g + 1) * GN],
                    lnt[:, 0:GN])), incs=[(s_ve, 1)])

            emit_ln(GN, (lambda c: ysb5[c][:]), put_res5, w_first=[(s_ac, P.n(s_ac))])
        ve_p5 = P.n(s_ve)
        tap("r2", [r2[:]], [(s_ve, ve_p5)])
        ctx_p5.close()

        P.phase = "p6"
        for c in range(NCH):
            P.dma("gpsimd", d_out[128 * c:128 * c + 128, :], x_res[c][:], s_st,
                  waits=([(s_ve, ve_p5)] if c == 0 else []))
        final_sem, final = s_st, P.n(s_st)

        block = ctx.enter_context(nc.Block())
        for eng_name, deco in [("sync", block.sync), ("tensor", block.tensor),
                               ("scalar", block.scalar), ("vector", block.vector),
                               ("gpsimd", block.gpsimd)]:
            def make(eng_name=eng_name):
                def _(e):
                    P.run(e, eng_name)
                    if eng_name == "gpsimd":
                        e.wait_ge(final_sem, final)
                return _
            deco(make())
    return nc


_NC_CACHE = {}
DEBUG = ()


def kernel(L0_attn_hidden_states, L0_attn_Wqkv, L0_attn_bqkv, L0_attn_Wout, L0_attn_bout,
           L0_attn_Wg, L0_attn_bg, L0_ffn_fast_b, L0_ffn_w_fc, L0_ffn_w_proj, L0_ffn_u,
           L0_ffn_v, L0_ffn_gate):
    f = np.float32
    x = np.ascontiguousarray(np.asarray(L0_attn_hidden_states, dtype=f))
    wqkv = np.ascontiguousarray(np.asarray(L0_attn_Wqkv, dtype=f))
    bqkv = np.asarray(L0_attn_bqkv, dtype=f).reshape(-1)
    wout = np.ascontiguousarray(np.asarray(L0_attn_Wout, dtype=f))
    bout = np.asarray(L0_attn_bout, dtype=f).reshape(-1)
    wgv = np.ascontiguousarray(np.asarray(L0_attn_Wg, dtype=f))
    bg = np.asarray(L0_attn_bg, dtype=f).reshape(1, 1)
    fb = np.asarray(L0_ffn_fast_b, dtype=f)
    wfc = np.ascontiguousarray(np.asarray(L0_ffn_w_fc, dtype=f))
    wproj = np.ascontiguousarray(np.asarray(L0_ffn_w_proj, dtype=f))
    uu = np.asarray(L0_ffn_u, dtype=f)
    vv = np.asarray(L0_ffn_v, dtype=f)
    gate = np.asarray(L0_ffn_gate, dtype=f).reshape(1, 1)

    e16 = np.zeros((128, NCH * 16), dtype=BF)
    for c in range(NCH):
        e16[0:64, c * 16 + 2 * c] = 1
        e16[64:128, c * 16 + 2 * c + 1] = 1

    shared = {
        "wqkv": wqkv,
        "bqkv_pp": np.ascontiguousarray(bqkv.reshape(24, 128).T),
        "bv_row": np.ascontiguousarray(bqkv[2048:3072].reshape(1, D)),
        "wout": wout,
        "bout_pp": np.ascontiguousarray(bout.reshape(8, 128).T),
        "wg": wgv,
        "bg": bg,
        "w_fc": wfc,
        "w_proj": wproj,
        "uT": np.ascontiguousarray(uu.T),
        "vT": np.ascontiguousarray(vv.T),
        "fbT": np.ascontiguousarray(fb.T),
        "gate": gate,
        "ident": np.eye(128, dtype=BF),
        "negi": (np.eye(128) * -1.0e9).astype(BF),
        "i16": np.eye(16, dtype=BF),
        "row1": np.ones((1, 128), dtype=BF),
        "statc": np.full((128, 1), -1.0 / 1024.0, dtype=BF),
        "one11": np.ones((1, 1), dtype=BF),
        "e16": e16,
    }

    in_maps = []
    for core in range(8):
        seq, hh = core // 2, core % 2
        start = hh * TOWN
        rows = np.zeros((EXT, D), dtype=f)
        lo = max(0, start - 256)
        rows[256 - (start - lo):256] = x[seq, lo:start]
        rows[256:256 + TOWN] = x[seq, start:start + TOWN]
        hi = min(T, start + TOWN + 256)
        rows[2304:2304 + (hi - start - TOWN)] = x[seq, start + TOWN:hi]
        rows[2560:2624] = x[seq, 0:NG]
        xe = np.ascontiguousarray(rows.T)

        t01 = np.zeros((NQT, 128, 640), dtype=BF)
        a = np.arange(128)[:, None]
        b = np.arange(640)[None, :]
        for qt in range(NQT):
            band = (b >= a) & (b <= a + 512)
            e_idx = 128 * qt + b
            gk = 2048 * hh + (e_idx - 256)
            valid = (gk >= 0) & (gk < T)
            t01[qt] = (~(band & valid)).astype(BF)

        m = dict(shared)
        m["x_ext"] = xe
        m["masks"] = np.ascontiguousarray(t01)
        in_maps.append(m)

    key = ("nc",) + tuple(sorted(DEBUG))
    if key not in _NC_CACHE:
        _NC_CACHE[key] = build_program(debug=DEBUG)
    nc = _NC_CACHE[key]
    res = run_bass_kernel_spmd(nc, in_maps, core_ids=list(range(8)))
    kernel.last_results = res.results

    out = np.empty((B, T, D), dtype=f)
    for core in range(8):
        seq, hh = core // 2, core % 2
        out[seq, hh * TOWN:(hh + 1) * TOWN] = res.results[core]["out"].T
    return out


# version 06 (final)
# speedup vs baseline: 2.1080x; 1.3113x over previous
import numpy as np
import ml_dtypes
from contextlib import ExitStack

import concourse.bass as bass
import concourse.mybir as mybir
from concourse.bass_utils import run_bass_kernel_spmd

F32 = mybir.dt.float32
BF16 = mybir.dt.bfloat16
BF = ml_dtypes.bfloat16
AX = mybir.AxisListType.X
ALU = mybir.AluOpType
ACT = mybir.ActivationFunctionType

B, T, D, H, HD = 4, 4096, 1024, 16, 64
WLOC, NG = 256, 64
TOWN = 2048
EXT = 2688
NQT = TOWN // 128
NKT = EXT // 128
NCH = 8
HIDN = 4096
NHI = HIDN // 128
RANK = 16

G1N, G1 = 448, 6
GN, G = 512, 4


SAFE_MODE = 1
SAFE_PHASES = None
UNSAFE_PHASES = ("p1a", "p1b", "pv", "pg", "p5", "p2b_s", "p2b_e", "p2b_t",
                 "p2b_v", "p2b_pv", "p2b_av")
DMA_FINE = True
DMA_FINE_PHASES = None


class Plan:
    def __init__(self):
        self.ops = []
        self.count = {}
        self.safe = SAFE_MODE
        self.phase = "p0"
        self.eng_sem = {}
        self.sems = []
        self.joined = {}

    def n(self, sem):
        return self.count.get(id(sem), 0)

    def add(self, eng, fn, waits=(), incs=(), dma=False):
        waits = list(waits)
        incs = list(incs)
        for s, v in waits:
            assert v <= self.n(s), f"wait for future sem value {v} > {self.n(s)}"
        base = self.phase.split("_")[0]
        insafe = (SAFE_PHASES is None or self.phase in SAFE_PHASES or
                  base in SAFE_PHASES) and \
            self.phase not in UNSAFE_PHASES
        if dma and DMA_FINE and (DMA_FINE_PHASES is None or base in DMA_FINE_PHASES
                                 or self.phase in DMA_FINE_PHASES):
            insafe = False
        if self.safe and insafe:
            if not incs and eng in self.eng_sem:
                incs = [(self.eng_sem[eng], 1)]
            j = self.joined.setdefault(eng, {})
            for s in self.sems:
                nv = self.n(s)
                if nv > j.get(id(s), 0):
                    waits.append((s, nv))
                    j[id(s)] = nv
        for s, k in incs:
            self.count[id(s)] = self.n(s) + k
        self.ops.append((eng, fn, tuple(waits), tuple(incs)))

    def dma(self, eng, out, in_, semh, waits=(), slow=False):
        self.add(eng, (lambda e, out=out, in_=in_, slow=slow: e.dma_start(
            out=out, in_=in_, allow_slow_non_contiguous=slow)),
                 waits=waits, incs=[(semh, 16)], dma=True)

    def run(self, e, eng):
        for oeng, fn, waits, incs in self.ops:
            if oeng != eng:
                continue
            for s, v in waits:
                e.wait_ge(s, v)
            inst = fn(e)
            for s, k in incs:
                inst = inst.then_inc(s, k)


def build_program(debug=()):
    debug = set(debug)
    nc = bass.Bass("TRN2", target_bir_lowering=False)

    d_x = nc.dram_tensor("x_ext", [D, EXT], F32, kind="ExternalInput").ap()
    d_wqkv = nc.dram_tensor("wqkv", [D, 3 * D], F32, kind="ExternalInput").ap()
    d_bqkv = nc.dram_tensor("bqkv_pp", [128, 24], F32, kind="ExternalInput").ap()
    d_bvrow = nc.dram_tensor("bv_row", [1, D], F32, kind="ExternalInput").ap()
    d_wout = nc.dram_tensor("wout", [D, D], F32, kind="ExternalInput").ap()
    d_bout = nc.dram_tensor("bout_pp", [128, 8], F32, kind="ExternalInput").ap()
    d_wg = nc.dram_tensor("wg", [D, 1], F32, kind="ExternalInput").ap()
    d_bg = nc.dram_tensor("bg", [1, 1], F32, kind="ExternalInput").ap()
    d_wfc = nc.dram_tensor("w_fc", [HIDN, D], F32, kind="ExternalInput").ap()
    d_wproj = nc.dram_tensor("w_proj", [D, HIDN], F32, kind="ExternalInput").ap()
    d_uT = nc.dram_tensor("uT", [RANK, HIDN], F32, kind="ExternalInput").ap()
    d_vT = nc.dram_tensor("vT", [D, RANK], F32, kind="ExternalInput").ap()
    d_fbT = nc.dram_tensor("fbT", [RANK, RANK], F32, kind="ExternalInput").ap()
    d_gate = nc.dram_tensor("gate", [1, 1], F32, kind="ExternalInput").ap()
    d_masks = nc.dram_tensor("masks", [NQT, 128, 640], BF16, kind="ExternalInput").ap()
    d_ident = nc.dram_tensor("ident", [128, 128], BF16, kind="ExternalInput").ap()
    d_negi = nc.dram_tensor("negi", [128, 128], BF16, kind="ExternalInput").ap()
    d_e16 = nc.dram_tensor("e16", [128, NCH * 16], BF16, kind="ExternalInput").ap()
    d_i16 = nc.dram_tensor("i16", [16, 16], BF16, kind="ExternalInput").ap()
    d_row1 = nc.dram_tensor("row1", [1, 128], BF16, kind="ExternalInput").ap()
    d_statc = nc.dram_tensor("statc", [128, 1], BF16, kind="ExternalInput").ap()
    d_one11 = nc.dram_tensor("one11", [1, 1], BF16, kind="ExternalInput").ap()

    d_out = nc.dram_tensor("out", [D, TOWN], F32, kind="ExternalOutput").ap()
    d_tproj = nc.dram_tensor("tproj_spill", [NHI, 128, D], BF16).ap()
    d_tfcs = nc.dram_tensor("tfc_spill", [NCH, 128, HIDN], BF16).ap()

    P = Plan()
    dbg_outs = {}

    def tap(name, src_aps, waits):
        if name not in debug:
            return
        p0, f0 = src_aps[0].shape[-2], src_aps[0].shape[-1]
        dt = src_aps[0].dtype
        dten = nc.dram_tensor("dbg_" + name, [len(src_aps), p0, f0], dt,
                              kind="ExternalOutput").ap()
        dbg_outs[name] = True
        for i, ap in enumerate(src_aps):
            P.dma("gpsimd", dten[i], ap, s_st, waits=(waits if i == 0 else []))

    ctx = ExitStack()
    with ctx:
        def mksb(stack, name, shape, dt=F32, side="left"):
            return stack.enter_context(nc.sbuf_tensor("sb_" + name, shape, dt, side=side))

        def sb(name, shape, dt=F32, side="left"):
            return mksb(ctx, name, shape, dt, side)

        def sem(name):
            return ctx.enter_context(nc.semaphore(name))

        s_ld = sem("s_ld"); s_x = sem("s_x"); s_w = sem("s_w"); s_mk = sem("s_mk")
        s_st = sem("s_st"); s_te = sem("s_te"); s_ve = sem("s_ve"); s_ac = sem("s_ac")
        s_gp = sem("s_gp")
        sxp = [sem("sxp0"), sem("sxp1")]
        swq = [sem("swq0"), sem("swq1")]
        swv = [sem("swv0"), sem("swv1")]
        swo = [sem("swo0"), sem("swo1")]
        swt = [sem("swt0"), sem("swt1")]
        sfc2 = [sem("sfc0"), sem("sfc1")]
        spr = [sem("spr0"), sem("spr1")]
        ssp = [sem("ssp0"), sem("ssp1")]
        smk2 = [sem("smk0"), sem("smk1")]
        s_u = sem("s_u")
        P.eng_sem = {"tensor": s_te, "vector": s_ve, "scalar": s_ac, "gpsimd": s_gp}
        P.sems = [s_ld, s_x, s_w, s_mk, s_st, s_te, s_ve, s_ac, s_gp, s_u] + \
            sxp + swq + swv + swo + swt + sfc2 + spr + ssp + smk2

        psA = ctx.enter_context(nc.psum_tensor("psA", [128, 1024], F32))
        psB = ctx.enter_context(nc.psum_tensor("psB", [128, 1024], F32))
        psC = ctx.enter_context(nc.psum_tensor("psC", [128, 1024], F32))
        psD = ctx.enter_context(nc.psum_tensor("psD", [128, 1024], F32))

        ident = sb("ident", [128, 128], BF16)
        i16 = sb("i16", [16, 16], BF16)
        row1 = sb("row1", [1, 128], BF16)
        statc = sb("statc", [128, 1], BF16)
        one11 = sb("one11", [1, 1], BF16)
        bqkv = sb("bqkv", [128, 24], F32)
        bvrow_bf = sb("bvrow_bf", [1, D], BF16)
        bout = sb("bout", [128, 8], F32)
        wg = sb("wg", [128, 8], F32)
        wg_bf = sb("wg_bf", [128, 8], BF16)
        bg11 = sb("bg11", [1, 1], F32)
        bg_bf = sb("bg_bf", [1, 1], BF16)
        bg128 = sb("bg128", [128, 1], F32)
        gate11 = sb("gate11", [1, 1], F32)
        gate_bf = sb("gate_bf", [1, 1], BF16)
        g16 = sb("g16", [16, 1], F32)
        vT_f = sb("vT_f", [128, NCH * 16], F32)
        vT_bf = sb("vT_bf", [128, NCH * 16], BF16)
        fbT_f = sb("fbT_f", [16, 16], F32)
        fbTg = sb("fbTg", [16, 16], BF16)
        gcol = sb("gcol", [128, NQT], F32)
        gm1col = sb("gm1col", [128, NQT], F32)
        negm8 = sb("negm8", [128, 1], F32)
        nm8bf = sb("nm8bf", [1, 1], BF16)
        rowbuf = sb("rowbuf", [1, 2 * GN], F32)
        rowbuf2 = sb("rowbuf2", [1, GN], F32)
        rown_bf = sb("rown_bf", [1, GN], BF16)
        rowr_bf = sb("rowr_bf", [1, GN], BF16)
        rb_sb = sb("rb_sb", [128, GN], F32)
        lnx = sb("lnx", [128, NCH * GN], BF16)
        lnx2 = sb("lnx2", [128, 2 * GN], BF16)
        lnt = sb("lnt", [128, GN], BF16)
        ltile = sb("ltile", [128, 16], F32)
        ltecho = sb("ltecho", [128, 16], F32)
        lgtile = sb("lgtile", [128, 16], F32)
        rltile = sb("rltile", [128, 16], F32)
        rlgtile = sb("rlgtile", [128, 16], F32)
        bndrow = sb("bndrow", [1, 16], F32)
        bnd16 = sb("bnd16", [16, 4], F32)
        bnd16b = sb("bnd16b", [16, 1], BF16)
        bndsc = sb("bndsc", [16, 12], F32)

        ctx_qkv = ExitStack()
        q_fm = [mksb(ctx_qkv, f"q{c}", [128, TOWN], BF16) for c in range(NCH)]
        k_fm = [mksb(ctx_qkv, f"k{c}", [128, EXT], BF16) for c in range(NCH)]
        v_tok = [mksb(ctx_qkv, f"v{t}", [128, D], BF16) for t in range(NKT)]

        for o, i in [(ident, d_ident), (i16, d_i16), (row1, d_row1),
                     (statc, d_statc), (one11, d_one11), (bqkv, d_bqkv),
                     (bout, d_bout), (bg11, d_bg), (gate11, d_gate), (fbT_f, d_fbT)]:
            P.dma("sync", o[:], i, s_ld)
        P.dma("sync", wg.ap().rearrange("p (c o) -> p c o", o=1),
              d_wg.rearrange("(c p) o -> p c o", p=128), s_ld, slow=True)
        P.dma("sync", vT_f.ap().rearrange("p (c n) -> p c n", c=NCH),
              d_vT.rearrange("(c p) n -> p c n", p=128), s_ld)
        P.dma("sync", rowbuf[0:1, 0:D], d_bvrow, s_ld)
        n_ld = P.n(s_ld)

        P.add("vector", lambda e: e.tensor_copy(bvrow_bf[:], rowbuf[0:1, 0:D]),
              waits=[(s_ld, n_ld)])
        P.add("vector", lambda e: e.tensor_copy(wg_bf[:], wg[:]))
        P.add("vector", lambda e: e.tensor_copy(vT_bf[:], vT_f[:]))
        P.add("vector", lambda e: e.tensor_copy(gate_bf[:], gate11[:]))
        P.add("vector", lambda e: e.tensor_copy(bg_bf[:], bg11[:]), incs=[(s_ve, 1)])
        ve0 = P.n(s_ve)
        P.add("tensor", lambda e: e.matmul(psD[0:16, 0:1], row1[:, 0:16], gate_bf[:],
                                           start=True, stop=True),
              waits=[(s_ve, ve0)], incs=[(s_te, 1)])
        P.add("vector", lambda e: e.tensor_copy(g16[:], psD[0:16, 0:1]),
              waits=[(s_te, P.n(s_te))])
        P.add("vector", lambda e: e.tensor_scalar(out=fbTg[:], in0=fbT_f[:], scalar1=g16[:],
                                                  scalar2=None, op0=ALU.mult), incs=[(s_ve, 1)])
        P.add("tensor", lambda e: e.matmul(psD[:, 1:2], row1[:], bg_bf[:], start=True, stop=True),
              waits=[(s_ve, P.n(s_ve))], incs=[(s_te, 1)])
        P.add("vector", lambda e: e.tensor_copy(bg128[:], psD[:, 1:2]),
              waits=[(s_te, P.n(s_te))], incs=[(s_ve, 1)])
        ve_consts = P.n(s_ve)

        def emit_ln(gn, get_src, put, w_first=(), cast_hook=None, post_cast=None):
            te_sq = {}
            for c in range(NCH):
                w_c = list(w_first) if c == 0 else []
                if cast_hook is not None:
                    w_c = w_c + cast_hook(c)
                sq = lnx2[:, (c % 2) * gn:(c % 2) * gn + gn]
                if c >= 2:
                    w_c.append((s_te, te_sq[c - 2]))
                P.add("vector",
                      (lambda e, c=c: e.tensor_copy(lnx[:, c * gn:(c + 1) * gn], get_src(c))),
                      waits=w_c)
                P.add("vector",
                      (lambda e, c=c, sq=sq: e.tensor_mul(sq, lnx[:, c * gn:(c + 1) * gn],
                                                          lnx[:, c * gn:(c + 1) * gn])),
                      incs=[(s_ve, 1)])
                ve_c = P.n(s_ve)
                if post_cast is not None:
                    post_cast(c)
                P.add("tensor",
                      (lambda e, c=c: e.matmul(psD[0:1, 0:gn], statc[:],
                                               lnx[:, c * gn:(c + 1) * gn],
                                               start=(c == 0), stop=(c == NCH - 1))),
                      waits=[(s_ve, ve_c)])
                P.add("tensor",
                      (lambda e, c=c, sq=sq: e.matmul(psD[0:1, 512:512 + gn], statc[:], sq,
                                               start=(c == 0), stop=(c == NCH - 1))),
                      incs=[(s_te, 1)])
                te_sq[c] = P.n(s_te)
            te_stats = P.n(s_te)
            P.add("vector", lambda e: e.tensor_copy(rowbuf[:, 0:gn], psD[0:1, 0:gn]),
                  waits=[(s_te, te_stats)])
            P.add("vector", lambda e: e.tensor_copy(rowbuf[:, gn:2 * gn],
                                                    psD[0:1, 512:512 + gn]))
            P.add("vector", lambda e: e.tensor_mul(rowbuf2[:, 0:gn], rowbuf[:, 0:gn],
                                                   rowbuf[:, 0:gn]))
            P.add("vector", lambda e: e.scalar_tensor_tensor(
                out=rowbuf2[:, 0:gn], in0=rowbuf[:, gn:2 * gn], scalar=-1.0,
                in1=rowbuf2[:, 0:gn], op0=ALU.mult, op1=ALU.subtract))
            P.add("vector", lambda e: e.tensor_copy(rown_bf[:, 0:gn], rowbuf[:, 0:gn]),
                  incs=[(s_ve, 1)])
            ve_rows = P.n(s_ve)
            P.add("scalar", lambda e: e.activation(out=rowbuf2[:, 0:gn], in_=rowbuf2[:, 0:gn],
                                                   func=ACT.Sqrt, bias=np.float32(1e-5),
                                                   scale=1.0),
                  waits=[(s_ve, ve_rows)], incs=[(s_ac, 1)])
            P.add("vector", lambda e: e.reciprocal(rowbuf2[:, 0:gn], rowbuf2[:, 0:gn]),
                  waits=[(s_ac, P.n(s_ac))])
            P.add("vector", lambda e: e.tensor_copy(rowr_bf[:, 0:gn], rowbuf2[:, 0:gn]),
                  incs=[(s_ve, 1)])
            ve_r = P.n(s_ve)
            P.add("tensor", lambda e: e.matmul(psC[:, 0:gn], row1[:], rowr_bf[:, 0:gn],
                                               start=True, stop=True),
                  waits=[(s_ve, ve_r)], incs=[(s_te, 1)])
            P.add("vector", lambda e: e.tensor_copy(rb_sb[:, 0:gn], psC[:, 0:gn]),
                  waits=[(s_te, P.n(s_te))], incs=[(s_ve, 1)])
            ve_rb = P.n(s_ve)
            for c in range(NCH):
                reg = psA[:, 0:gn] if c % 2 == 0 else psB[:, 0:gn]
                w_reg = [(s_ve, P.n(s_ve))] if c >= 2 else [(s_ve, ve_rb)]
                P.add("tensor", (lambda e, reg=reg: e.matmul(reg, row1[:], rown_bf[:, 0:gn],
                                                             start=True, stop=False)),
                      waits=w_reg)
                P.add("tensor", (lambda e, c=c, reg=reg: e.matmul(
                    reg, ident[:], lnx[:, c * gn:(c + 1) * gn], start=False, stop=True)),
                      incs=[(s_te, 1)])
                put(c, reg, P.n(s_te))

        P.phase = "p1a"
        ctx_p1 = ExitStack()
        xn = [mksb(ctx_p1, f"xn{c}", [128, EXT], BF16, side="right") for c in range(NCH)]
        ctx_p1a = ExitStack()
        xstg = [mksb(ctx_p1a, f"xstg{b}", [128, G1N], F32, side="right") for b in range(2)]

        x_view = d_x.rearrange("(cc p) n -> cc p n", p=128)
        cast_ve = {}

        for g in range(G1):
            x_lds = {}

            def plan_xdma(g, c):
                prev = (g, c - 2) if c >= 2 else ((g - 1, NCH - 2 + c) if g > 0 else None)
                if prev is not None and prev in cast_ve:
                    w = [(s_ve, cast_ve[prev])]
                else:
                    w = [(s_ve, ve_consts)]
                P.dma("sync", xstg[c % 2][:],
                      x_view[c, :, g * G1N:(g + 1) * G1N], sxp[c % 2], waits=w)
                x_lds[(g, c)] = (sxp[c % 2], P.n(sxp[c % 2]))

            plan_xdma(g, 0)
            plan_xdma(g, 1)

            def cast_hook(c, g=g):
                return [x_lds[(g, c)]]

            def post_cast(c, g=g):
                cast_ve[(g, c)] = P.n(s_ve)
                if c + 2 < NCH:
                    plan_xdma(g, c + 2)

            def put_xn(c, reg, te_val, g=g):
                P.add("vector", (lambda e, c=c, g=g, reg=reg: e.tensor_mul(
                    xn[c][:, g * G1N:(g + 1) * G1N], reg, rb_sb[:, 0:G1N])),
                      waits=[(s_te, te_val)], incs=[(s_ve, 1)])

            emit_ln(G1N, (lambda c: xstg[c % 2][:]), put_xn, cast_hook=cast_hook,
                    post_cast=post_cast)
        ve_ln1 = P.n(s_ve)
        tap("xn", [xn[c][:] for c in range(NCH)], [(s_ve, ve_ln1)])
        ctx_p1a.close()

        P.phase = "p1b"
        ctx_qk = ExitStack()
        wstg = [mksb(ctx_qk, f"wstg{b}", [128, 1024], F32, side="right") for b in range(2)]
        wbfq = [mksb(ctx_qk, f"wbfq{b}", [128, 1024], BF16, side="right") for b in range(2)]

        te_lastmm = {0: None, 1: None}
        cast_done_w = {0: None, 1: None}
        ridx = 0
        ac_ev = [None, None]
        for fo in range(16):
            par = fo % 2
            stg, wbf = wstg[par], wbfq[par]
            w_w = [(s_ve, cast_done_w[par])] if cast_done_w[par] is not None else \
                [(s_ve, ve_ln1)]
            for fi in range(NCH):
                P.dma("sync", stg[:, fi * 128:(fi + 1) * 128],
                      d_wqkv[128 * fi:128 * fi + 128, 128 * fo:128 * fo + 128], swq[par],
                      waits=(w_w if fi == 0 else []))
            w_ld = P.n(swq[par])
            w_cast = [(swq[par], w_ld)]
            if te_lastmm[par] is not None:
                w_cast.append((s_te, te_lastmm[par]))
            P.add("vector", (lambda e, stg=stg, wbf=wbf: e.tensor_copy(wbf[:], stg[:])),
                  waits=w_cast, incs=[(s_ve, 1)])
            ve_cast = P.n(s_ve)
            cast_done_w[par] = ve_cast
            is_q = fo < 8
            ngrp = 4 if is_q else G1
            gn = 512 if is_q else G1N
            for g in range(ngrp):
                reg = psA[:, 0:gn] if ridx % 2 == 0 else psB[:, 0:gn]
                w_mm = [(s_ve, ve_cast)]
                if ac_ev[ridx % 2] is not None:
                    w_mm.append((s_ac, ac_ev[ridx % 2]))
                for fi in range(NCH):
                    if is_q:
                        mk_mm = (lambda e, fi=fi, g=g, wbf=wbf, reg=reg: e.matmul(
                            reg, wbf[:, fi * 128:(fi + 1) * 128],
                            xn[fi][:, 256 + g * 512:256 + (g + 1) * 512],
                            start=(fi == 0), stop=(fi == NCH - 1)))
                    else:
                        mk_mm = (lambda e, fi=fi, g=g, wbf=wbf, reg=reg: e.matmul(
                            reg, wbf[:, fi * 128:(fi + 1) * 128],
                            xn[fi][:, g * G1N:(g + 1) * G1N],
                            start=(fi == 0), stop=(fi == NCH - 1)))
                    P.add("tensor", mk_mm, waits=(w_mm if fi == 0 else []),
                          incs=([(s_te, 1)] if fi == NCH - 1 else []))
                te_mm = P.n(s_te)
                te_lastmm[par] = te_mm
                if is_q:
                    dst = (lambda e, fo=fo, g=g, reg=reg: e.activation(
                        out=q_fm[fo][:, g * 512:(g + 1) * 512], in_=reg, func=ACT.Identity,
                        bias=bqkv[:, fo:fo + 1], scale=1.0))
                else:
                    dst = (lambda e, fo=fo, g=g, reg=reg: e.activation(
                        out=k_fm[fo - 8][:, g * G1N:(g + 1) * G1N], in_=reg, func=ACT.Identity,
                        bias=bqkv[:, fo:fo + 1], scale=1.0))
                P.add("scalar", dst, waits=[(s_te, te_mm)], incs=[(s_ac, 1)])
                ac_ev[ridx % 2] = P.n(s_ac)
                ridx += 1
        ctx_qk.close()

        P.phase = "pv"
        ctx_v = ExitStack()
        wvstg = [mksb(ctx_v, f"wvstg{b}", [128, 512], F32, side="right") for b in range(2)]
        wvbf = mksb(ctx_v, "wvbf", [128, NCH * 512], BF16, side="right")
        te_qk_done = P.n(s_te)
        for vh in range(2):
            for fi in range(NCH):
                stg = wvstg[fi % 2]
                w_w = [(s_ve, P.n(s_ve))] if (vh * NCH + fi) >= 2 else \
                    [(s_te, te_qk_done)]
                P.dma("sync", stg[:],
                      d_wqkv[128 * fi:128 * fi + 128, 2048 + vh * 512:2048 + (vh + 1) * 512],
                      swv[fi % 2], waits=w_w)
                w_c = [(swv[fi % 2], P.n(swv[fi % 2]))]
                if vh == 1 and fi == 0:
                    w_c.append((s_te, P.n(s_te)))
                P.add("vector", (lambda e, stg=stg, fi=fi: e.tensor_copy(
                    wvbf[:, fi * 512:(fi + 1) * 512], stg[:])),
                      waits=w_c, incs=[(s_ve, 1)])
            ve_wv = P.n(s_ve)
            for t in range(NKT):
                reg = psA[:, 0:512] if ridx % 2 == 0 else psB[:, 0:512]
                w_mm = [(s_ve, ve_wv)]
                if ac_ev[ridx % 2] is not None:
                    w_mm.append((s_ac, ac_ev[ridx % 2]))
                P.add("tensor", (lambda e, reg=reg, vh=vh: e.matmul(
                    reg, row1[:], bvrow_bf[:, vh * 512:(vh + 1) * 512],
                    start=True, stop=False)), waits=w_mm)
                for fi in range(NCH):
                    P.add("tensor", (lambda e, fi=fi, t=t, reg=reg: e.matmul(
                        reg, xn[fi][:, t * 128:(t + 1) * 128], wvbf[:, fi * 512:(fi + 1) * 512],
                        start=False, stop=(fi == NCH - 1))),
                          incs=([(s_te, 1)] if fi == NCH - 1 else []))
                te_mm = P.n(s_te)
                P.add("scalar", (lambda e, t=t, vh=vh, reg=reg: e.activation(
                    out=v_tok[t][:, vh * 512:(vh + 1) * 512], in_=reg, func=ACT.Copy)),
                      waits=[(s_te, te_mm)], incs=[(s_ac, 1)])
                ac_ev[ridx % 2] = P.n(s_ac)
                ridx += 1
        ctx_v.close()

        ctx_g = ExitStack()
        P.phase = "pg"
        zrow_bf = mksb(ctx_g, "zrow_bf", [1, TOWN], BF16, side="right")
        ve_zg = None
        for g in range(4):
            w_z = [(s_ve, ve_zg)] if ve_zg is not None else [(s_ve, P.n(s_ve))]
            for fi in range(NCH):
                P.add("tensor", (lambda e, fi=fi, g=g: e.matmul(
                    psD[0:1, 0:512], wg_bf[:, fi:fi + 1],
                    xn[fi][:, 256 + g * 512:256 + (g + 1) * 512],
                    start=(fi == 0), stop=(fi == NCH - 1))),
                      waits=(w_z if fi == 0 else []),
                      incs=([(s_te, 1)] if fi == NCH - 1 else []))
            te_z = P.n(s_te)
            P.add("vector", (lambda e: e.tensor_copy(rowbuf[:, 0:512], psD[0:1, 0:512])),
                  waits=[(s_te, te_z)])
            P.add("vector", (lambda e, g=g: e.tensor_copy(
                zrow_bf[:, g * 512:(g + 1) * 512], rowbuf[:, 0:512])), incs=[(s_ve, 1)])
            ve_zg = P.n(s_ve)
        for t in range(NQT):
            P.add("tensor", (lambda e, t=t: e.matmul(
                psC[:, t:t + 1], zrow_bf[:, t * 128:(t + 1) * 128], one11[:],
                start=True, stop=(t == NQT - 1))),
                  waits=([(s_ve, ve_zg)] if t == 0 else []),
                  incs=([(s_te, 1)] if t == NQT - 1 else []))
        te_zt = P.n(s_te)
        P.add("scalar", lambda e: e.activation(out=gcol[:], in_=psC[:, 0:16], func=ACT.Sigmoid,
                                               bias=bg128[:], scale=1.0),
              waits=[(s_te, te_zt)], incs=[(s_ac, 1)])
        P.add("vector", lambda e: e.tensor_scalar(out=gm1col[:], in0=gcol[:], scalar1=-1.0,
                                                  scalar2=1.0, op0=ALU.mult, op1=ALU.add),
              waits=[(s_ac, P.n(s_ac))], incs=[(s_ve, 1)])
        ve_gate = P.n(s_ve)
        tap("q", [q_fm[c][:] for c in range(NCH)], [(s_ac, P.n(s_ac))])
        tap("k", [k_fm[c][:] for c in range(NCH)], [(s_ac, P.n(s_ac))])
        tap("v", [v_tok[t][:] for t in range(NKT)], [(s_ac, P.n(s_ac))])
        tap("gate", [gcol[:], gm1col[:]], [(s_ve, ve_gate)])

        P.phase = "p2a"
        e16s = mksb(ctx_g, "e16s", [128, NCH * 16], BF16, side="right")
        q2b = [mksb(ctx_g, f"q2b{b}", [128, 512], BF16, side="right") for b in range(2)]
        P.dma("sync", e16s[:], d_e16, s_ld, waits=[(s_te, P.n(s_te))])
        n_ld2 = P.n(s_ld)

        def norms(buf_fm, ncols, outcol0):
            njg = (ncols + 511) // 512
            for jg in range(njg):
                j0 = jg * 512
                n = min(512, ncols - j0)
                for c in range(NCH):
                    b = q2b[c % 2]
                    w_sq = [(s_te, P.n(s_te))] if (jg or outcol0 or c >= 2) \
                        else [(s_ac, P.n(s_ac)), (s_ld, n_ld2)]
                    P.add("vector", (lambda e, c=c, b=b, j0=j0, n=n, buf_fm=buf_fm: e.tensor_mul(
                        b[:, 0:n], buf_fm[c][:, j0:j0 + n], buf_fm[c][:, j0:j0 + n])),
                          waits=w_sq, incs=[(s_ve, 1)])
                    ve_sq = P.n(s_ve)
                    P.add("tensor", (lambda e, c=c, b=b, n=n: e.matmul(
                        psA[0:16, 0:n], e16s[:, c * 16:(c + 1) * 16], b[:, 0:n],
                        start=(c == 0), stop=(c == NCH - 1))),
                          waits=[(s_ve, ve_sq)], incs=[(s_te, 1)])
                te_n = P.n(s_te)
                P.add("vector", (lambda e, jg=jg, n=n, outcol0=outcol0: e.reduce_max(
                    bndsc[:, outcol0 + jg:outcol0 + jg + 1], psA[0:16, 0:n], axis=AX)),
                      waits=[(s_te, te_n)], incs=[(s_ve, 1)])
            return njg

        nq = norms(q_fm, TOWN, 0)
        nk = norms(k_fm, 2624, nq)
        P.add("vector", lambda e, nq=nq: e.reduce_max(bnd16[:, 0:1], bndsc[:, 0:nq], axis=AX))
        P.add("vector", lambda e, nq=nq, nk=nk: e.reduce_max(bnd16[:, 1:2],
                                                             bndsc[:, nq:nq + nk], axis=AX))
        P.add("vector", lambda e: e.tensor_mul(bnd16[:, 2:3], bnd16[:, 0:1], bnd16[:, 1:2]))
        P.add("vector", lambda e: e.tensor_copy(bnd16b[:], bnd16[:, 2:3]), incs=[(s_ve, 1)])
        P.add("tensor", lambda e: e.matmul(psD[0:1, 0:16], bnd16b[:], i16[:],
                                           start=True, stop=True),
              waits=[(s_ve, P.n(s_ve))], incs=[(s_te, 1)])
        P.add("vector", lambda e: e.tensor_copy(bndrow[:], psD[0:1, 0:16]),
              waits=[(s_te, P.n(s_te))])
        P.add("vector", lambda e: e.reduce_max(bnd16[0:1, 3:4], bndrow[:], axis=AX),
              incs=[(s_ve, 1)])
        P.add("scalar", lambda e: e.activation(out=bndrow[:, 1:2], in_=bnd16[0:1, 3:4],
                                               func=ACT.Sqrt, bias=np.float32(1e-6), scale=1.0),
              waits=[(s_ve, P.n(s_ve))], incs=[(s_ac, 1)])
        P.add("vector", lambda e: e.tensor_scalar(out=bndrow[:, 2:3], in0=bndrow[:, 1:2],
                                                  scalar1=-0.125, scalar2=-0.25,
                                                  op0=ALU.mult, op1=ALU.add),
              waits=[(s_ac, P.n(s_ac))])
        P.add("vector", lambda e: e.tensor_copy(nm8bf[:], bndrow[:, 2:3]), incs=[(s_ve, 1)])
        P.add("tensor", lambda e: e.matmul(psD[:, 2:3], row1[:], nm8bf[:], start=True, stop=True),
              waits=[(s_ve, P.n(s_ve))], incs=[(s_te, 1)])
        P.add("vector", lambda e: e.tensor_copy(negm8[:], psD[:, 2:3]),
              waits=[(s_te, P.n(s_te))], incs=[(s_ve, 1)])
        ve_negm8 = P.n(s_ve)
        tap("negm8", [negm8[:]], [(s_ve, ve_negm8)])
        tap("bndsc", [bndsc[:]], [(s_ve, ve_negm8)])
        tap("bndrow", [bndrow[:]], [(s_ve, ve_negm8)])

        ctx_g.close()
        ctx_p1.close()

        P.phase = "p2b"
        ctx_p2 = ExitStack()
        attnfm = mksb(ctx_p2, "attnfm", [128, NCH * TOWN], BF16, side="right")
        mks = [mksb(ctx_p2, f"mks{b}", [128, 640], BF16, side="right") for b in range(2)]
        pbuf = [mksb(ctx_p2, f"pbuf{b}", [128, 704], BF16, side="right") for b in range(2)]
        ptn = mksb(ctx_p2, "ptn", [128, 768], BF16, side="right")
        negi = mksb(ctx_p2, "negi", [128, 128], BF16, side="right")
        gdiag = mksb(ctx_p2, "gdiag", [128, 128], BF16, side="right")
        ggdiag = mksb(ctx_p2, "ggdiag", [128, 128], BF16, side="right")
        dlb = [mksb(ctx_p2, f"dlb{i}", [128, 128], BF16, side="right") for i in range(2)]
        dgb = [mksb(ctx_p2, f"dgb{i}", [128, 128], BF16, side="right") for i in range(2)]
        P.dma("sync", negi[:], d_negi, s_ld, waits=[(s_te, P.n(s_te))])
        n_ld3 = P.n(s_ld)

        te_T = [None, None]
        ve_av = [None, None]
        ac_exp_last = None
        te_mask_last = [None, None]
        ev_pt_prev = None

        attn_view = attnfm.ap().rearrange("p (c t) -> p c t", c=NCH)
        psd_view = psD.ap().rearrange("p (c t) -> p c t", c=NCH)

        for qt in range(NQT):
            mk = mks[qt % 2]
            if te_mask_last[qt % 2] is not None:
                w_mk = [(s_te, te_mask_last[qt % 2])]
            elif qt == 0:
                w_mk = [(s_te, P.n(s_te))]
            else:
                w_mk = []
            P.dma("sync", mk[:], d_masks[qt], smk2[qt % 2], waits=w_mk)
            mk_ld = (smk2[qt % 2], P.n(smk2[qt % 2]))

            w0 = [(s_ve, max(ve_negm8, ve_gate))] if qt == 0 else [(s_ac, ac_exp_last)]
            P.add("vector", lambda e: e.memset(ltile[:], 0.0), waits=w0)
            P.add("vector", lambda e: e.memset(lgtile[:], 0.0))
            P.add("vector", (lambda e, qt=qt: e.tensor_scalar(
                out=gdiag[:], in0=ident[:], scalar1=gm1col[:, qt:qt + 1],
                scalar2=None, op0=ALU.mult)))
            P.add("vector", (lambda e, qt=qt: e.tensor_scalar(
                out=ggdiag[:], in0=ident[:], scalar1=gcol[:, qt:qt + 1],
                scalar2=None, op0=ALU.mult)), incs=[(s_ve, 1)])
            ve_qt = P.n(s_ve)

            for pr in range(8):
                c = pr
                qq, kk = q_fm[c], k_fm[c]
                j0 = qt * 128
                P.phase = "p2b_s"
                w_s = [(s_ac, ac_exp_last)] if ac_exp_last is not None else [(s_ld, n_ld3)]
                P.add("tensor", (lambda e, qq=qq, kk=kk, j0=j0: e.matmul(
                    psA[:, 0:512], qq[0:64, j0:j0 + 128], kk[0:64, j0:j0 + 512],
                    start=True, stop=False, tile_position=(0, 0))), waits=w_s)
                P.add("tensor", (lambda e, qq=qq, kk=kk, j0=j0: e.matmul(
                    psB[:, 0:512], qq[64:128, j0:j0 + 128], kk[64:128, j0:j0 + 512],
                    start=True, stop=False, tile_position=(64, 0))))
                P.add("tensor", (lambda e, qq=qq, kk=kk, j0=j0: e.matmul(
                    psA[:, 640:704], qq[0:64, j0:j0 + 128], kk[0:64, 2560:2624],
                    start=True, stop=False, tile_position=(0, 0))))
                P.add("tensor", (lambda e, qq=qq, kk=kk, j0=j0: e.matmul(
                    psB[:, 640:704], qq[64:128, j0:j0 + 128], kk[64:128, 2560:2624],
                    start=True, stop=False, tile_position=(64, 0))))
                P.add("tensor", (lambda e, qq=qq, kk=kk, j0=j0: e.matmul(
                    psA[:, 512:640], qq[0:64, j0:j0 + 128], kk[0:64, j0 + 512:j0 + 640],
                    start=True, stop=False, tile_position=(0, 0))))
                P.add("tensor", (lambda e, qq=qq, kk=kk, j0=j0: e.matmul(
                    psB[:, 512:640], qq[64:128, j0:j0 + 128], kk[64:128, j0 + 512:j0 + 640],
                    start=True, stop=False, tile_position=(64, 0))))
                P.add("tensor", (lambda e, mk=mk: e.matmul(
                    psA[:, 0:512], negi[:], mk[:, 0:512], start=False, stop=False)),
                      waits=[mk_ld])
                P.add("tensor", (lambda e, mk=mk: e.matmul(
                    psA[:, 512:640], negi[:], mk[:, 512:640], start=False, stop=True)),
                      incs=[(s_te, 1)])
                te_s = [P.n(s_te), None]
                P.add("tensor", (lambda e, mk=mk: e.matmul(
                    psB[:, 0:512], negi[:], mk[:, 0:512], start=False, stop=False)))
                P.add("tensor", (lambda e, mk=mk: e.matmul(
                    psB[:, 512:640], negi[:], mk[:, 512:640], start=False, stop=True)),
                      incs=[(s_te, 1)])
                te_s[1] = P.n(s_te)
                te_mask_last[qt % 2] = te_s[1]

                for hh in range(2):
                    h = 2 * pr + hh
                    ps = psA if hh == 0 else psB
                    pb = pbuf[hh]
                    P.phase = "p2b_e"
                    w_e = [(s_te, te_s[hh]), (s_ve, ve_qt)]
                    if te_T[hh] is not None:
                        w_e.append((s_te, te_T[hh]))
                    P.add("scalar", (lambda e, ps=ps, pb=pb, h=h: e.activation(
                        out=pb[:], in_=ps[:, 0:704], func=ACT.Exp, bias=negm8[:],
                        scale=0.125)),
                          waits=w_e, incs=[(s_ac, 1)])
                    ac_exp = P.n(s_ac)
                    ac_exp_last = ac_exp
                    P.phase = "p2b_d"
                    P.add("vector", (lambda e, pb=pb, h=h: e.tensor_scalar(
                        out=pb[:, 0:704], in0=pb[:, 0:704], scalar1=1.0, scalar2=0.0,
                        op0=ALU.mult, op1=ALU.add, accum_out=ltile[:, h:h + 1])),
                          waits=[(s_ac, ac_exp)])
                    P.add("vector", (lambda e, pb=pb, h=h: e.reduce_sum(
                        lgtile[:, h:h + 1], pb[:, 640:704], axis=AX)))
                    P.add("vector", (lambda e, h=h: e.tensor_sub(
                        ltile[:, h:h + 1], ltile[:, h:h + 1], lgtile[:, h:h + 1])))
                    P.add("vector", (lambda e, h=h: e.reciprocal(
                        rltile[:, h:h + 1], ltile[:, h:h + 1])))
                    P.add("vector", (lambda e, h=h: e.reciprocal(
                        rlgtile[:, h:h + 1], lgtile[:, h:h + 1])))
                    w_d = [(s_te, te_T[hh])] if te_T[hh] is not None else []
                    P.add("vector", (lambda e, h=h, hh=hh: e.tensor_scalar(
                        out=dlb[hh][:], in0=gdiag[:], scalar1=rltile[:, h:h + 1],
                        scalar2=None, op0=ALU.mult)), waits=w_d)
                    P.add("vector", (lambda e, h=h, hh=hh: e.tensor_scalar(
                        out=dgb[hh][:], in0=ggdiag[:], scalar1=rlgtile[:, h:h + 1],
                        scalar2=None, op0=ALU.mult)), incs=[(s_ve, 1)])
                    ve_d = P.n(s_ve)
                    P.phase = "p2b_t"
                    w_t = [(s_ve, ve_d)]
                    if ev_pt_prev is not None:
                        w_t.append(ev_pt_prev)
                    for s in range(5):
                        P.add("tensor", (lambda e, pb=pb, hh=hh, s=s: e.matmul(
                            psC[:, 128 * s:128 * (s + 1)], pb[:, 128 * s:128 * (s + 1)],
                            dlb[hh][:], start=True, stop=(s == 3))),
                              waits=(w_t if s == 0 else []))
                    P.add("tensor", (lambda e, pb=pb, hh=hh: e.matmul(
                        psC[0:64, 640:768], pb[:, 640:704], dgb[hh][:],
                        start=True, stop=True)), incs=[(s_te, 1)])
                    te_T[hh] = P.n(s_te)
                    P.phase = "p2b_v"
                    if qt == 0 and pr == 0 and hh == 0:
                        tap("pb0", [pb[:]], [(s_ac, ac_exp)])
                        tap("lt0", [ltile[:], lgtile[:], rltile[:], rlgtile[:]],
                            [(s_ve, ve_d)])
                        tap("dl0", [dlb[0][:], dgb[0][:]], [(s_ve, ve_d)])
                    if hh == 0:
                        P.add("scalar", (lambda e: e.activation(out=ptn[:], in_=psC[:, 0:768],
                                                                func=ACT.Copy)),
                              waits=[(s_te, te_T[hh])], incs=[(s_ac, 1)])
                        ev_pt = (s_ac, P.n(s_ac))
                        ev_pt_prev = ev_pt
                    else:
                        P.add("vector", (lambda e: e.tensor_copy(ptn[:], psC[:, 0:768])),
                              waits=[(s_te, te_T[hh])], incs=[(s_ve, 1)])
                        ev_pt = (s_ve, P.n(s_ve))
                    ev_pt_prev = ev_pt
                    if qt == 0 and pr == 0 and hh == 0:
                        tap("ptn0", [ptn[:]], [ev_pt])
                    P.phase = "p2b_pv"
                    bank = pr // 4
                    col0 = 512 * bank + 128 * (pr % 4)
                    w_pv = [ev_pt]
                    if pr % 4 == 0 and hh == 0 and ve_av[bank] is not None:
                        w_pv.append((s_ve, ve_av[bank]))
                    for s in range(5):
                        P.add("tensor", (lambda e, s=s, qt=qt, pr=pr, hh=hh, col0=col0: e.matmul(
                            psD[64 * hh:64 * hh + 64, col0:col0 + 128],
                            v_tok[qt + s][:, 128 * pr + 64 * hh:128 * pr + 64 * hh + 64],
                            ptn[:, 128 * s:128 * (s + 1)],
                            start=(s == 0), stop=False,
                            tile_position=(0, 64 * hh))),
                              waits=(w_pv if s == 0 else []))
                    is_last = (pr % 4 == 3 and hh == 1)
                    P.add("tensor", (lambda e, pr=pr, hh=hh, col0=col0: e.matmul(
                        psD[64 * hh:64 * hh + 64, col0:col0 + 128],
                        v_tok[20][0:64, 128 * pr + 64 * hh:128 * pr + 64 * hh + 64],
                        ptn[0:64, 640:768], start=False, stop=True,
                        tile_position=(0, 64 * hh))),
                          incs=([(s_te, 1)] if is_last else []))
                    if is_last:
                        te_pv = P.n(s_te)
                        P.phase = "p2b_av"
                        P.add("vector", (lambda e, qt=qt, bank=bank: e.tensor_copy(
                            attn_view[:, 4 * bank:4 * bank + 4, qt * 128:(qt + 1) * 128],
                            psd_view[:, 4 * bank:4 * bank + 4, 0:128])),
                              waits=[(s_te, te_pv)], incs=[(s_ve, 1)])
                        ve_av[bank] = P.n(s_ve)
        ve_attn = max(v for v in ve_av if v is not None)
        tap("attnfm", [attnfm[:]], [(s_ve, ve_attn)])

        P.phase = "p3"
        ctx_qkv.close()
        x_res = [sb(f"x{c}", [128, TOWN], F32) for c in range(NCH)]

        ctx_p3 = ExitStack()
        woutbf = [mksb(ctx_p3, f"wo{c}", [128, D], BF16, side="right") for c in range(NCH)]
        wostg = [mksb(ctx_p3, f"wostg{b}", [128, D], F32, side="right") for b in range(2)]
        ysb = [mksb(ctx_p3, f"ysb{c}", [128, GN], F32, side="right") for c in range(NCH)]

        for c in range(NCH):
            P.dma("sync", x_res[c][:], d_x[128 * c:128 * c + 128, 256:256 + TOWN], s_x,
                  waits=([(s_ve, ve_attn)] if c == 0 else []))
        x_ld = P.n(s_x)
        for c in range(NCH):
            stg = wostg[c % 2]
            w_w = [(s_ve, P.n(s_ve))] if c >= 2 else []
            P.dma("sync", stg[:], d_wout[128 * c:128 * c + 128, :], swo[c % 2], waits=w_w)
            P.add("vector", (lambda e, c=c, stg=stg: e.tensor_copy(woutbf[c][:], stg[:])),
                  waits=[(swo[c % 2], P.n(swo[c % 2]))], incs=[(s_ve, 1)])
        ve_wo = P.n(s_ve)

        for g in range(G):
            ac_prev = [None, None]
            for fo in range(NCH):
                reg = psA[:, 0:GN] if fo % 2 == 0 else psB[:, 0:GN]
                w_mm = [(s_ve, max(ve_wo, ve_attn))]
                if ac_prev[fo % 2] is not None:
                    w_mm.append((s_ac, ac_prev[fo % 2]))
                elif g > 0:
                    w_mm.append((s_ve, P.n(s_ve)))
                for fi in range(NCH):
                    P.add("tensor", (lambda e, fi=fi, fo=fo, g=g, reg=reg: e.matmul(
                        reg, woutbf[fi][:, 128 * fo:128 * (fo + 1)],
                        attn_view[:, fi, g * GN:(g + 1) * GN],
                        start=(fi == 0), stop=(fi == NCH - 1))),
                          waits=(w_mm if fi == 0 else []),
                          incs=([(s_te, 1)] if fi == NCH - 1 else []))
                te_mm = P.n(s_te)
                P.add("scalar", (lambda e, fo=fo, reg=reg: e.activation(
                    out=ysb[fo][:], in_=reg, func=ACT.Identity, bias=bout[:, fo:fo + 1],
                    scale=1.0)), waits=[(s_te, te_mm)], incs=[(s_ac, 1)])
                ac_prev[fo % 2] = P.n(s_ac)

            def put_res(cc, reg, te_val, g=g):
                P.add("vector", (lambda e, cc=cc, reg=reg: e.tensor_mul(
                    lnt[:, 0:GN], reg, rb_sb[:, 0:GN])),
                      waits=[(s_te, te_val)])
                P.add("vector", (lambda e, cc=cc, g=g: e.tensor_add(
                    x_res[cc][:, g * GN:(g + 1) * GN], x_res[cc][:, g * GN:(g + 1) * GN],
                    lnt[:, 0:GN])), incs=[(s_ve, 1)])

            w_ln = [(s_ac, P.n(s_ac))]
            if g == 0:
                w_ln.append((s_x, x_ld))
            emit_ln(GN, (lambda c: ysb[c][:]), put_res, w_first=w_ln)
        ve_p3 = P.n(s_ve)
        tap("x1", [x_res[c][:] for c in range(NCH)], [(s_ve, ve_p3)])
        ctx_p3.close()
        ctx_p2.close()

        P.phase = "p4"
        ctx_p4 = ExitStack()
        wtstg = [mksb(ctx_p4, f"wtstg{b}", [128, 1024], F32, side="right") for b in range(2)]
        wtbf = mksb(ctx_p4, "wtbf", [128, 1024], BF16, side="right")
        c1b = mksb(ctx_p4, "c1b", [128, 1024], BF16, side="right")
        weff = [mksb(ctx_p4, f"weff{b}", [128, 1024], BF16, side="right") for b in range(2)]
        tspill = [mksb(ctx_p4, f"tspill{b}", [128, 1024], BF16, side="right") for b in range(2)]
        scl = mksb(ctx_p4, "scl", [128, 8], F32, side="right")

        tern_state = {"ve_red": {}, "te_tr": {}, "ev_ps": {}, "st_tb": {}}

        def tern_weight(d_w, d_dst, nt, wcols, ve_start):
            np_ = wcols // 1024
            ve_red = tern_state["ve_red"]
            te_tr = tern_state["te_tr"]
            ev_ps = tern_state["ev_ps"]
            st_tb = tern_state["st_tb"]
            for t in range(nt):
                for dd in range(np_):
                    par = dd % 2
                    if par in ve_red:
                        w_w = [(s_ve, ve_red[par])]
                    elif t == 0:
                        w_w = [(s_ve, ve_start)]
                    else:
                        w_w = []
                    P.dma("sync", wtstg[par][:],
                          d_w[128 * t:128 * (t + 1), 1024 * dd:1024 * (dd + 1)], swt[par],
                          waits=w_w)
                    P.add("vector", (lambda e, par=par, dd=dd: e.reduce_sum(
                        scl[:, dd:dd + 1], wtstg[par][:], axis=AX,
                        apply_absolute_value=True)),
                          waits=[(swt[par], P.n(swt[par]))], incs=[(s_ve, 1)])
                    ve_red[par] = P.n(s_ve)
                if np_ > 1:
                    P.add("vector", (lambda e, np_=np_: e.reduce_sum(
                        scl[:, 4:5], scl[:, 0:np_], axis=AX)))
                else:
                    P.add("vector", (lambda e: e.tensor_copy(scl[:, 4:5], scl[:, 0:1])))
                P.add("vector", (lambda e, wcols=wcols: e.tensor_scalar(
                    out=scl[:, 5:6], in0=scl[:, 4:5], scalar1=1.0 / wcols,
                    scalar2=1e-5, op0=ALU.mult, op1=ALU.max)))
                P.add("vector", (lambda e: e.tensor_scalar(
                    out=scl[:, 6:7], in0=scl[:, 5:6], scalar1=0.5, scalar2=None,
                    op0=ALU.mult)))
                P.add("vector", (lambda e: e.tensor_scalar(
                    out=scl[:, 7:8], in0=scl[:, 5:6], scalar1=-0.5, scalar2=None,
                    op0=ALU.mult)), incs=[(s_ve, 1)])
                ve_scl = P.n(s_ve)
                for dd in range(np_):
                    par = dd % 2
                    P.dma("sync", wtstg[par][:],
                          d_w[128 * t:128 * (t + 1), 1024 * dd:1024 * (dd + 1)], swt[par],
                          waits=[(s_ve, ve_scl if dd < 2 else ve_red[par])])
                    P.add("vector", (lambda e, par=par: e.tensor_scalar(
                        out=c1b[:], in0=wtstg[par][:], scalar1=scl[:, 6:7], scalar2=scl[:, 5:6],
                        op0=ALU.is_gt, op1=ALU.mult)),
                          waits=[(swt[par], P.n(swt[par]))])
                    we = weff[par]
                    w_we = [(s_te, te_tr[par])] if par in te_tr else []
                    P.add("vector", (lambda e, we=we, par=par: e.tensor_scalar(
                        out=we[:], in0=wtstg[par][:], scalar1=scl[:, 7:8], scalar2=scl[:, 5:6],
                        op0=ALU.is_lt, op1=ALU.mult)), waits=w_we, incs=[(s_ve, 1)])
                    ve_red[par] = P.n(s_ve)
                    P.add("vector", (lambda e, we=we: e.tensor_sub(we[:], c1b[:], we[:])),
                          incs=[(s_ve, 1)])
                    ve_we = P.n(s_ve)
                    psT = psC if par == 0 else psD
                    w_tr = [(s_ve, ve_we)]
                    if id(psT) in ev_ps:
                        w_tr.append(ev_ps[id(psT)])
                    for j in range(8):
                        P.add("tensor", (lambda e, j=j, we=we, psT=psT: e.matmul(
                            psT[:, 128 * j:128 * (j + 1)], we[:, 128 * j:128 * (j + 1)],
                            ident[:], start=True, stop=(j in (3, 7)))),
                              waits=(w_tr if j == 0 else []),
                              incs=([(s_te, 1)] if j == 7 else []))
                    te_tr[par] = P.n(s_te)
                    tb = tspill[par]
                    w_e = [(s_te, te_tr[par])]
                    if par in st_tb:
                        w_e.append(st_tb[par])
                    P.add("scalar", (lambda e, tb=tb, psT=psT: e.activation(
                        out=tb[:], in_=psT[:, 0:1024], func=ACT.Copy)),
                          waits=w_e, incs=[(s_ac, 1)])
                    ev_ps[id(psT)] = (s_ac, P.n(s_ac))
                    P.add("gpsimd", (lambda e, t=t, dd=dd, tb=tb, d_dst=d_dst: e.dma_start(
                        out=d_dst.rearrange("j p n -> p j n")[:, 8 * dd:8 * dd + 8,
                                                              128 * t:128 * (t + 1)],
                        in_=tb.ap().rearrange("p (j n) -> p j n", j=8))),
                          waits=[ev_ps[id(psT)]], incs=[(ssp[par], 16)], dma=True)
                    st_tb[par] = (ssp[par], P.n(ssp[par]))

        tern_weight(d_wfc, d_tfcs, NHI, D, ve_p3)
        tern_weight(d_wproj, d_tproj, NCH, HIDN, P.n(s_ve))
        w_spill = [(ssp[0], P.n(ssp[0])), (ssp[1], P.n(ssp[1]))]
        ctx_p4.close()

        P.phase = "p5"
        ctx_p5 = ExitStack()
        xn2 = [mksb(ctx_p5, f"xn2_{c}", [128, GN], BF16, side="right") for c in range(NCH)]
        r2 = mksb(ctx_p5, "r2", [128, NHI * GN], BF16, side="right")
        rbuf = [mksb(ctx_p5, f"rbuf{b}", [128, GN], BF16, side="right") for b in range(2)]
        t1bf = mksb(ctx_p5, "t1bf", [16, GN], BF16, side="right")
        t2bf = mksb(ctx_p5, "t2bf", [16, GN], BF16, side="right")
        fstg = [mksb(ctx_p5, f"fstg{b}", [128, 1024], BF16, side="right") for b in range(2)]
        pstg = [mksb(ctx_p5, f"pstg{b}", [128, HIDN], BF16, side="right") for b in range(2)]
        ysb5 = [mksb(ctx_p5, f"y5_{c}", [128, GN], F32, side="right") for c in range(NCH)]
        ustg = mksb(ctx_p5, "ustg", [16, 1024], F32, side="right")
        uT_bf = mksb(ctx_p5, "uT_bf", [16, HIDN], BF16, side="right")

        ve_u = None
        for dd in range(4):
            w_u = [(s_ve, ve_u)] if ve_u is not None else list(w_spill)
            P.dma("sync", ustg[:], d_uT[:, 1024 * dd:1024 * (dd + 1)], s_u, waits=w_u)
            P.add("vector", (lambda e, dd=dd: e.tensor_copy(
                uT_bf[:, 1024 * dd:1024 * (dd + 1)], ustg[:])),
                  waits=[(s_u, P.n(s_u))], incs=[(s_ve, 1)])
            ve_u = P.n(s_ve)

        tfcs_view = d_tfcs.rearrange("j p n -> p j n")
        tproj_view = d_tproj.rearrange("j p n -> p j n")
        te_fcmm = {0: None, 1: None}
        te_projmm = {0: None, 1: None}

        for g in range(G):
            def put_xn2(c, reg, te_val):
                P.add("vector", (lambda e, c=c, reg=reg: e.tensor_mul(
                    xn2[c][:], reg, rb_sb[:, 0:GN])),
                      waits=[(s_te, te_val)], incs=[(s_ve, 1)])

            w_ln3 = [(s_ac, P.n(s_ac))] + list(w_spill)
            emit_ln(GN, (lambda c, g=g: x_res[c][:, g * GN:(g + 1) * GN]), put_xn2,
                    w_first=w_ln3)
            ve_xn2 = P.n(s_ve)
            for c in range(NCH):
                P.add("tensor", (lambda e, c=c: e.matmul(
                    psD[0:16, 0:GN], vT_bf[:, 16 * c:16 * (c + 1)], xn2[c][:],
                    start=(c == 0), stop=(c == NCH - 1))),
                      waits=([(s_ve, ve_xn2)] if c == 0 else []),
                      incs=([(s_te, 1)] if c == NCH - 1 else []))
            P.add("vector", lambda e: e.tensor_copy(t1bf[:], psD[0:16, 0:GN]),
                  waits=[(s_te, P.n(s_te))], incs=[(s_ve, 1)])
            P.add("tensor", lambda e: e.matmul(psD[0:16, 512:512 + GN], fbTg[:], t1bf[:],
                                               start=True, stop=True),
                  waits=[(s_ve, P.n(s_ve))], incs=[(s_te, 1)])
            P.add("vector", lambda e: e.tensor_copy(t2bf[:], psD[0:16, 512:512 + GN]),
                  waits=[(s_te, P.n(s_te))], incs=[(s_ve, 1)])
            ve_t2 = P.n(s_ve)
            ac_r = [None, None]
            ve_r2l = [None, None]
            for hi in range(NHI):
                fpar = hi % 2
                fst = fstg[fpar]
                if te_fcmm[fpar] is not None:
                    w_f = [(s_te, te_fcmm[fpar])]
                elif g == 0 and hi < 2:
                    w_f = list(w_spill)
                else:
                    w_f = []
                P.dma("sync", fst.ap().rearrange("p (j n) -> p j n", j=8),
                      tfcs_view[:, :, 128 * hi:128 * (hi + 1)], sfc2[fpar], waits=w_f)
                w_fl = (sfc2[fpar], P.n(sfc2[fpar]))
                reg = psA[:, 0:GN] if hi % 2 == 0 else psB[:, 0:GN]
                w_mm = [(s_ve, ve_t2), w_fl]
                if ac_r[hi % 2] is not None:
                    w_mm.append((s_ac, ac_r[hi % 2]))
                for fi in range(NCH):
                    P.add("tensor", (lambda e, fi=fi, fst=fst, reg=reg: e.matmul(
                        reg, fst[:, 128 * fi:128 * (fi + 1)], xn2[fi][:],
                        start=(fi == 0), stop=False)),
                          waits=(w_mm if fi == 0 else []))
                P.add("tensor", (lambda e, hi=hi, reg=reg: e.matmul(
                    reg, uT_bf[:, 128 * hi:128 * (hi + 1)], t2bf[:],
                    start=False, stop=True)), incs=[(s_te, 1)])
                te_h = P.n(s_te)
                te_fcmm[fpar] = te_h
                rb = rbuf[hi % 2]
                w_relu = [(s_te, te_h)]
                if ve_r2l[hi % 2] is not None:
                    w_relu.append((s_ve, ve_r2l[hi % 2]))
                P.add("scalar", (lambda e, rb=rb, reg=reg: e.activation(
                    out=rb[:], in_=reg, func=ACT.Relu)),
                      waits=w_relu, incs=[(s_ac, 1)])
                ac_r[hi % 2] = P.n(s_ac)
                P.add("vector", (lambda e, hi=hi, rb=rb: e.tensor_mul(
                    r2[:, hi * GN:(hi + 1) * GN], rb[:], rb[:])),
                      waits=[(s_ac, ac_r[hi % 2])], incs=[(s_ve, 1)])
                ve_r2l[hi % 2] = P.n(s_ve)
            ve_r2 = P.n(s_ve)
            ac_y = [None, None]
            for fo in range(NCH):
                pstgb = pstg[fo % 2]
                if te_projmm[fo % 2] is not None:
                    w_p = [(s_te, te_projmm[fo % 2])]
                elif g == 0:
                    w_p = list(w_spill)
                else:
                    w_p = []
                for dd in range(4):
                    P.dma("gpsimd", pstgb.ap().rearrange("p (j n) -> p j n", j=NHI)[
                        :, 8 * dd:8 * dd + 8, :],
                        tproj_view[:, 8 * dd:8 * dd + 8, 128 * fo:128 * (fo + 1)],
                        spr[fo % 2], waits=(w_p if dd == 0 else []))
                w_pl = (spr[fo % 2], P.n(spr[fo % 2]))
                reg = psC[:, 0:GN] if fo % 2 == 0 else psC[:, 512:512 + GN]
                w_y = [w_pl, (s_ve, ve_r2)]
                if ac_y[fo % 2] is not None:
                    w_y.append((s_ac, ac_y[fo % 2]))
                for hi in range(NHI):
                    P.add("tensor", (lambda e, hi=hi, reg=reg, pstgb=pstgb: e.matmul(
                        reg, pstgb[:, 128 * hi:128 * (hi + 1)], r2[:, hi * GN:(hi + 1) * GN],
                        start=(hi == 0), stop=(hi == NHI - 1))),
                          waits=(w_y if hi == 0 else []),
                          incs=([(s_te, 1)] if hi == NHI - 1 else []))
                te_projmm[fo % 2] = P.n(s_te)
                te_y = P.n(s_te)
                P.add("scalar", (lambda e, fo=fo, reg=reg: e.activation(
                    out=ysb5[fo][:], in_=reg, func=ACT.Copy)),
                      waits=[(s_te, te_y)], incs=[(s_ac, 1)])
                ac_y[fo % 2] = P.n(s_ac)

            def put_res5(cc, reg, te_val, g=g):
                P.add("vector", (lambda e, cc=cc, reg=reg: e.tensor_mul(
                    lnt[:, 0:GN], reg, rb_sb[:, 0:GN])),
                      waits=[(s_te, te_val)])
                P.add("vector", (lambda e, cc=cc, g=g: e.tensor_add(
                    x_res[cc][:, g * GN:(g + 1) * GN], x_res[cc][:, g * GN:(g + 1) * GN],
                    lnt[:, 0:GN])), incs=[(s_ve, 1)])

            emit_ln(GN, (lambda c: ysb5[c][:]), put_res5, w_first=[(s_ac, P.n(s_ac))])
        ve_p5 = P.n(s_ve)
        tap("r2", [r2[:]], [(s_ve, ve_p5)])
        ctx_p5.close()

        P.phase = "p6"
        for c in range(NCH):
            P.dma("gpsimd", d_out[128 * c:128 * c + 128, :], x_res[c][:], s_st,
                  waits=([(s_ve, ve_p5)] if c == 0 else []))
        final_sem, final = s_st, P.n(s_st)

        block = ctx.enter_context(nc.Block())
        for eng_name, deco in [("sync", block.sync), ("tensor", block.tensor),
                               ("scalar", block.scalar), ("vector", block.vector),
                               ("gpsimd", block.gpsimd)]:
            def make(eng_name=eng_name):
                def _(e):
                    P.run(e, eng_name)
                    if eng_name == "gpsimd":
                        e.wait_ge(final_sem, final)
                return _
            deco(make())
    return nc


_NC_CACHE = {}
DEBUG = ()


def kernel(L0_attn_hidden_states, L0_attn_Wqkv, L0_attn_bqkv, L0_attn_Wout, L0_attn_bout,
           L0_attn_Wg, L0_attn_bg, L0_ffn_fast_b, L0_ffn_w_fc, L0_ffn_w_proj, L0_ffn_u,
           L0_ffn_v, L0_ffn_gate):
    f = np.float32
    x = np.ascontiguousarray(np.asarray(L0_attn_hidden_states, dtype=f))
    wqkv = np.ascontiguousarray(np.asarray(L0_attn_Wqkv, dtype=f))
    bqkv = np.asarray(L0_attn_bqkv, dtype=f).reshape(-1)
    wout = np.ascontiguousarray(np.asarray(L0_attn_Wout, dtype=f))
    bout = np.asarray(L0_attn_bout, dtype=f).reshape(-1)
    wgv = np.ascontiguousarray(np.asarray(L0_attn_Wg, dtype=f))
    bg = np.asarray(L0_attn_bg, dtype=f).reshape(1, 1)
    fb = np.asarray(L0_ffn_fast_b, dtype=f)
    wfc = np.ascontiguousarray(np.asarray(L0_ffn_w_fc, dtype=f))
    wproj = np.ascontiguousarray(np.asarray(L0_ffn_w_proj, dtype=f))
    uu = np.asarray(L0_ffn_u, dtype=f)
    vv = np.asarray(L0_ffn_v, dtype=f)
    gate = np.asarray(L0_ffn_gate, dtype=f).reshape(1, 1)

    e16 = np.zeros((128, NCH * 16), dtype=BF)
    for c in range(NCH):
        e16[0:64, c * 16 + 2 * c] = 1
        e16[64:128, c * 16 + 2 * c + 1] = 1

    shared = {
        "wqkv": wqkv,
        "bqkv_pp": np.ascontiguousarray(bqkv.reshape(24, 128).T),
        "bv_row": np.ascontiguousarray(bqkv[2048:3072].reshape(1, D)),
        "wout": wout,
        "bout_pp": np.ascontiguousarray(bout.reshape(8, 128).T),
        "wg": wgv,
        "bg": bg,
        "w_fc": wfc,
        "w_proj": wproj,
        "uT": np.ascontiguousarray(uu.T),
        "vT": np.ascontiguousarray(vv.T),
        "fbT": np.ascontiguousarray(fb.T),
        "gate": gate,
        "ident": np.eye(128, dtype=BF),
        "negi": (np.eye(128) * -1.0e9).astype(BF),
        "i16": np.eye(16, dtype=BF),
        "row1": np.ones((1, 128), dtype=BF),
        "statc": np.full((128, 1), -1.0 / 1024.0, dtype=BF),
        "one11": np.ones((1, 1), dtype=BF),
        "e16": e16,
    }

    in_maps = []
    for core in range(8):
        seq, hh = core // 2, core % 2
        start = hh * TOWN
        rows = np.zeros((EXT, D), dtype=f)
        lo = max(0, start - 256)
        rows[256 - (start - lo):256] = x[seq, lo:start]
        rows[256:256 + TOWN] = x[seq, start:start + TOWN]
        hi = min(T, start + TOWN + 256)
        rows[2304:2304 + (hi - start - TOWN)] = x[seq, start + TOWN:hi]
        rows[2560:2624] = x[seq, 0:NG]
        xe = np.ascontiguousarray(rows.T)

        t01 = np.zeros((NQT, 128, 640), dtype=BF)
        a = np.arange(128)[:, None]
        b = np.arange(640)[None, :]
        for qt in range(NQT):
            band = (b >= a) & (b <= a + 512)
            e_idx = 128 * qt + b
            gk = 2048 * hh + (e_idx - 256)
            valid = (gk >= 0) & (gk < T)
            t01[qt] = (~(band & valid)).astype(BF)

        m = dict(shared)
        m["x_ext"] = xe
        m["masks"] = np.ascontiguousarray(t01)
        in_maps.append(m)

    key = ("nc",) + tuple(sorted(DEBUG))
    if key not in _NC_CACHE:
        _NC_CACHE[key] = build_program(debug=DEBUG)
    nc = _NC_CACHE[key]
    res = run_bass_kernel_spmd(nc, in_maps, core_ids=list(range(8)))
    kernel.last_results = res.results

    out = np.empty((B, T, D), dtype=f)
    for core in range(8):
        seq, hh = core // 2, core % 2
        out[seq, hh * TOWN:(hh + 1) * TOWN] = res.results[core]["out"].T
    return out
